# version 02
# baseline (speedup 1.0000x reference)
import math
import os
import numpy as np
import ml_dtypes
from contextlib import ExitStack

import concourse.bass as bass
import concourse.mybir as mybir
from concourse.bass_utils import run_bass_kernel_spmd

F32 = mybir.dt.float32
BF16 = mybir.dt.bfloat16
AF = mybir.ActivationFunctionType
ALU = mybir.AluOpType
AX = mybir.AxisListType

S = 2048
C = 1024
NT = S // 128
NCH = C // 128
NHEAD = 16
DH = 64

ENGINES = ("tensor", "vector", "scalar", "gpsimd", "sync")


class Em:
    def __init__(self, n_dma_sems=24):
        self.evs = []
        self.eng_ops = {e: [] for e in ENGINES}
        self.lw = {}
        self.rd = {}
        self.n_dma = n_dma_sems
        self.dma_use = [0] * n_dma_sems
        self.dma_rr = 0
        self.emitted = []

    @staticmethod
    def _keys(x):
        if x is None:
            return []
        if isinstance(x, str):
            return [x]
        out = []
        for e in x:
            if isinstance(e, str):
                out.append(e)
            else:
                out.extend(e)
        return out

    def _deps(self, reads, writes):
        deps = set()
        for k in reads:
            if k in self.lw:
                deps.add(self.lw[k])
        for k in writes:
            if k in self.lw:
                deps.add(self.lw[k])
            for r in self.rd.get(k, ()):
                deps.add(r)
        return deps

    def _record(self, kind, eng, fn, reads, writes, tag, extra):
        reads = self._keys(reads)
        writes = self._keys(writes)
        evid = len(self.evs)
        ev = dict(kind=kind, eng=eng, fn=fn, deps=self._deps(reads, writes),
                  marked=False, tag=tag)
        ev.update(extra)
        self.evs.append(ev)
        self.eng_ops[eng].append(evid)
        for k in reads:
            self.rd.setdefault(k, []).append(evid)
        for k in writes:
            self.lw[k] = evid
            self.rd[k] = []
        return evid

    def op(self, eng, fn, reads=(), writes=(), tag=""):
        return self._record("op", eng, fn, reads, writes, tag, {})

    def dma(self, eng, fn, reads=(), writes=(), tag=""):
        q = self.dma_rr
        self.dma_rr = (self.dma_rr + 1) % self.n_dma
        self.dma_use[q] += 1
        return self._record("dma", eng, fn, reads, writes, tag,
                            dict(q=q, use=self.dma_use[q]))

    def build(self, nc, es):
        for ev in self.evs:
            for d in ev["deps"]:
                dev = self.evs[d]
                if dev["kind"] == "op":
                    dev["marked"] = True
        inc = {e: 0 for e in ENGINES}
        for e in ENGINES:
            for evid in self.eng_ops[e]:
                ev = self.evs[evid]
                if ev["kind"] == "op" and ev["marked"]:
                    inc[e] += 1
                ev["incval"] = inc[e]
        esem = {e: es.enter_context(nc.semaphore(f"sem_{e}")) for e in ENGINES}
        dsem = [es.enter_context(nc.semaphore(f"dma{i}")) for i in range(self.n_dma)]
        self._esem, self._dsem = esem, dsem

        emr = self

        def body(eng_name):
            def run(engobj):
                wm = {}
                for evid in emr.eng_ops[eng_name]:
                    ev = emr.evs[evid]
                    waits = []
                    for d in ev["deps"]:
                        dev = emr.evs[d]
                        if dev["kind"] == "dma":
                            waits.append((("d", dev["q"]), 16 * dev["use"], dsem[dev["q"]]))
                        else:
                            pe = dev["eng"]
                            if pe == eng_name and eng_name == "tensor":
                                continue
                            waits.append((("e", pe), dev["incval"], esem[pe]))
                    if ev["kind"] == "dma" and ev["use"] > 1:
                        waits.append((("d", ev["q"]), 16 * (ev["use"] - 1), dsem[ev["q"]]))
                    wlog = []
                    for key, val, sem in sorted(waits, key=lambda w: (str(w[0]), w[1])):
                        if wm.get(key, 0) < val:
                            engobj.wait_ge(sem, val)
                            wm[key] = val
                            wlog.append((key, val))
                    emr.emitted.append((eng_name, evid, ev["tag"], wlog,
                                        ev["incval"] if ev.get("marked") else None))
                    inst = ev["fn"](engobj)
                    if ev["kind"] == "dma":
                        inst.then_inc(dsem[ev["q"]], 16)
                    elif ev["marked"]:
                        inst.then_inc(esem[eng_name], 1)
                if eng_name == "sync":
                    for i in range(emr.n_dma):
                        if emr.dma_use[i]:
                            engobj.wait_ge(dsem[i], 16 * emr.dma_use[i])
            return run

        with nc.Block() as block:
            block.sync(body("sync"))
            block.gpsimd(body("gpsimd"))
            block.vector(body("vector"))
            block.scalar(body("scalar"))
            block.tensor(body("tensor"))



def _chunked(w):
    n, rem = divmod(w.shape[0], 128)
    assert rem == 0
    return np.ascontiguousarray(
        w.reshape(n, 128, w.shape[1]).transpose(1, 0, 2).reshape(128, n * w.shape[1]))


def _pcols(v):
    n = v.shape[0] // 128
    return np.ascontiguousarray(v.reshape(n, 128).T)


def _row(v):
    return np.ascontiguousarray(v.reshape(1, -1))


def host_consts():
    c = {}
    c["onesf"] = np.ones((128, 128), np.float32)
    c["idf"] = np.eye(128, dtype=np.float32)
    s64 = np.zeros((64, 128), np.float32)
    for d in range(64):
        s64[d, 64 + d] = 1.0
    c["s64"] = s64
    eh = np.zeros((128, 16 * NCH), np.float32)
    for ch in range(NCH):
        for p in range(128):
            eh[p, 16 * ch + 2 * ch + p // 64] = 1.0
    c["ehead"] = eh
    eh2 = np.zeros((16, 128 * NCH), np.float32)
    for ch in range(NCH):
        for p in range(128):
            eh2[2 * ch + p // 64, 128 * ch + p] = 1.0
    c["ehead2"] = eh2
    s16 = np.zeros((16, 128 * 16), np.float32)
    for r in range(16):
        s16[r, 128 * r:128 * (r + 1)] = 1.0
    c["sel16"] = s16
    egn = np.zeros((128, 32 * NCH), np.float32)
    egnT = np.zeros((32, 128 * NCH), np.float32)
    for ch in range(NCH):
        for p in range(128):
            g = (128 * ch + p) // 32
            egn[p, 32 * ch + g] = 1.0
            egnT[g, 128 * ch + p] = 1.0
    c["egn"] = egn
    c["egnT"] = egnT
    tril = np.tril(np.ones((128, 128), np.float32)).T
    c["tril"] = np.ascontiguousarray(tril)
    trie = np.zeros((128, 128), np.float32)
    for p2 in range(128):
        for p in range(128):
            if (p2 % 8) == (p % 8) and (p2 // 8) < (p // 8):
                trie[p2, p] = 1.0
    c["trie"] = trie
    c["iota512"] = np.broadcast_to(np.arange(512, dtype=np.float32), (128, 512)).copy()
    ic = np.zeros((128, 4), np.float32)
    for sc in range(4):
        ic[:, sc] = 128 * sc + np.arange(128)
    c["iotacol"] = ic
    c["epsrow"] = np.full((128, 1), 1e-5, np.float32)
    s8 = np.zeros((128, 8), np.float32)
    for f in range(128):
        s8[f, f % 8] = 1.0
    c["sel8"] = s8
    return c


def host_prep(inputs):
    f = np.float32
    d = {}
    d["wqkv"] = _chunked(inputs["L0_attn_W_qkv"].astype(f, copy=False))
    d["bqkv"] = _pcols(inputs["L0_attn_b_qkv"])
    d["gng"] = _pcols(inputs["L0_attn_gn_gamma"])
    d["gnb"] = _pcols(inputs["L0_attn_gn_beta"])
    d["wol"] = _chunked(inputs["L0_attn_W_o"])
    d["bo"] = _row(inputs["L0_attn_b_o"])
    su = inputs["L0_ffn_shared_up"]
    sd = inputs["L0_ffn_shared_down"]
    ru = inputs["L0_ffn_routed_up"]
    rdn = inputs["L0_ffn_routed_down"]
    d["sut"] = np.stack([_chunked(np.ascontiguousarray(su[i].T)) for i in range(2)])
    d["sdt"] = np.stack([_chunked(np.ascontiguousarray(sd[i].T)) for i in range(2)])
    d["rut"] = np.stack([_chunked(np.ascontiguousarray(ru[i].T)) for i in range(16)])
    d["rdt"] = np.stack([_chunked(np.ascontiguousarray(rdn[i].T)) for i in range(16)])
    d["rwt"] = _chunked(np.ascontiguousarray(inputs["L0_ffn_router_w"].T))
    d["wq1"] = _chunked(inputs["L1_attn_Wq"])
    d["wk1"] = _chunked(inputs["L1_attn_Wk"])
    d["wv1"] = _chunked(inputs["L1_attn_Wv"])
    d["wo1"] = _chunked(inputs["L1_attn_Wo"])
    d["bq1"] = _pcols(inputs["L1_attn_bq"])
    d["bk1"] = _pcols(inputs["L1_attn_bk"])
    d["bv1"] = _pcols(inputs["L1_attn_bv"])
    d["bo1"] = _row(inputs["L1_attn_bo"])
    d["gwl"] = _chunked(inputs["L1_ffn_gate_w"])
    w1 = inputs["L1_ffn_w1"]
    w2 = inputs["L1_ffn_w2"]
    d["w1l"] = np.stack([_chunked(w1[e]) for e in range(8)])
    d["w2l"] = np.stack([_chunked(w2[e]) for e in range(8)])
    d.update(host_consts())
    return d



class Ctx:
    pass


def ak(lo, hi):
    return [f"R:{g}" for g in range(lo // 512, (hi + 511) // 512)]


def xkk(lo, hi):
    return [f"xnT:{g}" for g in range(lo // 512, (hi + 511) // 512)]


def pak(lo, hi):
    return [f"pA:{b}" for b in range(lo // 512, (hi + 511) // 512)]


def pbk(lo, hi):
    return [f"pB:{b}" for b in range(lo // 512, (hi + 511) // 512)]


XT = lambda: [f"x:{t}" for t in range(NT)]
XNT_ALL = lambda: [f"xnT:{g}" for g in range(32)]

R_XNTOK = 0
R_QP = 0
R_KP = 2048
R_VP = 4096
R_OT = 16384
R_A0 = 0
R_A1 = 4096
R_GATEB = 8192
R_GATET = 10240
R_U0 = 12288
R_U1 = 16384
R_D0 = 20480
R_D1 = 24576
R_VBUF = 16384
R_SEL = 16384
R_Y = 24576
R_XIN = 28672


def build_program(taps=(), upto="full"):
    nc = bass.Bass("TRN2", target_bir_lowering=False)
    em = Em()
    c = Ctx()
    c.nc, c.em, c.taps = nc, em, {}

    dr = {}
    def din(name, shape, dt=F32):
        dr[name] = nc.dram_tensor(name, list(shape), dt, kind="ExternalInput").ap()
    din("x", (S, C))
    din("wqkv", (128, 8 * 3072)); din("bqkv", (128, 24))
    din("gng", (128, 8)); din("gnb", (128, 8))
    din("wol", (128, 8 * 1024)); din("bo", (1, 1024))
    din("sut", (2, 128, 4096)); din("sdt", (2, 128, 4096))
    din("rut", (16, 128, 4096)); din("rdt", (16, 128, 4096))
    din("rwt", (128, 128))
    din("wq1", (128, 8192)); din("wk1", (128, 8192)); din("wv1", (128, 8192))
    din("wo1", (128, 8192))
    din("bq1", (128, 8)); din("bk1", (128, 8)); din("bv1", (128, 8)); din("bo1", (1, 1024))
    din("gwl", (128, 64))
    din("w1l", (8, 128, 32768)); din("w2l", (8, 128, 32768))
    for n, sh in (("onesf", (128, 128)), ("idf", (128, 128)), ("s64", (64, 128)),
                  ("ehead", (128, 128)), ("ehead2", (16, 1024)), ("sel16", (16, 2048)),
                  ("egn", (128, 256)), ("egnT", (32, 1024)), ("tril", (128, 128)),
                  ("trie", (128, 128)), ("iota512", (128, 512)), ("iotacol", (128, 4)),
                  ("epsrow", (128, 1)), ("sel8", (128, 8))):
        din(n, sh)
    out = nc.dram_tensor("out", [S, C], F32, kind="ExternalOutput").ap()
    c.dr, c.out = dr, out

    es = ExitStack()
    sb = lambda n, sh, dt: es.enter_context(nc.sbuf_tensor(n, sh, dt))
    with es:
        c.x = sb("x_sb", [128, NT * 1024], F32)
        c.xnT = sb("xnT_sb", [128, 16384], BF16)
        c.R = sb("reg64", [128, 32768], BF16)
        c.ET = sb("ET", [128, 2048], BF16)
        c.ws = sb("wstream", [128, 6144], BF16)
        c.scr0 = sb("scr0", [128, 1024], F32)
        c.scr1 = sb("scr1", [128, 2048], F32)
        c.onesf = sb("onesf_sb", [128, 128], F32)
        c.onesb = sb("onesb_sb", [128, 128], BF16)
        c.idf = sb("idf_sb", [128, 128], F32)
        c.idb = sb("idb_sb", [128, 128], BF16)
        c.s64b = sb("s64b_sb", [64, 128], BF16)
        c.eheadb = sb("eheadb_sb", [128, 128], BF16)
        c.egnb = sb("egnb_sb", [128, 256], BF16)
        c.egnTb = sb("egnTb_sb", [32, 1024], BF16)
        c.trilb = sb("trilb_sb", [128, 128], BF16)
        c.trieb = sb("trieb_sb", [128, 128], BF16)
        c.sel8b = sb("sel8b_sb", [128, 8], BF16)
        c.iota512 = sb("iota512_sb", [128, 512], F32)
        c.iotacol = sb("iotacol_sb", [128, 4], F32)
        c.lnS = sb("lnS", [128, 16], F32)
        c.lnQ = sb("lnQ", [128, 16], F32)
        c.lnM = sb("lnM", [128, 16], F32)
        c.lnV = sb("lnV", [128, 16], F32)
        c.lnR = sb("lnR", [128, 16], F32)
        c.gnSQ = sb("gnSQ", [128, 16], BF16)
        c.gnSQf = sb("gnSQf", [128, 16], F32)
        c.gst = sb("gst", [32, 4], F32)
        c.gstb = sb("gstb", [32, 4], BF16)
        c.gnE = sb("gnE", [128, 32], F32)
        c.gnA = sb("gnA", [128, 8], F32)
        c.gnAb = sb("gnAb", [128, 8], BF16)
        c.gnB = sb("gnB", [128, 8], F32)
        c.gnBb = sb("gnBb", [128, 8], BF16)
        c.bcall = sb("bcall", [128, 24], F32)
        c.mcol2 = sb("mcol2", [128, 4], F32)
        c.trow = sb("trow", [1, 512], F32)
        c.ehead2b = sb("ehead2b_sb", [16, 1024], BF16)
        c.tmpc = sb("tmpc", [128, 4], F32)
        c.gngs = sb("gng_s", [128, 8], F32)
        c.gnbs = sb("gnb_s", [128, 8], F32)
        c.bqkvc = sb("bqkv_s", [128, 24], F32)
        c.bq1s = sb("bq1_s", [128, 8], F32)
        c.bk1s = sb("bk1_s", [128, 8], F32)
        c.bv1s = sb("bv1_s", [128, 8], F32)
        c.rwts = sb("rwt_s", [128, 128], BF16)
        c.gwls = sb("gwl_s", [128, 64], BF16)
        c.mcol = sb("mcol", [128, 16], F32)
        c.mbias = sb("mbias", [128, 1], F32)
        c.botile = sb("botile", [128, 1024], BF16)
        c.probs = sb("probs128", [128, 128], F32)
        c.probsb = sb("probsb", [128, 128], BF16)
        c.mask = sb("mask128", [128, 128], BF16)
        c.maskf = sb("maskf", [128, 128], F32)
        c.cum = sb("cum128", [128, 128], F32)
        c.offsb = sb("offsb", [128, 128], F32)
        c.totrow = sb("totrow", [1, 128], F32)
        c.bis = sb("bis", [8, 8], F32)
        c.cnt1b = sb("cnt1b", [128, 1], BF16)
        c.totb = sb("totb", [128, 1], BF16)
        c.slots = sb("slots", [128, 16], F32)
        c.neg9 = sb("neg9", [128, 1], F32)
        c.pcols = sb("pcols_sb", [128, 4], F32)
        c.epsc = sb("epsc", [128, 1], F32)
        c.pA = es.enter_context(nc.psum_tensor("pA", [128, 2048], F32))
        c.pB = es.enter_context(nc.psum_tensor("pB", [128, 2048], F32))

        emit_all(c, upto, taps)
        em.build(nc, es)
    return nc


SCR1 = ["scr1:0", "scr1:1"]
WSK = lambda lo, hi: [f"ws:{g}" for g in range(lo // 512, (hi + 511) // 512)]


def emit_all(c, upto, taps):
    em, dr, nc = c.em, c.dr, c.nc
    R, pA, pB = c.R, c.pA, c.pB

    def top(fn, r=(), w=(), tag=""):
        em.op("tensor", fn, r, w, tag)

    def vop(fn, r=(), w=(), tag=""):
        em.op("vector", fn, r, w, tag)

    def sop(fn, r=(), w=(), tag=""):
        em.op("scalar", fn, r, w, tag)

    def gop(fn, r=(), w=(), tag=""):
        em.op("gpsimd", fn, r, w, tag)

    def sdma(fn, r=(), w=(), tag=""):
        em.dma("sync", fn, r, w, tag)

    def gdma(fn, r=(), w=(), tag=""):
        em.dma("gpsimd", fn, r, w, tag)

    tap_stage = {}

    def tap(name, ap, rkeys, shape, dt):
        t = nc.dram_tensor("tap_" + name, list(shape), dt, kind="ExternalOutput").ap()
        sdma(lambda e, t=t, ap=ap: e.dma_start(out=t[:, :], in_=ap), r=rkeys, w=(), tag="tap")

    sdma(lambda e: e.dma_start(
        out=c.x[:, :].rearrange("p (n f) -> p n f", n=NT),
        in_=dr["x"].rearrange("(n p) f -> p n f", p=128)), r=(), w=XT(), tag="ldx")
    for name, dst in (("onesf", c.onesf), ("idf", c.idf), ("iota512", c.iota512),
                      ("iotacol", c.iotacol), ("epsrow", c.epsc), ("gng", c.gngs),
                      ("gnb", c.gnbs), ("bqkv", c.bqkvc), ("bq1", c.bq1s),
                      ("bk1", c.bk1s), ("bv1", c.bv1s)):
        sdma(lambda e, d=dr[name], t=dst: e.dma_start(out=t[:, :], in_=d[:, :]),
             r=(), w=[name + "_sb"], tag="ldc")
    for name, dst in (("onesf", c.onesb), ("idf", c.idb), ("s64", c.s64b),
                      ("ehead", c.eheadb), ("ehead2", c.ehead2b),
                      ("egn", c.egnb), ("egnT", c.egnTb), ("tril", c.trilb),
                      ("trie", c.trieb), ("sel8", c.sel8b), ("rwt", c.rwts),
                      ("gwl", c.gwls)):
        gdma(lambda e, d=dr[name], t=dst: e.dma_start(out=t[:, :], in_=d[:, :]),
             r=(), w=[name + "_b"], tag="ldcb")
    vop(lambda e: e.memset(c.neg9[:, :], -1.0e9), r=(), w=["neg9"])

    def mm(out, lhsT, rhs, st, sp, r, w):
        top(lambda e, o=out, l=lhsT, rr=rhs, st=st, sp=sp: e.matmul(o, l, rr, start=st, stop=sp), r=r, w=w)

    def tr(out, in_, ident, r, w):
        idap = ident if isinstance(ident, bass.AP) else ident[:, :]
        top(lambda e, o=out, i=in_, d=idap: e.transpose(o, i, d), r=r, w=w)

    def ln(layer):
        addxn = (layer == 0)
        for t in range(NT):
            xt = c.x[:, 1024 * t:1024 * (t + 1)]
            vop(lambda e, xt=xt, t=t: e.reduce_sum(out=c.lnS[:, t:t + 1], in_=xt, axis=AX.X),
                r=[f"x:{t}"], w=["lnS"])
            sop(lambda e, xt=xt, t=t: e.activation(out=c.scr0[:, :], in_=xt, func=AF.Square,
                                                   accum_out=c.lnQ[:, t:t + 1]),
                r=[f"x:{t}"], w=["scr0", "lnQ"])
        vop(lambda e: e.tensor_scalar(out=c.lnM[:, :], in0=c.lnS[:, :], scalar1=1.0 / C,
                                      scalar2=None, op0=ALU.mult), r=["lnS"], w=["lnM"])
        vop(lambda e: e.tensor_mul(out=c.lnV[:, :], in0=c.lnM[:, :], in1=c.lnM[:, :]),
            r=["lnM"], w=["lnV"])
        vop(lambda e: e.scalar_tensor_tensor(out=c.lnV[:, :], in0=c.lnQ[:, :], scalar=1.0 / C,
                                             in1=c.lnV[:, :], op0=ALU.mult, op1=ALU.subtract),
            r=["lnQ", "lnV"], w=["lnV"])
        vop(lambda e: e.tensor_scalar(out=c.lnV[:, :], in0=c.lnV[:, :], scalar1=1e-5,
                                      scalar2=None, op0=ALU.add), r=["lnV"], w=["lnV"])
        sop(lambda e: e.activation(out=c.lnR[:, :], in_=c.lnV[:, :], func=AF.Sqrt),
            r=["lnV"], w=["lnR"])
        vop(lambda e: e.reciprocal(out=c.lnR[:, :], in_=c.lnR[:, :]), r=["lnR"], w=["lnR"])
        for t in range(NT):
            xt = c.x[:, 1024 * t:1024 * (t + 1)]
            dst = R[:, R_XNTOK + 1024 * t:R_XNTOK + 1024 * (t + 1)]
            dkeys = ak(R_XNTOK + 1024 * t, R_XNTOK + 1024 * (t + 1))
            if addxn:
                h = (t % 2) * 1024
                scr = c.scr1[:, h:h + 1024]
                skey = [SCR1[t % 2]]
                vop(lambda e, xt=xt, t=t, scr=scr: e.tensor_scalar(
                    out=scr, in0=xt, scalar1=c.lnM[:, t:t + 1], scalar2=c.lnR[:, t:t + 1],
                    op0=ALU.subtract, op1=ALU.mult), r=[f"x:{t}", "lnM", "lnR"], w=skey)
                vop(lambda e, xt=xt, scr=scr: e.tensor_add(out=xt, in0=xt, in1=scr),
                    r=skey + [f"x:{t}"], w=[f"x:{t}"])
                vop(lambda e, dst=dst, scr=scr: e.tensor_copy(out=dst, in_=scr),
                    r=skey, w=dkeys)
            else:
                vop(lambda e, xt=xt, t=t, dst=dst: e.tensor_scalar(
                    out=dst, in0=xt, scalar1=c.lnM[:, t:t + 1], scalar2=c.lnR[:, t:t + 1],
                    op0=ALU.subtract, op1=ALU.mult), r=[f"x:{t}", "lnM", "lnR"], w=dkeys)

    def build_xnT():
        for cc in range(NCH):
            for tg in range(2):
                bank = (cc * 2 + tg) % 4
                pview = pA[:, 512 * bank:512 * (bank + 1)].bitcast(BF16)
                bkey = [f"pA:{bank}"]
                for j in range(8):
                    t = tg * 8 + j
                    tr(pview[:, 128 * j:128 * (j + 1)],
                       R[:, R_XNTOK + 1024 * t + 128 * cc:R_XNTOK + 1024 * t + 128 * (cc + 1)],
                       c.idb,
                       r=ak(R_XNTOK + 1024 * t + 128 * cc, R_XNTOK + 1024 * t + 128 * (cc + 1)) + ["idf_b"],
                       w=bkey)
                dlo = 2048 * cc + 1024 * tg
                vop(lambda e, pv=pview, cc=cc, tg=tg: e.tensor_copy(
                    out=c.xnT[:, 2048 * cc + 1024 * tg: 2048 * cc + 1024 * tg + 1024], in_=pv),
                    r=bkey, w=xkk(dlo, dlo + 1024))

    def gn():
        for cc in range(NCH):
            ch = c.xnT[:, 2048 * cc:2048 * (cc + 1)]
            rk = xkk(2048 * cc, 2048 * (cc + 1))
            vop(lambda e, ch=ch, cc=cc: e.reduce_sum(out=c.gnSQf[:, 2 * cc:2 * cc + 1], in_=ch, axis=AX.X),
                r=rk, w=["gnSQf"])
            sop(lambda e, ch=ch, cc=cc: e.activation(out=c.scr1[:, :], in_=ch, func=AF.Square,
                                                     accum_out=c.gnSQf[:, 2 * cc + 1:2 * cc + 2]),
                r=rk, w=SCR1 + ["gnSQf"])
        vop(lambda e: e.tensor_copy(out=c.gnSQ[:, :], in_=c.gnSQf[:, :]), r=["gnSQf"], w=["gnSQ"])
        for cc in range(NCH):
            mm(pB[0:32, 0:2], c.egnb[:, 32 * cc:32 * (cc + 1)], c.gnSQ[:, 2 * cc:2 * cc + 2],
               cc == 0, cc == 7, r=["egn_b", "gnSQ"], w=["pB:0"])
        vop(lambda e: e.tensor_scalar(out=c.gst[:, 0:2], in0=pB[0:32, 0:2], scalar1=1.0 / 65536.0,
                                      scalar2=None, op0=ALU.mult), r=["pB:0"], w=["gst"])
        vop(lambda e: e.tensor_mul(out=c.gst[:, 2:3], in0=c.gst[:, 0:1], in1=c.gst[:, 0:1]),
            r=["gst"], w=["gst"])
        vop(lambda e: e.tensor_sub(out=c.gst[:, 3:4], in0=c.gst[:, 1:2], in1=c.gst[:, 2:3]),
            r=["gst"], w=["gst"])
        vop(lambda e: e.tensor_scalar(out=c.gst[:, 3:4], in0=c.gst[:, 3:4], scalar1=1e-6,
                                      scalar2=None, op0=ALU.add), r=["gst"], w=["gst"])
        sop(lambda e: e.activation(out=c.gst[:, 2:3], in_=c.gst[:, 3:4], func=AF.Sqrt),
            r=["gst"], w=["gst"])
        vop(lambda e: e.reciprocal(out=c.gst[:, 3:4], in_=c.gst[:, 2:3]), r=["gst"], w=["gst"])
        vop(lambda e: e.tensor_copy(out=c.gstb[:, 0:1], in_=c.gst[:, 3:4]), r=["gst"], w=["gstb"])
        vop(lambda e: e.tensor_copy(out=c.gstb[:, 1:2], in_=c.gst[:, 0:1]), r=["gst"], w=["gstb"])
        vop(lambda e: e.tensor_sub(out=c.gst[:, 1:2], in0=c.gst[:, 3:4], in1=c.gstb[:, 0:1]),
            r=["gst", "gstb"], w=["gst"])
        vop(lambda e: e.tensor_copy(out=c.gstb[:, 2:3], in_=c.gst[:, 1:2]), r=["gst"], w=["gstb"])
        vop(lambda e: e.tensor_sub(out=c.gst[:, 1:2], in0=c.gst[:, 0:1], in1=c.gstb[:, 1:2]),
            r=["gst", "gstb"], w=["gst"])
        vop(lambda e: e.tensor_copy(out=c.gstb[:, 3:4], in_=c.gst[:, 1:2]), r=["gst"], w=["gstb"])
        for cc in range(NCH):
            bank = 1 + (cc % 2)
            mm(pB[:, 512 * bank:512 * bank + 4], c.egnTb[:, 128 * cc:128 * (cc + 1)], c.gstb[:, 0:4],
               True, True, r=["egnT_b", "gstb"], w=[f"pB:{bank}"])
            vop(lambda e, cc=cc, bank=bank: e.tensor_copy(
                out=c.gnE[:, 4 * cc:4 * cc + 4], in_=pB[:, 512 * bank:512 * bank + 4]),
                r=[f"pB:{bank}"], w=["gnE"])
        for cc in range(NCH):
            vop(lambda e, cc=cc: e.tensor_add(out=c.gnA[:, cc:cc + 1], in0=c.gnE[:, 4 * cc:4 * cc + 1],
                                              in1=c.gnE[:, 4 * cc + 2:4 * cc + 3]), r=["gnE"], w=["gnA"])
            vop(lambda e, cc=cc: e.tensor_mul(out=c.gnA[:, cc:cc + 1], in0=c.gnA[:, cc:cc + 1],
                                              in1=c.gngs[:, cc:cc + 1]), r=["gnA", "gng_sb"], w=["gnA"])
            vop(lambda e, cc=cc: e.tensor_add(out=c.gnB[:, cc:cc + 1], in0=c.gnE[:, 4 * cc + 1:4 * cc + 2],
                                              in1=c.gnE[:, 4 * cc + 3:4 * cc + 4]), r=["gnE"], w=["gnB"])
            vop(lambda e, cc=cc: e.tensor_mul(out=c.gnB[:, cc:cc + 1], in0=c.gnB[:, cc:cc + 1],
                                              in1=c.gnA[:, cc:cc + 1]), r=["gnB", "gnA"], w=["gnB"])
            vop(lambda e, cc=cc: e.tensor_sub(out=c.gnB[:, cc:cc + 1], in0=c.gnbs[:, cc:cc + 1],
                                              in1=c.gnB[:, cc:cc + 1]), r=["gnb_sb", "gnB"], w=["gnB"])
        vop(lambda e: e.tensor_copy(out=c.gnAb[:, :], in_=c.gnA[:, :]), r=["gnA"], w=["gnAb"])
        vop(lambda e: e.tensor_copy(out=c.gnBb[:, :], in_=c.gnB[:, :]), r=["gnB"], w=["gnBb"])

    def l0_attn_bias():
        wq3 = dr["wqkv"].rearrange("p (n f) -> p n f", n=8)
        for os6 in range(6):
            gdma(lambda e, os6=os6: e.dma_start(
                out=c.ws[:, 0:4096].rearrange("p (n f) -> p n f", n=8),
                in_=wq3[:, :, 512 * os6:512 * (os6 + 1)]), r=(), w=WSK(0, 4096), tag="wb")
            for cc in range(NCH):
                mm(pA[0:1, 1536:2048], c.gnBb[:, cc:cc + 1], c.ws[:, 512 * cc:512 * (cc + 1)],
                   cc == 0, cc == 7, r=["gnBb"] + WSK(0, 4096), w=["pA:3"])
            vop(lambda e: e.tensor_copy(out=c.trow[0:1, 0:512], in_=pA[0:1, 1536:2048]),
                r=["pA:3"], w=["trow"])
            for q4 in range(4):
                och = os6 * 4 + q4
                tr(pB[:, 1536 + q4:1537 + q4], c.trow[0:1, 128 * q4:128 * (q4 + 1)],
                   c.idf[0:1, 0:1], r=["trow", "idf_sb"], w=["pB:3"])
            vop(lambda e, os6=os6: e.tensor_add(out=c.bcall[:, 4 * os6:4 * (os6 + 1)],
                                                in0=pB[:, 1536:1540],
                                                in1=c.bqkvc[:, 4 * os6:4 * (os6 + 1)]),
                r=["pB:3", "bqkv_sb"], w=["bcall"])
        for vch in range(8):
            tr(pA[0:1, 1536 + 128 * (vch % 4):1664 + 128 * (vch % 4)], c.bcall[:, 16 + vch:17 + vch],
               c.idf, r=["bcall", "idf_sb"], w=["pA:3"])
            vop(lambda e, vch=vch: e.tensor_copy(
                out=c.trow[0:1, 0:512][0:1, 128 * (vch % 4):128 * (vch % 4 + 1)],
                in_=pA[0:1, 1536 + 128 * (vch % 4):1664 + 128 * (vch % 4)]),
                r=["pA:3"], w=["trow"])
            if vch % 4 == 3:
                half = vch // 4
                for s2 in range(1):
                    mm(pB[:, 1024 + 512 * half:1024 + 512 * (half + 1)], c.onesf[0:1, 0:128],
                       c.trow[0:1, 0:512], True, True,
                       r=["onesf_sb", "trow"], w=pbk(1024 + 512 * half, 1024 + 512 * (half + 1)))
                vop(lambda e, half=half: e.tensor_copy(
                    out=c.botile[:, 512 * half:512 * (half + 1)],
                    in_=pB[:, 1024 + 512 * half:1024 + 512 * (half + 1)]),
                    r=pbk(1024 + 512 * half, 1024 + 512 * (half + 1)), w=["botile"])

    def l0_attn():
        l0_attn_bias()
        wq3 = dr["wqkv"].rearrange("p (n f) -> p n f", n=8)

        def emit_qk(p):
            qb, kb = 1024 * (p % 2), 2048 + 1024 * (p % 2)
            for base, osrc in ((qb, 128 * p), (kb, 1024 + 128 * p)):
                gdma(lambda e, base=base, osrc=osrc: e.dma_start(
                    out=c.ws[:, base:base + 1024].rearrange("p (n f) -> p n f", n=8),
                    in_=wq3[:, :, osrc:osrc + 128]),
                    r=(), w=WSK(base, base + 1024), tag="wqkv")
            for base in (qb, kb):
                w3 = c.ws[:, base:base + 1024].rearrange("p (n f) -> p n f", n=8)
                ab = c.gnAb[:, :].rearrange("p (n o) -> p n o", o=1).to_broadcast([128, 8, 128])
                vop(lambda e, w3=w3, ab=ab: e.tensor_mul(out=w3, in0=w3, in1=ab),
                    r=WSK(base, base + 1024) + ["gnAb"], w=WSK(base, base + 1024))
            for dstbase, wsb, bci in ((R_QP, qb, p), (R_KP, kb, 8 + p)):
                for s4 in range(4):
                    bank = s4 % 2
                    for cc in range(NCH):
                        mm(pA[:, 512 * bank:512 * (bank + 1)],
                           c.ws[:, wsb + 128 * cc:wsb + 128 * (cc + 1)],
                           c.xnT[:, 2048 * cc + 512 * s4:2048 * cc + 512 * (s4 + 1)],
                           cc == 0, cc == 7,
                           r=WSK(wsb, wsb + 1024) + xkk(2048 * cc + 512 * s4, 2048 * cc + 512 * (s4 + 1)),
                           w=[f"pA:{bank}"])
                    sop(lambda e, dstbase=dstbase, s4=s4, bank=bank, bci=bci: e.activation(
                        out=R[:, dstbase + 512 * s4:dstbase + 512 * (s4 + 1)],
                        in_=pA[:, 512 * bank:512 * (bank + 1)], func=AF.Identity,
                        bias=c.bcall[:, bci:bci + 1], scale=1.0),
                        r=[f"pA:{bank}", "bcall"], w=ak(dstbase + 512 * s4, dstbase + 512 * (s4 + 1)))

        def emit_vdma(p):
            vb = 4096 + 1024 * (p % 2)
            gdma(lambda e, p=p, vb=vb: e.dma_start(
                out=c.ws[:, vb:vb + 1024].rearrange("p (n f) -> p n f", n=8),
                in_=wq3[:, :, 2048 + 128 * p:2048 + 128 * (p + 1)]),
                r=(), w=WSK(vb, vb + 1024), tag="wv")
            w3 = c.ws[:, vb:vb + 1024].rearrange("p (n f) -> p n f", n=8)
            ab = c.gnAb[:, :].rearrange("p (n o) -> p n o", o=1).to_broadcast([128, 8, 128])
            vop(lambda e, w3=w3, ab=ab: e.tensor_mul(out=w3, in0=w3, in1=ab),
                r=WSK(vb, vb + 1024) + ["gnAb"], w=WSK(vb, vb + 1024))
            gop(lambda e: e.memset(R[:, R_VP:R_VP + 2080], 1.0), r=(), w=ak(R_VP, R_VP + 2080))

        def emit_v_tile(p, tt):
            vb = 4096 + 1024 * (p % 2)
            for cc in range(NCH):
                mm(pA[:, 1536:1664],
                   c.xnT[:, 2048 * cc + 128 * tt:2048 * cc + 128 * (tt + 1)],
                   c.ws[:, vb + 128 * cc:vb + 128 * (cc + 1)],
                   cc == 0, cc == 7,
                   r=WSK(vb, vb + 1024) + xkk(2048 * cc + 128 * tt, 2048 * cc + 128 * (tt + 1)),
                   w=["pA:3"])
            dst = R[:, R_VP + 130 * tt:R_VP + 130 * tt + 130].rearrange(
                "p (a b) -> p a b", a=2)[:, :, 0:64]
            src = pA[:, 1536:1664].rearrange("p (a b) -> p a b", a=2)
            vbt3 = c.botile[:, 128 * p:128 * (p + 1)].rearrange("p (a b) -> p a b", a=2)
            vop(lambda e, dst=dst, src=src, vbt3=vbt3: e.tensor_add(out=dst, in0=src, in1=vbt3),
                r=["pA:3", "botile"], w=ak(R_VP + 130 * tt, R_VP + 130 * tt + 130))

        def emit_p1_qt(p, j, qt):
            r0 = 64 * j
            for kq in range(4):
                mm(pA[:, 1024:1536],
                   R[r0:r0 + 64, R_QP + 128 * qt:R_QP + 128 * (qt + 1)],
                   R[r0:r0 + 64, R_KP + 512 * kq:R_KP + 512 * (kq + 1)],
                   True, True,
                   r=ak(R_QP + 128 * qt, R_QP + 128 * (qt + 1)) + ak(R_KP + 512 * kq, R_KP + 512 * (kq + 1)),
                   w=["pA:2"])
                vop(lambda e, kq=kq: e.reduce_max(out=c.mcol2[:, kq:kq + 1], in_=pA[:, 1024:1536],
                                                  axis=AX.X), r=["pA:2"], w=["mcol2"])
            vop(lambda e, qt=qt: e.reduce_max(out=c.mcol[:, qt:qt + 1], in_=c.mcol2[:, 0:4],
                                              axis=AX.X), r=["mcol2"], w=["mcol"])

        def emit_p1_fold(p, j):
            vop(lambda e: e.reduce_max(out=c.tmpc[:, 1:2], in_=c.mcol[:, :], axis=AX.X),
                r=["mcol"], w=["tmpc"])
            tr(pA[0:1, 1536:1664], c.tmpc[:, 1:2], c.idf, r=["tmpc", "idf_sb"], w=["pA:3"])
            vop(lambda e: e.tensor_copy(out=c.trow[0:1, 256:384], in_=pA[0:1, 1536:1664]),
                r=["pA:3"], w=["trow"])
            vop(lambda e: e.reduce_max(out=c.trow[0:1, 511:512], in_=c.trow[0:1, 256:384], axis=AX.X),
                r=["trow"], w=["trow"])
            mm(pA[:, 1664:1665], c.onesf[0:1, 0:128], c.trow[0:1, 511:512], True, True,
               r=["onesf_sb", "trow"], w=["pA:3"])
            vop(lambda e: e.tensor_scalar(out=c.mbias[:, :], in0=pA[:, 1664:1665],
                                          scalar1=-0.125, scalar2=None, op0=ALU.mult),
                r=["pA:3"], w=["mbias"])

        def p2_okeys(qh):
            return pak(0, 1024) if qh == 0 else pbk(1024, 2048)

        def p2_oview(qh):
            return pA[0:65, 0:1024] if qh == 0 else pB[0:65, 1024:2048]

        def emit_p2_st(p, j, qh, kt):
            r0 = 64 * j
            sv = pB[:, 0:1024]
            for s in range(2):
                mm(sv[:, 512 * s:512 * (s + 1)],
                   R[r0:r0 + 64, R_KP + 128 * kt:R_KP + 128 * (kt + 1)],
                   R[r0:r0 + 64, R_QP + 1024 * qh + 512 * s:R_QP + 1024 * qh + 512 * (s + 1)],
                   True, True,
                   r=ak(R_KP + 128 * kt, R_KP + 128 * (kt + 1)) + ak(R_QP + 1024 * qh + 512 * s, R_QP + 1024 * qh + 512 * (s + 1)),
                   w=pbk(512 * s, 512 * (s + 1)))
            hb = kt % 2
            sop(lambda e, sv=sv, hb=hb: e.activation(
                out=c.ET[:, 1024 * hb:1024 * (hb + 1)], in_=sv, func=AF.Exp,
                bias=c.mbias[:, 0:1], scale=0.125),
                r=pbk(0, 1024) + ["mbias"], w=[f"ET:{hb}"])

        def emit_p2_pv(p, j, qh, kt):
            hb = kt % 2
            ov = p2_oview(qh)
            for s in range(2):
                mm(ov[:, 512 * s:512 * (s + 1)],
                   R[:, R_VP + 130 * kt + 65 * j:R_VP + 130 * kt + 65 * (j + 1)],
                   c.ET[:, 1024 * hb + 512 * s:1024 * hb + 512 * (s + 1)],
                   kt == 0, kt == 15,
                   r=[f"ET:{hb}"] + ak(R_VP + 130 * kt, R_VP + 130 * (kt + 1)),
                   w=p2_okeys(qh))

        def emit_p2_norm(p, j, qh):
            okeys = p2_okeys(qh)
            ov = p2_oview(qh)
            vop(lambda e, ov=ov: e.tensor_copy(out=c.scr1[64:65, 0:1024], in_=ov[64:65, 0:1024]),
                r=okeys, w=[SCR1[0]])
            vop(lambda e: e.reciprocal(out=c.scr1[64:65, 1024:2048], in_=c.scr1[64:65, 0:1024]),
                r=[SCR1[0]], w=[SCR1[1]])
            for s in range(2):
                mm(pA[0:64, 1024:1536], c.onesf[64:65, 0:64],
                   c.scr1[64:65, 1024 + 512 * s:1024 + 512 * (s + 1)], True, True,
                   r=["onesf_sb", SCR1[1]], w=["pA:2"])
                vop(lambda e, s=s: e.tensor_copy(out=c.scr1[0:64, 512 * s:512 * (s + 1)],
                                                 in_=pA[0:64, 1024:1536]),
                    r=["pA:2"], w=[SCR1[0]])
            dst0 = R_OT + 2048 * p + 1024 * qh
            if j == 0:
                vop(lambda e, ov=ov, dst0=dst0: e.tensor_mul(
                    out=R[0:64, dst0:dst0 + 1024], in0=ov[0:64, 0:1024],
                    in1=c.scr1[0:64, 0:1024]),
                    r=okeys + [SCR1[0]], w=ak(dst0, dst0 + 1024))
            else:
                vop(lambda e, ov=ov: e.tensor_mul(
                    out=c.ET[0:64, 0:1024], in0=ov[0:64, 0:1024],
                    in1=c.scr1[0:64, 0:1024]),
                    r=okeys + [SCR1[0]], w=["ET:0"])
                for s in range(2):
                    mm(pA[:, 1024:1536], c.s64b[:, :],
                       c.ET[0:64, 512 * s:512 * (s + 1)],
                       True, True, r=["s64_b", "ET:0"], w=["pA:2"])
                    vop(lambda e, s=s, dst0=dst0: e.tensor_copy(
                        out=R[64:128, dst0 + 512 * s:dst0 + 512 * (s + 1)],
                        in_=pA[64:128, 1024:1536]),
                        r=["pA:2"], w=ak(dst0 + 512 * s, dst0 + 512 * (s + 1)))

        for p in range(int(os.environ.get("K_NPAIRS", "8"))):
            emit_qk(p)
            emit_vdma(p)
            for u in range(16):
                emit_v_tile(p, u)
                emit_p1_qt(p, 0, u)
            emit_p1_fold(p, 0)
            units = []
            for qh in range(2):
                for kt in range(NT):
                    units.append((qh, kt))
            qi = 0
            for (qh, kt) in units:
                emit_p2_st(p, 0, qh, kt)
                emit_p2_pv(p, 0, qh, kt)
                if qi < 16 and (kt % 2 == 1):
                    emit_p1_qt(p, 1, qi)
                    qi += 1
                if kt == NT - 1:
                    emit_p2_norm(p, 0, qh)
            emit_p1_fold(p, 1)
            for (qh, kt) in units:
                emit_p2_st(p, 1, qh, kt)
                emit_p2_pv(p, 1, qh, kt)
                if kt == NT - 1:
                    emit_p2_norm(p, 1, qh)

        sdma(lambda e: e.dma_start(out=c.scr1[0:1, 0:1024], in_=dr["bo"][:, :]), r=(), w=[SCR1[0]])
        for s in range(2):
            mm(pB[:, 512 * s:512 * (s + 1)], c.onesf[0:1, 0:128],
               c.scr1[0:1, 512 * s:512 * (s + 1)], True, True,
               r=["onesf_sb", SCR1[0]], w=pbk(512 * s, 512 * (s + 1)))
        vop(lambda e: e.tensor_copy(out=c.botile[:, :], in_=pB[:, 0:1024]), r=pbk(0, 1024), w=["botile"])
        wo3 = dr["wol"].rearrange("p (n f) -> p n f", n=8)
        for os_ in range(2):
            gdma(lambda e, os_=os_: e.dma_start(
                out=c.ws[:, 0:4096].rearrange("p (n f) -> p n f", n=8),
                in_=wo3[:, :, 512 * os_:512 * (os_ + 1)]),
                r=(), w=WSK(0, 4096), tag="wo")
            for tt in range(NT):
                bank = tt % 4
                for pch in range(8):
                    mm(pA[:, 512 * bank:512 * (bank + 1)],
                       R[:, R_OT + 2048 * pch + 128 * tt:R_OT + 2048 * pch + 128 * (tt + 1)],
                       c.ws[:, 512 * pch:512 * pch + 512],
                       pch == 0, pch == 7,
                       r=ak(R_OT + 2048 * pch + 128 * tt, R_OT + 2048 * pch + 128 * (tt + 1)) + WSK(512 * pch, 512 * pch + 512),
                       w=[f"pA:{bank}"])
                xs = c.x[:, 1024 * tt + 512 * os_:1024 * tt + 512 * (os_ + 1)]
                vop(lambda e, xs=xs, bank=bank: e.tensor_add(out=xs, in0=xs,
                                                             in1=pA[:, 512 * bank:512 * (bank + 1)]),
                    r=[f"pA:{bank}", f"x:{tt}"], w=[f"x:{tt}"])
                vop(lambda e, xs=xs, os_=os_: e.tensor_add(out=xs, in0=xs,
                                                           in1=c.botile[:, 512 * os_:512 * (os_ + 1)]),
                    r=["botile", f"x:{tt}"], w=[f"x:{tt}"])

    def l0_ffn():
        gdma(lambda e: e.dma_start(out=R[0:16, 28672:30720], in_=dr["sel16"][:, :]),
             r=(), w=ak(28672, 30720), tag="sel16")
        for tt in range(NT):
            bank = tt % 2
            pv = pA[:, 512 * bank:512 * bank + 16]
            for cc in range(NCH):
                mm(pv, c.xnT[:, 2048 * cc + 128 * tt:2048 * cc + 128 * (tt + 1)],
                   c.rwts[:, 16 * cc:16 * (cc + 1)], cc == 0, cc == 7,
                   r=["rwt_b"] + xkk(2048 * cc + 128 * tt, 2048 * cc + 128 * (tt + 1)),
                   w=[f"pA:{bank}"])
            vop(lambda e, pv=pv: e.reduce_max(out=c.tmpc[:, 2:3], in_=pv, axis=AX.X),
                r=[f"pA:{bank}"], w=["tmpc"])
            vop(lambda e: e.tensor_scalar(out=c.tmpc[:, 2:3], in0=c.tmpc[:, 2:3], scalar1=-1.0,
                                          scalar2=None, op0=ALU.mult), r=["tmpc"], w=["tmpc"])
            sop(lambda e, pv=pv: e.activation(out=c.scr0[:, 0:16], in_=pv, func=AF.Exp,
                                              bias=c.tmpc[:, 2:3], scale=1.0),
                r=[f"pA:{bank}", "tmpc"], w=["scr0"])
            vop(lambda e: e.reduce_sum(out=c.tmpc[:, 3:4], in_=c.scr0[:, 0:16], axis=AX.X),
                r=["scr0"], w=["tmpc"])
            vop(lambda e: e.reciprocal(out=c.tmpc[:, 3:4], in_=c.tmpc[:, 3:4]), r=["tmpc"], w=["tmpc"])
            vop(lambda e: e.max(out=c.scr0[:, 16:24], in_=c.scr0[:, 0:16]), r=["scr0"], w=["scr0"])
            vop(lambda e: e.tensor_scalar(out=c.scr0[:, 24:40], in0=c.scr0[:, 0:16],
                                          scalar1=c.scr0[:, 19:20], scalar2=0.0,
                                          op0=ALU.subtract, op1=ALU.is_ge), r=["scr0"], w=["scr0"])
            vop(lambda e: e.scalar_tensor_tensor(out=c.scr0[:, 40:56], in0=c.scr0[:, 24:40],
                                                 scalar=c.tmpc[:, 3:4], in1=c.scr0[:, 0:16],
                                                 op0=ALU.mult, op1=ALU.mult),
                r=["scr0", "tmpc"], w=["scr0"])
            tr(pB[0:16, 1024 + 128 * (tt % 2):1024 + 128 * (tt % 2) + 128], c.scr0[:, 40:56], c.idf,
               r=["scr0", "idf_sb"], w=["pB:2"])
            vop(lambda e, tt=tt: e.tensor_copy(
                out=R[0:16, R_GATET + 128 * tt:R_GATET + 128 * (tt + 1)],
                in_=pB[0:16, 1024 + 128 * (tt % 2):1024 + 128 * (tt % 2) + 128]),
                r=["pB:2"], w=ak(R_GATET + 128 * tt, R_GATET + 128 * (tt + 1)))

        EXP = [("sut", "sdt", 0, False), ("sut", "sdt", 1, False)] + \
              [("rut", "rdt", r, True) for r in range(16)]
        for g in range(9):
            for gi in range(2):
                usrc, dsrc, idx, routed = EXP[2 * g + gi]
                ub = R_U0 if gi == 0 else R_U1
                db = R_D0 if gi == 0 else R_D1
                gdma(lambda e, usrc=usrc, idx=idx, ub=ub: e.dma_start(
                    out=R[:, ub:ub + 4096], in_=dr[usrc][idx]), r=(), w=ak(ub, ub + 4096), tag="U")
                gdma(lambda e, dsrc=dsrc, idx=idx, db=db: e.dma_start(
                    out=R[:, db:db + 4096], in_=dr[dsrc][idx]), r=(), w=ak(db, db + 4096), tag="D")
            for th in range(2):
                for gi in range(2):
                    usrc, dsrc, idx, routed = EXP[2 * g + gi]
                    ub = R_U0 if gi == 0 else R_U1
                    ab = R_A0 if gi == 0 else R_A1
                    for ec in range(4):
                        for s in range(2):
                            bank = (ec * 2 + s) % 2
                            for cc in range(NCH):
                                mm(pA[:, 512 * bank:512 * (bank + 1)],
                                   R[:, ub + 512 * cc + 128 * ec:ub + 512 * cc + 128 * (ec + 1)],
                                   c.xnT[:, 2048 * cc + 1024 * th + 512 * s:2048 * cc + 1024 * th + 512 * (s + 1)],
                                   cc == 0, cc == 7,
                                   r=ak(ub, ub + 4096) + xkk(2048 * cc + 1024 * th + 512 * s, 2048 * cc + 1024 * th + 512 * (s + 1)),
                                   w=[f"pA:{bank}"])
                            sop(lambda e, ab=ab, ec=ec, s=s, bank=bank: e.activation(
                                out=R[:, ab + 1024 * ec + 512 * s:ab + 1024 * ec + 512 * (s + 1)],
                                in_=pA[:, 512 * bank:512 * (bank + 1)], func=AF.Gelu),
                                r=[f"pA:{bank}"], w=ak(ab + 1024 * ec + 512 * s, ab + 1024 * ec + 512 * (s + 1)))
                    if routed:
                        for s in range(2):
                            mm(pB[:, 512 * s:512 * (s + 1)], R[0:16, 28672 + 128 * idx:28672 + 128 * (idx + 1)],
                               R[0:16, R_GATET + 1024 * th + 512 * s:R_GATET + 1024 * th + 512 * (s + 1)],
                               True, True,
                               r=ak(28672, 30720) + ak(R_GATET + 1024 * th + 512 * s, R_GATET + 1024 * th + 512 * (s + 1)),
                               w=[f"pB:{s}"])
                        for ec in range(4):
                            a3 = R[:, ab + 1024 * ec:ab + 1024 * (ec + 1)]
                            vop(lambda e, a3=a3: e.tensor_mul(out=a3, in0=a3, in1=pB[:, 0:1024]),
                                r=["pB:0", "pB:1"] + ak(ab + 1024 * ec, ab + 1024 * (ec + 1)),
                                w=ak(ab + 1024 * ec, ab + 1024 * (ec + 1)))
                for tt8 in range(8):
                    for os_ in range(2):
                        bank = 2 + (tt8 * 2 + os_) % 2
                        nmm = 0
                        for gi in range(2):
                            ab = R_A0 if gi == 0 else R_A1
                            db = R_D0 if gi == 0 else R_D1
                            for ec in range(4):
                                mm(pB[:, 512 * bank:512 * (bank + 1)],
                                   R[:, ab + 1024 * ec + 128 * tt8:ab + 1024 * ec + 128 * (tt8 + 1)],
                                   R[:, db + 1024 * ec + 512 * os_:db + 1024 * ec + 512 * (os_ + 1)],
                                   nmm == 0, nmm == 7,
                                   r=ak(ab + 1024 * ec + 128 * tt8, ab + 1024 * ec + 128 * (tt8 + 1)) + ak(db, db + 4096),
                                   w=[f"pB:{bank}"])
                                nmm += 1
                        tglob = th * 8 + tt8
                        xs = c.x[:, 1024 * tglob + 512 * os_:1024 * tglob + 512 * (os_ + 1)]
                        vop(lambda e, xs=xs, bank=bank: e.tensor_add(
                            out=xs, in0=xs, in1=pB[:, 512 * bank:512 * (bank + 1)]),
                            r=[f"pB:{bank}", f"x:{tglob}"], w=[f"x:{tglob}"])

    def l1_attn():
        sdma(lambda e: e.dma_start(out=c.scr1[0:1, 0:1024], in_=dr["bo1"][:, :]), r=(), w=[SCR1[0]])
        for s in range(2):
            mm(pB[:, 512 * s:512 * (s + 1)], c.onesf[0:1, 0:128],
               c.scr1[0:1, 512 * s:512 * (s + 1)], True, True,
               r=["onesf_sb", SCR1[0]], w=pbk(512 * s, 512 * (s + 1)))
        vop(lambda e: e.tensor_copy(out=c.botile[:, :], in_=pB[:, 0:1024]), r=pbk(0, 1024), w=["botile"])
        wv3 = dr["wv1"].rearrange("p (n f) -> p n f", n=8)
        for ch in range(NCH):
            wb = 4096 + 1024 * (ch % 2)
            gdma(lambda e, ch=ch, wb=wb: e.dma_start(
                out=c.ws[:, wb:wb + 1024].rearrange("p (n f) -> p n f", n=8),
                in_=wv3[:, :, 128 * ch:128 * (ch + 1)]), r=(), w=WSK(wb, wb + 1024), tag="wv1")
            for s4 in range(4):
                bank = s4 % 2
                for cc in range(NCH):
                    mm(pB[:, 512 * bank:512 * (bank + 1)],
                       c.ws[:, wb + 128 * cc:wb + 128 * (cc + 1)],
                       c.xnT[:, 2048 * cc + 512 * s4:2048 * cc + 512 * (s4 + 1)],
                       cc == 0, cc == 7,
                       r=WSK(wb, wb + 1024) + xkk(2048 * cc + 512 * s4, 2048 * cc + 512 * (s4 + 1)),
                       w=[f"pB:{bank}"])
                sop(lambda e, ch=ch, s4=s4, bank=bank: e.activation(
                    out=R[:, R_VBUF + 2048 * ch + 512 * s4:R_VBUF + 2048 * ch + 512 * (s4 + 1)],
                    in_=pB[:, 512 * bank:512 * (bank + 1)], func=AF.Identity,
                    bias=c.bv1s[:, ch:ch + 1], scale=1.0),
                    r=[f"pB:{bank}", "bv1_sb"],
                    w=ak(R_VBUF + 2048 * ch + 512 * s4, R_VBUF + 2048 * ch + 512 * (s4 + 1)))
        kcv = c.scr1[:, :].bitcast(BF16)
        wq3l = dr["wq1"].rearrange("p (n f) -> p n f", n=8)
        wk3l = dr["wk1"].rearrange("p (n f) -> p n f", n=8)
        for ch in range(NCH):
            wbq = 1024 * (ch % 2)
            wbk = 2048 + 1024 * (ch % 2)
            gdma(lambda e, ch=ch, wbq=wbq: e.dma_start(
                out=c.ws[:, wbq:wbq + 1024].rearrange("p (n f) -> p n f", n=8),
                in_=wq3l[:, :, 128 * ch:128 * (ch + 1)]), r=(), w=WSK(wbq, wbq + 1024), tag="wq1")
            gdma(lambda e, ch=ch, wbk=wbk: e.dma_start(
                out=c.ws[:, wbk:wbk + 1024].rearrange("p (n f) -> p n f", n=8),
                in_=wk3l[:, :, 128 * ch:128 * (ch + 1)]), r=(), w=WSK(wbk, wbk + 1024), tag="wk1")
            for (wb, bsb, bkey, dst, dkey) in (
                    (wbq, c.bq1s, "bq1_sb", c.ET, ["ET:0", "ET:1"]),
                    (wbk, c.bk1s, "bk1_sb", kcv, SCR1)):
                for s4 in range(4):
                    bank = s4 % 2
                    for cc in range(NCH):
                        mm(pB[:, 512 * bank:512 * (bank + 1)],
                           c.ws[:, wb + 128 * cc:wb + 128 * (cc + 1)],
                           c.xnT[:, 2048 * cc + 512 * s4:2048 * cc + 512 * (s4 + 1)],
                           cc == 0, cc == 7,
                           r=WSK(wb, wb + 1024) + xkk(2048 * cc + 512 * s4, 2048 * cc + 512 * (s4 + 1)),
                           w=[f"pB:{bank}"])
                    sop(lambda e, ch=ch, bank=bank, s4=s4, bsb=bsb: e.activation(
                        out=c.scr0[:, 512 * (s4 % 2):512 * (s4 % 2) + 512],
                        in_=pB[:, 512 * bank:512 * (bank + 1)], func=AF.Square,
                        bias=bsb[:, ch:ch + 1], scale=1.0),
                        r=[f"pB:{bank}", bkey], w=["scr0"])
                    sop(lambda e, s4=s4, dst=dst: e.activation(
                        out=dst[:, 512 * s4:512 * (s4 + 1)],
                        in_=c.scr0[:, 512 * (s4 % 2):512 * (s4 % 2) + 512], func=AF.Exp, scale=-0.5),
                        r=["scr0"], w=dkey)
            vop(lambda e: e.tensor_mul(out=c.ET[:, :], in0=c.ET[:, :], in1=kcv[:, 0:2048]),
                r=["ET:0", "ET:1"] + SCR1, w=["ET:0", "ET:1"])
            for s4 in range(4):
                mm(pA[0:16, 512 * s4:512 * (s4 + 1)], c.eheadb[:, 16 * ch:16 * (ch + 1)],
                   c.ET[:, 512 * s4:512 * (s4 + 1)], ch == 0, ch == 7,
                   r=["ehead_b", "ET:0", "ET:1"], w=[f"pA:{s4}"])
        vop(lambda e: e.tensor_copy(out=c.ET[0:16, 0:2048], in_=pA[0:16, 0:2048]),
            r=pak(0, 2048), w=["ET:0", "ET:1"])
        for ch in range(NCH):
            for s4 in range(4):
                bank = s4 % 2
                mm(pB[:, 512 * bank:512 * (bank + 1)], c.ehead2b[:, 128 * ch:128 * (ch + 1)],
                   c.ET[0:16, 512 * s4:512 * (s4 + 1)], True, True,
                   r=["ehead2_b", "ET:0", "ET:1"], w=[f"pB:{bank}"])
                vv = R[:, R_VBUF + 2048 * ch + 512 * s4:R_VBUF + 2048 * ch + 512 * (s4 + 1)]
                vk = ak(R_VBUF + 2048 * ch + 512 * s4, R_VBUF + 2048 * ch + 512 * (s4 + 1))
                vop(lambda e, vv=vv, bank=bank: e.tensor_mul(
                    out=vv, in0=vv, in1=pB[:, 512 * bank:512 * (bank + 1)]),
                    r=[f"pB:{bank}"] + vk, w=vk)
        wo3l = dr["wo1"].rearrange("p (n f) -> p n f", n=8)
        for os_ in range(2):
            gdma(lambda e, os_=os_: e.dma_start(
                out=c.ws[:, 0:4096].rearrange("p (n f) -> p n f", n=8),
                in_=wo3l[:, :, 512 * os_:512 * (os_ + 1)]), r=(), w=WSK(0, 4096), tag="wo1")
            for tt in range(NT):
                bank = tt % 4
                for ch in range(NCH):
                    mm(pA[:, 512 * bank:512 * (bank + 1)],
                       R[:, R_VBUF + 2048 * ch + 128 * tt:R_VBUF + 2048 * ch + 128 * (tt + 1)],
                       c.ws[:, 512 * ch:512 * ch + 512], ch == 0, ch == 7,
                       r=ak(R_VBUF + 2048 * ch + 128 * tt, R_VBUF + 2048 * ch + 128 * (tt + 1)) + WSK(512 * ch, 512 * ch + 512),
                       w=[f"pA:{bank}"])
                xs = c.x[:, 1024 * tt + 512 * os_:1024 * tt + 512 * (os_ + 1)]
                vop(lambda e, xs=xs, bank=bank: e.tensor_add(out=xs, in0=xs,
                                                             in1=pA[:, 512 * bank:512 * (bank + 1)]),
                    r=[f"pA:{bank}", f"x:{tt}"], w=[f"x:{tt}"])
                vop(lambda e, xs=xs, os_=os_: e.tensor_add(out=xs, in0=xs,
                                                           in1=c.botile[:, 512 * os_:512 * (os_ + 1)]),
                    r=["botile", f"x:{tt}"], w=[f"x:{tt}"])

    def l1_ffn():
        for tt in range(NT):
            bank = tt % 2
            pv = pA[:, 512 * bank:512 * bank + 8]
            for cc in range(NCH):
                mm(pv, c.xnT[:, 2048 * cc + 128 * tt:2048 * cc + 128 * (tt + 1)],
                   c.gwls[:, 8 * cc:8 * (cc + 1)], cc == 0, cc == 7,
                   r=["gwl_b"] + xkk(2048 * cc + 128 * tt, 2048 * cc + 128 * (tt + 1)),
                   w=[f"pA:{bank}"])
            vop(lambda e, pv=pv: e.reduce_max(out=c.tmpc[:, 2:3], in_=pv, axis=AX.X),
                r=[f"pA:{bank}"], w=["tmpc"])
            vop(lambda e: e.tensor_scalar(out=c.tmpc[:, 2:3], in0=c.tmpc[:, 2:3], scalar1=-1.0,
                                          scalar2=None, op0=ALU.mult), r=["tmpc"], w=["tmpc"])
            sop(lambda e, pv=pv: e.activation(out=c.scr0[:, 0:8], in_=pv, func=AF.Exp,
                                              bias=c.tmpc[:, 2:3], scale=1.0),
                r=[f"pA:{bank}", "tmpc"], w=["scr0"])
            vop(lambda e: e.reduce_sum(out=c.tmpc[:, 3:4], in_=c.scr0[:, 0:8], axis=AX.X),
                r=["scr0"], w=["tmpc"])
            vop(lambda e: e.reciprocal(out=c.tmpc[:, 3:4], in_=c.tmpc[:, 3:4]), r=["tmpc"], w=["tmpc"])
            vop(lambda e, tt=tt: e.tensor_scalar(out=c.probs[:, 8 * tt:8 * (tt + 1)],
                                                 in0=c.scr0[:, 0:8], scalar1=c.tmpc[:, 3:4],
                                                 scalar2=None, op0=ALU.mult),
                r=["scr0", "tmpc"], w=["probs"])
        vop(lambda e: e.tensor_copy(out=c.probsb[:, :], in_=c.probs[:, :]), r=["probs"], w=["probsb"])
        vop(lambda e: e.memset(c.bis[:, 0:1], 0.0), r=(), w=["bis"])
        vop(lambda e: e.memset(c.bis[:, 1:2], 1.0), r=(), w=["bis"])

        def mask_from(colidx, final):
            tr(pB[0:1, 1536:1544], c.bis[:, colidx:colidx + 1], c.idf[0:8, 0:8], r=["bis", "idf_sb"], w=["pB:3"])
            vop(lambda e: e.tensor_copy(out=c.trow[0:1, 0:8], in_=pB[0:1, 1536:1544]),
                r=["pB:3"], w=["trow"])
            vop(lambda e: e.tensor_copy(
                out=c.trow[0:1, 128:256].rearrange("a (o f) -> a o f", o=16),
                in_=c.trow[0:1, 0:8].rearrange("a (o f) -> a o f", o=1).to_broadcast([1, 16, 8])),
                r=["trow"], w=["trow"])
            mm(pA[:, 1536:1664], c.onesf[0:1, 0:128], c.trow[0:1, 128:256], True, True,
               r=["onesf_sb", "trow"], w=["pA:3"])
            vop(lambda e: e.tensor_tensor(out=c.mask[:, :], in0=c.probs[:, :],
                                          in1=pA[:, 1536:1664], op=ALU.is_ge),
                r=["probs", "pA:3"], w=["mask"])
            if final:
                return
            mm(pB[:, 1664:1665], c.mask[:, :], c.onesb[:, 0:1], True, True,
               r=["mask", "onesf_b"], w=["pB:3"])
            vop(lambda e: e.tensor_copy(out=c.cnt1b[:, :], in_=pB[:, 1664:1665]),
                r=["pB:3"], w=["cnt1b"])
            mm(pA[0:8, 1664:1665], c.sel8b[:, :], c.cnt1b[:, :], True, True,
               r=["sel8_b", "cnt1b"], w=["pA:3"])
            vop(lambda e: e.tensor_scalar(out=c.bis[:, 3:4], in0=pA[0:8, 1664:1665],
                                          scalar1=511.5, scalar2=None, op0=ALU.is_ge),
                r=["pA:3"], w=["bis"])
            vop(lambda e: e.tensor_sub(out=c.bis[:, 4:5], in0=c.bis[:, 2:3], in1=c.bis[:, 0:1]),
                r=["bis"], w=["bis"])
            vop(lambda e: e.tensor_sub(out=c.bis[:, 5:6], in0=c.bis[:, 1:2], in1=c.bis[:, 2:3]),
                r=["bis"], w=["bis"])
            vop(lambda e: e.scalar_tensor_tensor(out=c.bis[:, 0:1], in0=c.bis[:, 4:5],
                                                 scalar=c.bis[:, 3:4], in1=c.bis[:, 0:1],
                                                 op0=ALU.mult, op1=ALU.add), r=["bis"], w=["bis"])
            vop(lambda e: e.scalar_tensor_tensor(out=c.bis[:, 1:2], in0=c.bis[:, 5:6],
                                                 scalar=c.bis[:, 3:4], in1=c.bis[:, 2:3],
                                                 op0=ALU.mult, op1=ALU.add), r=["bis"], w=["bis"])

        for it in range(30):
            vop(lambda e: e.tensor_add(out=c.bis[:, 2:3], in0=c.bis[:, 0:1], in1=c.bis[:, 1:2]),
                r=["bis"], w=["bis"])
            vop(lambda e: e.tensor_scalar(out=c.bis[:, 2:3], in0=c.bis[:, 2:3], scalar1=0.5,
                                          scalar2=None, op0=ALU.mult), r=["bis"], w=["bis"])
            mask_from(2, False)
        mask_from(0, True)
        vop(lambda e: e.tensor_copy(out=c.maskf[:, :], in_=c.mask[:, :]), r=["mask"], w=["maskf"])
        for tt in range(NT):
            bank = tt % 2
            mm(pB[:, 512 * bank:512 * bank + 8], c.trilb[:, :], c.mask[:, 8 * tt:8 * (tt + 1)],
               True, True, r=["tril_b", "mask"], w=[f"pB:{bank}"])
            vop(lambda e, tt=tt, bank=bank: e.tensor_copy(
                out=c.cum[:, 8 * tt:8 * (tt + 1)], in_=pB[:, 512 * bank:512 * bank + 8]),
                r=[f"pB:{bank}"], w=["cum"])
            mm(pB[0:1, 1024 + 8 * tt:1024 + 8 * (tt + 1)], c.onesb[:, 0:1], c.mask[:, 8 * tt:8 * (tt + 1)],
               True, True, r=["onesf_b", "mask"], w=["pB:2"])
            vop(lambda e, tt=tt: e.tensor_copy(out=c.totrow[0:1, 8 * tt:8 * (tt + 1)],
                                               in_=pB[0:1, 1024 + 8 * tt:1024 + 8 * (tt + 1)]),
                r=["pB:2"], w=["totrow"])
        tr(pB[:, 1536:1537], c.totrow[0:1, 0:128], c.idf[0:1, 0:1], r=["totrow", "idf_sb"], w=["pB:3"])
        vop(lambda e: e.tensor_copy(out=c.totb[:, :], in_=pB[:, 1536:1537]), r=["pB:3"], w=["totb"])
        mm(pB[:, 1537:1538], c.trieb[:, :], c.totb[:, :], True, True,
           r=["trie_b", "totb"], w=["pB:3"])
        vop(lambda e: e.tensor_copy(out=c.tmpc[:, 1:2], in_=pB[:, 1537:1538]), r=["pB:3"], w=["tmpc"])
        tr(pB[0:1, 1544:1672], c.tmpc[:, 1:2], c.idf, r=["tmpc", "idf_sb"], w=["pB:3"])
        vop(lambda e: e.tensor_copy(out=c.trow[0:1, 256:384], in_=pB[0:1, 1544:1672]),
            r=["pB:3"], w=["trow"])
        mm(pA[:, 1536:1664], c.onesf[0:1, 0:128], c.trow[0:1, 256:384], True, True,
           r=["onesf_sb", "trow"], w=["pA:3"])
        vop(lambda e: e.tensor_copy(out=c.offsb[:, :], in_=pA[:, 1536:1664]), r=["pA:3"], w=["offsb"])
        for e_ in range(8):
            for tt in range(NT):
                vop(lambda e, e_=e_, tt=tt: e.tensor_scalar(
                    out=c.slots[:, tt:tt + 1], in0=c.cum[:, 8 * tt + e_:8 * tt + e_ + 1],
                    scalar1=c.offsb[:, 8 * tt + e_:8 * tt + e_ + 1], scalar2=32767.0,
                    op0=ALU.add, op1=ALU.add), r=["cum", "offsb"], w=["slots"])
                vop(lambda e, e_=e_, tt=tt: e.tensor_scalar(
                    out=c.slots[:, tt:tt + 1], in0=c.slots[:, tt:tt + 1],
                    scalar1=c.maskf[:, 8 * tt + e_:8 * tt + e_ + 1], scalar2=32768.0,
                    op0=ALU.mult, op1=ALU.subtract), r=["slots", "maskf"], w=["slots"])
            for tt in range(NT):
                vop(lambda e, tt=tt: e.tensor_scalar(
                    out=R[:, R_SEL + 512 * tt:R_SEL + 512 * (tt + 1)], in0=c.iota512[:, :],
                    scalar1=c.slots[:, tt:tt + 1], scalar2=0.0,
                    op0=ALU.subtract, op1=ALU.is_equal),
                    r=["iota512_sb", "slots"], w=ak(R_SEL + 512 * tt, R_SEL + 512 * (tt + 1)))
            for cc in range(NCH):
                bank = cc % 2
                for tt in range(NT):
                    mm(pA[:, 512 * bank:512 * (bank + 1)],
                       R[:, R_XNTOK + 1024 * tt + 128 * cc:R_XNTOK + 1024 * tt + 128 * (cc + 1)],
                       R[:, R_SEL + 512 * tt:R_SEL + 512 * (tt + 1)],
                       tt == 0, tt == 15,
                       r=ak(R_XNTOK + 1024 * tt + 128 * cc, R_XNTOK + 1024 * tt + 128 * (cc + 1)) + ak(R_SEL + 512 * tt, R_SEL + 512 * (tt + 1)),
                       w=[f"pA:{bank}"])
                vop(lambda e, cc=cc, bank=bank: e.tensor_copy(
                    out=R[:, R_XIN + 512 * cc:R_XIN + 512 * (cc + 1)],
                    in_=pA[:, 512 * bank:512 * (bank + 1)]),
                    r=[f"pA:{bank}"], w=ak(R_XIN + 512 * cc, R_XIN + 512 * (cc + 1)))
            for tt in range(NT):
                mm(pB[0:1, 1024:1536], c.probsb[:, 8 * tt + e_:8 * tt + e_ + 1],
                   R[:, R_SEL + 512 * tt:R_SEL + 512 * (tt + 1)], tt == 0, tt == 15,
                   r=["probsb"] + ak(R_SEL + 512 * tt, R_SEL + 512 * (tt + 1)), w=["pB:2"])
            vop(lambda e: e.tensor_copy(out=c.trow[0:1, 0:512], in_=pB[0:1, 1024:1536]),
                r=["pB:2"], w=["trow"])
            for sc in range(4):
                tr(pB[:, 1536 + sc:1537 + sc], c.trow[0:1, 128 * sc:128 * (sc + 1)], c.idf[0:1, 0:1],
                   r=["trow", "idf_sb"], w=["pB:3"])
            vop(lambda e: e.tensor_copy(out=c.pcols[:, 0:4], in_=pB[:, 1536:1540]),
                r=["pB:3"], w=["pcols_sb"])
            w13 = dr["w1l"][e_].rearrange("p (n f) -> p n f", n=8)
            for hg in range(16):
                wb = 2048 * (hg % 2)
                gdma(lambda e, hg=hg, wb=wb: e.dma_start(
                    out=c.ws[:, wb:wb + 2048].rearrange("p (n f) -> p n f", n=8),
                    in_=w13[:, :, 256 * hg:256 * (hg + 1)]), r=(), w=WSK(wb, wb + 2048), tag="w1")
                for hj in range(2):
                    hc = 2 * hg + hj
                    bank = hc % 2
                    for cc in range(NCH):
                        mm(pA[:, 512 * bank:512 * (bank + 1)],
                           c.ws[:, wb + 256 * cc + 128 * hj:wb + 256 * cc + 128 * (hj + 1)],
                           R[:, R_XIN + 512 * cc:R_XIN + 512 * (cc + 1)],
                           cc == 0, cc == 7,
                           r=WSK(wb, wb + 2048) + ak(R_XIN + 512 * cc, R_XIN + 512 * (cc + 1)),
                           w=[f"pA:{bank}"])
                    sop(lambda e, hc=hc, bank=bank: e.activation(
                        out=c.xnT[:, 512 * hc:512 * (hc + 1)],
                        in_=pA[:, 512 * bank:512 * (bank + 1)], func=AF.Gelu),
                        r=[f"pA:{bank}"], w=xkk(512 * hc, 512 * (hc + 1)))
            w23 = dr["w2l"][e_].rearrange("p (n f) -> p n f", n=32)
            for cs in range(2):
                for hg in range(8):
                    wb = 2048 * (hg % 2)
                    gdma(lambda e, hg=hg, cs=cs, wb=wb: e.dma_start(
                        out=c.ws[:, wb:wb + 2048].rearrange("p (n f) -> p n f", n=4),
                        in_=w23[:, 4 * hg:4 * (hg + 1), 512 * cs:512 * (cs + 1)]),
                        r=(), w=WSK(wb, wb + 2048), tag="w2")
                    for hj in range(4):
                        hc = 4 * hg + hj
                        for st in range(4):
                            mm(pB[:, 512 * st:512 * (st + 1)],
                               c.xnT[:, 512 * hc + 128 * st:512 * hc + 128 * (st + 1)],
                               c.ws[:, wb + 512 * hj:wb + 512 * (hj + 1)],
                               hc == 0, hc == 31,
                               r=WSK(wb, wb + 2048) + xkk(512 * hc + 128 * st, 512 * hc + 128 * (st + 1)),
                               w=[f"pB:{st}"])
                for st in range(4):
                    vop(lambda e, st=st, cs=cs: e.tensor_scalar(
                        out=R[:, R_Y + 1024 * st + 512 * cs:R_Y + 1024 * st + 512 * (cs + 1)],
                        in0=pB[:, 512 * st:512 * (st + 1)], scalar1=c.pcols[:, st:st + 1],
                        scalar2=None, op0=ALU.mult),
                        r=[f"pB:{st}", "pcols_sb"], w=ak(R_Y + 1024 * st + 512 * cs, R_Y + 1024 * st + 512 * (cs + 1)))
            for tt in range(NT):
                tr(pB[0:1, 1024 + 128 * (tt % 2):1024 + 128 * (tt % 2) + 128],
                   c.slots[:, tt:tt + 1], c.idf, r=["slots", "idf_sb"], w=["pB:2"])
                vop(lambda e, tt=tt: e.tensor_copy(
                    out=c.scr1[0:1, 128 * tt:128 * (tt + 1)],
                    in_=pB[0:1, 1024 + 128 * (tt % 2):1024 + 128 * (tt % 2) + 128]),
                    r=["pB:2"], w=[SCR1[tt // 8]])
            for s in range(4):
                mm(pA[:, 512 * s:512 * (s + 1)], c.onesf[0:1, 0:128],
                   c.scr1[0:1, 512 * s:512 * (s + 1)], True, True,
                   r=["onesf_sb"] + SCR1, w=[f"pA:{s}"])
            for s in range(4):
                vop(lambda e, s=s: e.tensor_copy(out=c.scr1[:, 512 * s:512 * (s + 1)],
                                                 in_=pA[:, 512 * s:512 * (s + 1)]),
                    r=[f"pA:{s}"], w=[SCR1[s // 2]])
            for sc in range(4):
                vop(lambda e, sc=sc: e.tensor_scalar(
                    out=R[:, R_SEL + 2048 * sc:R_SEL + 2048 * (sc + 1)], in0=c.scr1[:, :],
                    scalar1=c.iotacol[:, sc:sc + 1], scalar2=0.0,
                    op0=ALU.subtract, op1=ALU.is_equal),
                    r=SCR1 + ["iotacol_sb"], w=ak(R_SEL + 2048 * sc, R_SEL + 2048 * (sc + 1)))
            for tt in range(NT):
                for cs in range(2):
                    bank = (tt * 2 + cs) % 2
                    for sc in range(4):
                        mm(pA[:, 512 * bank:512 * (bank + 1)],
                           R[:, R_SEL + 2048 * sc + 128 * tt:R_SEL + 2048 * sc + 128 * (tt + 1)],
                           R[:, R_Y + 1024 * sc + 512 * cs:R_Y + 1024 * sc + 512 * (cs + 1)],
                           sc == 0, sc == 3,
                           r=ak(R_SEL + 2048 * sc + 128 * tt, R_SEL + 2048 * sc + 128 * (tt + 1)) + ak(R_Y + 1024 * sc + 512 * cs, R_Y + 1024 * sc + 512 * (cs + 1)),
                           w=[f"pA:{bank}"])
                    xs = c.x[:, 1024 * tt + 512 * cs:1024 * tt + 512 * (cs + 1)]
                    vop(lambda e, xs=xs, bank=bank: e.tensor_add(
                        out=xs, in0=xs, in1=pA[:, 512 * bank:512 * (bank + 1)]),
                        r=[f"pA:{bank}", f"x:{tt}"], w=[f"x:{tt}"])

    def store():
        sdma(lambda e: e.dma_start(
            out=c.out.rearrange("(n p) f -> p n f", p=128),
            in_=c.x[:, :].rearrange("p (n f) -> p n f", n=NT)), r=XT(), w=(), tag="stx")

    stages = ["ln0", "xnt0", "gn", "attn0", "ffn0", "ln1", "xnt1", "attn1", "ffn1"]
    stop = len(stages) if upto == "full" else stages.index(upto) + 1
    si = 0
    ln(0); si += 1
    if si < stop:
        build_xnT(); si += 1
    if si < stop:
        gn(); si += 1
    if si < stop:
        l0_attn(); si += 1
    if si < stop:
        l0_ffn(); si += 1
    if si < stop:
        ln(1); si += 1
    if si < stop:
        build_xnT(); si += 1
    if si < stop:
        l1_attn(); si += 1
    if si < stop:
        l1_ffn(); si += 1
    if "xntok" in taps:
        tap("xntok", R[:, R_XNTOK:R_XNTOK + 16384], ak(R_XNTOK, R_XNTOK + 16384), (128, 16384), BF16)
    if "xnt" in taps:
        tap("xnt", c.xnT[:, :], xkk(0, 16384), (128, 16384), BF16)
    if "gnab" in taps:
        tap("gnab", c.gnA[:, :], ["gnA"], (128, 8), F32)
        tap("gnbb", c.gnB[:, :], ["gnB"], (128, 8), F32)
    if "dbg1" in taps:
        tap("dbg1", c.offsb[0:1, :], ["offsb"], (1, 128), F32)
        tap("dbg2", c.offsb[:, :], ["offsb"], (128, 128), F32)
        tap("bqkvd", c.bqkvc[:, :], ["bqkv_sb"], (128, 24), F32)
        tap("gnbbd", c.gnBb[:, :], ["gnBb"], (128, 8), BF16)
    if "qkv" in taps:
        tap("qp", R[:, R_QP:R_QP + 2048], ak(R_QP, R_QP + 2048), (128, 2048), BF16)
        tap("kp", R[:, R_KP:R_KP + 2048], ak(R_KP, R_KP + 2048), (128, 2048), BF16)
        tap("vp", R[:, R_VP:R_VP + 2080], ak(R_VP, R_VP + 2080), (128, 2080), BF16)
        tap("mbias", c.mbias[:, :], ["mbias"], (128, 1), F32)
        tap("bcol", c.bcol[:, :], ["bcol"], (128, 4), F32)
        tap("vbt", c.vbt[:, :], ["vbt"], (128, 128), BF16)
    if "ot" in taps:
        tap("ot", R[:, R_OT:R_OT + 16384], ak(R_OT, R_OT + 16384), (128, 16384), BF16)
    if "gatet" in taps:
        tap("gatet", R[0:16, R_GATET:R_GATET + 2048], ak(R_GATET, R_GATET + 2048), (16, 2048), BF16)
    if "probs" in taps:
        tap("probs", c.probs[:, :], ["probs"], (128, 128), F32)
    if "mask" in taps:
        tap("mask", c.mask[:, :], ["mask"], (128, 128), BF16)
    if "cum" in taps:
        tap("cum", c.cum[:, :], ["cum"], (128, 128), F32)
        tap("offsb", c.offsb[:, :], ["offsb"], (128, 128), F32)
    if "coefb" in taps:
        tap("coefb", c.ET[0:16, :], ["ET:0", "ET:1"], (16, 2048), BF16)
    store()


_NC_CACHE = {}


def kernel(**inputs):
    host = host_prep(inputs)
    x_full = np.asarray(inputs["L0_attn_x"], dtype=np.float32)
    if "full" not in _NC_CACHE:
        _NC_CACHE["full"] = build_program()
    nc = _NC_CACHE["full"]
    in_maps = []
    for b in range(8):
        m = dict(host)
        m["x"] = np.ascontiguousarray(x_full[b])
        in_maps.append(m)
    res = run_bass_kernel_spmd(nc, in_maps, core_ids=list(range(8)))
    out = np.stack([np.asarray(res.results[i]["out"]) for i in range(8)])
    return out.astype(np.float32)


# version 03
# speedup vs baseline: 1.2053x; 1.2053x over previous
import math
import os
import numpy as np
import ml_dtypes
from contextlib import ExitStack

import concourse.bass as bass
import concourse.mybir as mybir
from concourse.bass_utils import run_bass_kernel_spmd

F32 = mybir.dt.float32
BF16 = mybir.dt.bfloat16
AF = mybir.ActivationFunctionType
ALU = mybir.AluOpType
AX = mybir.AxisListType

S = 2048
C = 1024
NT = S // 128
NCH = C // 128
NHEAD = 16
DH = 64

ENGINES = ("tensor", "vector", "scalar", "gpsimd", "sync")


class Em:
    def __init__(self, n_dma_sems=24):
        self.evs = []
        self.eng_ops = {e: [] for e in ENGINES}
        self.lw = {}
        self.rd = {}
        self.n_dma = n_dma_sems
        self.dma_use = [0] * n_dma_sems
        self.dma_rr = 0
        self.emitted = []

    @staticmethod
    def _keys(x):
        if x is None:
            return []
        if isinstance(x, str):
            return [x]
        out = []
        for e in x:
            if isinstance(e, str):
                out.append(e)
            else:
                out.extend(e)
        return out

    def _deps(self, reads, writes):
        deps = set()
        for k in reads:
            if k in self.lw:
                deps.add(self.lw[k])
        for k in writes:
            if k in self.lw:
                deps.add(self.lw[k])
            for r in self.rd.get(k, ()):
                deps.add(r)
        return deps

    def _record(self, kind, eng, fn, reads, writes, tag, extra):
        reads = self._keys(reads)
        writes = self._keys(writes)
        evid = len(self.evs)
        ev = dict(kind=kind, eng=eng, fn=fn, deps=self._deps(reads, writes),
                  marked=False, tag=tag)
        ev.update(extra)
        self.evs.append(ev)
        self.eng_ops[eng].append(evid)
        for k in reads:
            self.rd.setdefault(k, []).append(evid)
        for k in writes:
            self.lw[k] = evid
            self.rd[k] = []
        return evid

    def op(self, eng, fn, reads=(), writes=(), tag=""):
        return self._record("op", eng, fn, reads, writes, tag, {})

    def dma(self, eng, fn, reads=(), writes=(), tag=""):
        q = self.dma_rr
        self.dma_rr = (self.dma_rr + 1) % self.n_dma
        self.dma_use[q] += 1
        return self._record("dma", eng, fn, reads, writes, tag,
                            dict(q=q, use=self.dma_use[q]))

    def build(self, nc, es):
        for ev in self.evs:
            for d in ev["deps"]:
                dev = self.evs[d]
                if dev["kind"] == "op":
                    dev["marked"] = True
        inc = {e: 0 for e in ENGINES}
        for e in ENGINES:
            for evid in self.eng_ops[e]:
                ev = self.evs[evid]
                if ev["kind"] == "op" and ev["marked"]:
                    inc[e] += 1
                ev["incval"] = inc[e]
        esem = {e: es.enter_context(nc.semaphore(f"sem_{e}")) for e in ENGINES}
        dsem = [es.enter_context(nc.semaphore(f"dma{i}")) for i in range(self.n_dma)]
        self._esem, self._dsem = esem, dsem

        emr = self

        def body(eng_name):
            def run(engobj):
                wm = {}
                for evid in emr.eng_ops[eng_name]:
                    ev = emr.evs[evid]
                    waits = []
                    for d in ev["deps"]:
                        dev = emr.evs[d]
                        if dev["kind"] == "dma":
                            waits.append((("d", dev["q"]), 16 * dev["use"], dsem[dev["q"]]))
                        else:
                            pe = dev["eng"]
                            if pe == eng_name and eng_name == "tensor":
                                continue
                            waits.append((("e", pe), dev["incval"], esem[pe]))
                    if ev["kind"] == "dma" and ev["use"] > 1:
                        waits.append((("d", ev["q"]), 16 * (ev["use"] - 1), dsem[ev["q"]]))
                    wlog = []
                    for key, val, sem in sorted(waits, key=lambda w: (str(w[0]), w[1])):
                        if wm.get(key, 0) < val:
                            engobj.wait_ge(sem, val)
                            wm[key] = val
                            wlog.append((key, val))
                    emr.emitted.append((eng_name, evid, ev["tag"], wlog,
                                        ev["incval"] if ev.get("marked") else None))
                    inst = ev["fn"](engobj)
                    if ev["kind"] == "dma":
                        inst.then_inc(dsem[ev["q"]], 16)
                    elif ev["marked"]:
                        inst.then_inc(esem[eng_name], 1)
                if eng_name == "sync":
                    for i in range(emr.n_dma):
                        if emr.dma_use[i]:
                            engobj.wait_ge(dsem[i], 16 * emr.dma_use[i])
            return run

        with nc.Block() as block:
            block.sync(body("sync"))
            block.gpsimd(body("gpsimd"))
            block.vector(body("vector"))
            block.scalar(body("scalar"))
            block.tensor(body("tensor"))



def _chunked(w):
    n, rem = divmod(w.shape[0], 128)
    assert rem == 0
    return np.ascontiguousarray(
        w.reshape(n, 128, w.shape[1]).transpose(1, 0, 2).reshape(128, n * w.shape[1]))


def _pcols(v):
    n = v.shape[0] // 128
    return np.ascontiguousarray(v.reshape(n, 128).T)


def _row(v):
    return np.ascontiguousarray(v.reshape(1, -1))


def host_consts():
    c = {}
    c["onesf"] = np.ones((128, 128), np.float32)
    c["idf"] = np.eye(128, dtype=np.float32)
    s64 = np.zeros((64, 128), np.float32)
    for d in range(64):
        s64[d, 64 + d] = 1.0
    c["s64"] = s64
    eh = np.zeros((128, 16 * NCH), np.float32)
    for ch in range(NCH):
        for p in range(128):
            eh[p, 16 * ch + 2 * ch + p // 64] = 1.0
    c["ehead"] = eh
    eh2 = np.zeros((16, 128 * NCH), np.float32)
    for ch in range(NCH):
        for p in range(128):
            eh2[2 * ch + p // 64, 128 * ch + p] = 1.0
    c["ehead2"] = eh2
    s16 = np.zeros((16, 128 * 16), np.float32)
    for r in range(16):
        s16[r, 128 * r:128 * (r + 1)] = 1.0
    c["sel16"] = s16
    egn = np.zeros((128, 32 * NCH), np.float32)
    egnT = np.zeros((32, 128 * NCH), np.float32)
    for ch in range(NCH):
        for p in range(128):
            g = (128 * ch + p) // 32
            egn[p, 32 * ch + g] = 1.0
            egnT[g, 128 * ch + p] = 1.0
    c["egn"] = egn
    c["egnT"] = egnT
    tril = np.tril(np.ones((128, 128), np.float32)).T
    c["tril"] = np.ascontiguousarray(tril)
    trie = np.zeros((128, 128), np.float32)
    for p2 in range(128):
        for p in range(128):
            if (p2 % 8) == (p % 8) and (p2 // 8) < (p // 8):
                trie[p2, p] = 1.0
    c["trie"] = trie
    c["iota512"] = np.broadcast_to(np.arange(512, dtype=np.float32), (128, 512)).copy()
    ic = np.zeros((128, 4), np.float32)
    for sc in range(4):
        ic[:, sc] = 128 * sc + np.arange(128)
    c["iotacol"] = ic
    c["epsrow"] = np.full((128, 1), 1e-5, np.float32)
    h2 = np.zeros((128, 2), np.float32)
    h2[0:64, 0] = 1.0
    h2[64:128, 1] = 1.0
    c["h2"] = h2
    s8 = np.zeros((128, 8), np.float32)
    for f in range(128):
        s8[f, f % 8] = 1.0
    c["sel8"] = s8
    return c


def host_prep(inputs):
    f = np.float32
    d = {}
    d["wqkv"] = _chunked(inputs["L0_attn_W_qkv"].astype(f, copy=False))
    d["bqkv"] = _pcols(inputs["L0_attn_b_qkv"])
    d["gng"] = _pcols(inputs["L0_attn_gn_gamma"])
    d["gnb"] = _pcols(inputs["L0_attn_gn_beta"])
    d["wol"] = _chunked(inputs["L0_attn_W_o"])
    d["bo"] = _row(inputs["L0_attn_b_o"])
    su = inputs["L0_ffn_shared_up"]
    sd = inputs["L0_ffn_shared_down"]
    ru = inputs["L0_ffn_routed_up"]
    rdn = inputs["L0_ffn_routed_down"]
    d["sut"] = np.stack([_chunked(np.ascontiguousarray(su[i].T)) for i in range(2)])
    d["sdt"] = np.stack([_chunked(np.ascontiguousarray(sd[i].T)) for i in range(2)])
    d["rut"] = np.stack([_chunked(np.ascontiguousarray(ru[i].T)) for i in range(16)])
    d["rdt"] = np.stack([_chunked(np.ascontiguousarray(rdn[i].T)) for i in range(16)])
    d["rwt"] = _chunked(np.ascontiguousarray(inputs["L0_ffn_router_w"].T))
    d["wq1"] = _chunked(inputs["L1_attn_Wq"])
    d["wk1"] = _chunked(inputs["L1_attn_Wk"])
    d["wv1"] = _chunked(inputs["L1_attn_Wv"])
    d["wo1"] = _chunked(inputs["L1_attn_Wo"])
    d["bq1"] = _pcols(inputs["L1_attn_bq"])
    d["bk1"] = _pcols(inputs["L1_attn_bk"])
    d["bv1"] = _pcols(inputs["L1_attn_bv"])
    d["bo1"] = _row(inputs["L1_attn_bo"])
    d["gwl"] = _chunked(inputs["L1_ffn_gate_w"])
    w1 = inputs["L1_ffn_w1"]
    w2 = inputs["L1_ffn_w2"]
    d["w1l"] = np.stack([_chunked(w1[e]) for e in range(8)])
    d["w2l"] = np.stack([_chunked(w2[e]) for e in range(8)])
    d.update(host_consts())
    return d



class Ctx:
    pass


def ak(lo, hi):
    return [f"R:{g}" for g in range(lo // 512, (hi + 511) // 512)]


def xkk(lo, hi):
    return [f"xnT:{g}" for g in range(lo // 512, (hi + 511) // 512)]


def pak(lo, hi):
    return [f"pA:{b}" for b in range(lo // 512, (hi + 511) // 512)]


def pbk(lo, hi):
    return [f"pB:{b}" for b in range(lo // 512, (hi + 511) // 512)]


XT = lambda: [f"x:{t}" for t in range(NT)]
XNT_ALL = lambda: [f"xnT:{g}" for g in range(32)]

R_XNTOK = 0
R_QP = 0
R_KP = 2048
R_VP = 4096
R_OT = 16384
R_A0 = 0
R_A1 = 4096
R_GATEB = 8192
R_GATET = 10240
R_U0 = 12288
R_U1 = 16384
R_D0 = 20480
R_D1 = 24576
R_VBUF = 16384
R_SEL = 16384
R_Y = 24576
R_XIN = 28672


def build_program(taps=(), upto="full"):
    nc = bass.Bass("TRN2", target_bir_lowering=False)
    em = Em()
    c = Ctx()
    c.nc, c.em, c.taps = nc, em, {}

    dr = {}
    def din(name, shape, dt=F32):
        dr[name] = nc.dram_tensor(name, list(shape), dt, kind="ExternalInput").ap()
    din("x", (S, C))
    din("wqkv", (128, 8 * 3072)); din("bqkv", (128, 24))
    din("gng", (128, 8)); din("gnb", (128, 8))
    din("wol", (128, 8 * 1024)); din("bo", (1, 1024))
    din("sut", (2, 128, 4096)); din("sdt", (2, 128, 4096))
    din("rut", (16, 128, 4096)); din("rdt", (16, 128, 4096))
    din("rwt", (128, 128))
    din("wq1", (128, 8192)); din("wk1", (128, 8192)); din("wv1", (128, 8192))
    din("wo1", (128, 8192))
    din("bq1", (128, 8)); din("bk1", (128, 8)); din("bv1", (128, 8)); din("bo1", (1, 1024))
    din("gwl", (128, 64))
    din("w1l", (8, 128, 32768)); din("w2l", (8, 128, 32768))
    for n, sh in (("onesf", (128, 128)), ("idf", (128, 128)), ("s64", (64, 128)),
                  ("ehead", (128, 128)), ("ehead2", (16, 1024)), ("sel16", (16, 2048)),
                  ("egn", (128, 256)), ("egnT", (32, 1024)), ("tril", (128, 128)),
                  ("trie", (128, 128)), ("iota512", (128, 512)), ("iotacol", (128, 4)),
                  ("epsrow", (128, 1)), ("sel8", (128, 8)), ("h2", (128, 2))):
        din(n, sh)
    out = nc.dram_tensor("out", [S, C], F32, kind="ExternalOutput").ap()
    c.dr, c.out = dr, out

    es = ExitStack()
    sb = lambda n, sh, dt: es.enter_context(nc.sbuf_tensor(n, sh, dt))
    with es:
        c.x = sb("x_sb", [128, NT * 1024], F32)
        c.xnT = sb("xnT_sb", [128, 16384], BF16)
        c.R = sb("reg64", [128, 32768], BF16)
        c.ET = sb("ET", [128, 2048], BF16)
        c.ws = sb("wstream", [128, 6144], BF16)
        c.scr0 = sb("scr0", [128, 1024], F32)
        c.scr1 = sb("scr1", [128, 2048], F32)
        c.onesf = sb("onesf_sb", [128, 128], F32)
        c.onesb = sb("onesb_sb", [128, 128], BF16)
        c.idf = sb("idf_sb", [128, 128], F32)
        c.idb = sb("idb_sb", [128, 128], BF16)
        c.s64b = sb("s64b_sb", [64, 128], BF16)
        c.eheadb = sb("eheadb_sb", [128, 128], BF16)
        c.egnb = sb("egnb_sb", [128, 256], BF16)
        c.egnTb = sb("egnTb_sb", [32, 1024], BF16)
        c.trilb = sb("trilb_sb", [128, 128], BF16)
        c.trieb = sb("trieb_sb", [128, 128], BF16)
        c.sel8b = sb("sel8b_sb", [128, 8], BF16)
        c.iota512 = sb("iota512_sb", [128, 512], F32)
        c.iotacol = sb("iotacol_sb", [128, 4], F32)
        c.lnS = sb("lnS", [128, 16], F32)
        c.lnQ = sb("lnQ", [128, 16], F32)
        c.lnM = sb("lnM", [128, 16], F32)
        c.lnV = sb("lnV", [128, 16], F32)
        c.lnR = sb("lnR", [128, 16], F32)
        c.gnSQ = sb("gnSQ", [128, 16], BF16)
        c.gnSQf = sb("gnSQf", [128, 16], F32)
        c.gst = sb("gst", [32, 4], F32)
        c.gstb = sb("gstb", [32, 4], BF16)
        c.gnE = sb("gnE", [128, 32], F32)
        c.gnA = sb("gnA", [128, 8], F32)
        c.gnAb = sb("gnAb", [128, 8], BF16)
        c.gnB = sb("gnB", [128, 8], F32)
        c.gnBb = sb("gnBb", [128, 8], BF16)
        c.bcall = sb("bcall", [128, 24], F32)
        c.mcol2 = sb("mcol2", [128, 4], F32)
        c.trow = sb("trow", [1, 512], F32)
        c.ehead2b = sb("ehead2b_sb", [16, 1024], BF16)
        c.tmpc = sb("tmpc", [128, 4], F32)
        c.gngs = sb("gng_s", [128, 8], F32)
        c.gnbs = sb("gnb_s", [128, 8], F32)
        c.bqkvc = sb("bqkv_s", [128, 24], F32)
        c.bq1s = sb("bq1_s", [128, 8], F32)
        c.bk1s = sb("bk1_s", [128, 8], F32)
        c.bv1s = sb("bv1_s", [128, 8], F32)
        c.rwts = sb("rwt_s", [128, 128], BF16)
        c.gwls = sb("gwl_s", [128, 64], BF16)
        c.mcol = sb("mcol", [128, 16], F32)
        c.mbias = sb("mbias", [128, 1], F32)
        c.mb2 = sb("mb2", [128, 2], F32)
        c.h2b = sb("h2_s", [128, 2], BF16)
        c.botile = sb("botile", [128, 1024], BF16)
        c.probs = sb("probs128", [128, 128], F32)
        c.probsb = sb("probsb", [128, 128], BF16)
        c.mask = sb("mask128", [128, 128], BF16)
        c.maskf = sb("maskf", [128, 128], F32)
        c.cum = sb("cum128", [128, 128], F32)
        c.offsb = sb("offsb", [128, 128], F32)
        c.totrow = sb("totrow", [1, 128], F32)
        c.bis = sb("bis", [8, 8], F32)
        c.cnt1b = sb("cnt1b", [128, 1], BF16)
        c.totb = sb("totb", [128, 1], BF16)
        c.slots = sb("slots", [128, 16], F32)
        c.neg9 = sb("neg9", [128, 1], F32)
        c.pcols = sb("pcols_sb", [128, 4], F32)
        c.epsc = sb("epsc", [128, 1], F32)
        c.pA = es.enter_context(nc.psum_tensor("pA", [128, 2048], F32))
        c.pB = es.enter_context(nc.psum_tensor("pB", [128, 2048], F32))

        emit_all(c, upto, taps)
        em.build(nc, es)
    return nc


SCR1 = ["scr1:0", "scr1:1"]
WSK = lambda lo, hi: [f"ws:{g}" for g in range(lo // 512, (hi + 511) // 512)]


def emit_all(c, upto, taps):
    em, dr, nc = c.em, c.dr, c.nc
    R, pA, pB = c.R, c.pA, c.pB

    def top(fn, r=(), w=(), tag=""):
        em.op("tensor", fn, r, w, tag)

    def vop(fn, r=(), w=(), tag=""):
        em.op("vector", fn, r, w, tag)

    def sop(fn, r=(), w=(), tag=""):
        em.op("scalar", fn, r, w, tag)

    def gop(fn, r=(), w=(), tag=""):
        em.op("gpsimd", fn, r, w, tag)

    def sdma(fn, r=(), w=(), tag=""):
        em.dma("sync", fn, r, w, tag)

    def gdma(fn, r=(), w=(), tag=""):
        em.dma("gpsimd", fn, r, w, tag)

    tap_stage = {}

    def tap(name, ap, rkeys, shape, dt):
        t = nc.dram_tensor("tap_" + name, list(shape), dt, kind="ExternalOutput").ap()
        sdma(lambda e, t=t, ap=ap: e.dma_start(out=t[:, :], in_=ap), r=rkeys, w=(), tag="tap")

    sdma(lambda e: e.dma_start(
        out=c.x[:, :].rearrange("p (n f) -> p n f", n=NT),
        in_=dr["x"].rearrange("(n p) f -> p n f", p=128)), r=(), w=XT(), tag="ldx")
    for name, dst in (("onesf", c.onesf), ("idf", c.idf), ("iota512", c.iota512),
                      ("iotacol", c.iotacol), ("epsrow", c.epsc), ("gng", c.gngs),
                      ("gnb", c.gnbs), ("bqkv", c.bqkvc), ("bq1", c.bq1s),
                      ("bk1", c.bk1s), ("bv1", c.bv1s)):
        sdma(lambda e, d=dr[name], t=dst: e.dma_start(out=t[:, :], in_=d[:, :]),
             r=(), w=[name + "_sb"], tag="ldc")
    for name, dst in (("onesf", c.onesb), ("idf", c.idb), ("s64", c.s64b),
                      ("ehead", c.eheadb), ("ehead2", c.ehead2b),
                      ("egn", c.egnb), ("egnT", c.egnTb), ("tril", c.trilb),
                      ("trie", c.trieb), ("sel8", c.sel8b), ("h2", c.h2b), ("rwt", c.rwts),
                      ("gwl", c.gwls)):
        gdma(lambda e, d=dr[name], t=dst: e.dma_start(out=t[:, :], in_=d[:, :]),
             r=(), w=[name + "_b"], tag="ldcb")
    vop(lambda e: e.memset(c.neg9[:, :], -1.0e9), r=(), w=["neg9"])

    def mm(out, lhsT, rhs, st, sp, r, w, tag=""):
        top(lambda e, o=out, l=lhsT, rr=rhs, st=st, sp=sp: e.matmul(o, l, rr, start=st, stop=sp), r=r, w=w, tag=tag)

    def tr(out, in_, ident, r, w):
        idap = ident if isinstance(ident, bass.AP) else ident[:, :]
        top(lambda e, o=out, i=in_, d=idap: e.transpose(o, i, d), r=r, w=w)

    def ln(layer):
        addxn = (layer == 0)
        for t in range(NT):
            xt = c.x[:, 1024 * t:1024 * (t + 1)]
            vop(lambda e, xt=xt, t=t: e.reduce_sum(out=c.lnS[:, t:t + 1], in_=xt, axis=AX.X),
                r=[f"x:{t}"], w=["lnS"])
            sop(lambda e, xt=xt, t=t: e.activation(out=c.scr0[:, :], in_=xt, func=AF.Square,
                                                   accum_out=c.lnQ[:, t:t + 1]),
                r=[f"x:{t}"], w=["scr0", "lnQ"])
        vop(lambda e: e.tensor_scalar(out=c.lnM[:, :], in0=c.lnS[:, :], scalar1=1.0 / C,
                                      scalar2=None, op0=ALU.mult), r=["lnS"], w=["lnM"])
        vop(lambda e: e.tensor_mul(out=c.lnV[:, :], in0=c.lnM[:, :], in1=c.lnM[:, :]),
            r=["lnM"], w=["lnV"])
        vop(lambda e: e.scalar_tensor_tensor(out=c.lnV[:, :], in0=c.lnQ[:, :], scalar=1.0 / C,
                                             in1=c.lnV[:, :], op0=ALU.mult, op1=ALU.subtract),
            r=["lnQ", "lnV"], w=["lnV"])
        vop(lambda e: e.tensor_scalar(out=c.lnV[:, :], in0=c.lnV[:, :], scalar1=1e-5,
                                      scalar2=None, op0=ALU.add), r=["lnV"], w=["lnV"])
        sop(lambda e: e.activation(out=c.lnR[:, :], in_=c.lnV[:, :], func=AF.Sqrt),
            r=["lnV"], w=["lnR"])
        vop(lambda e: e.reciprocal(out=c.lnR[:, :], in_=c.lnR[:, :]), r=["lnR"], w=["lnR"])
        for t in range(NT):
            xt = c.x[:, 1024 * t:1024 * (t + 1)]
            dst = R[:, R_XNTOK + 1024 * t:R_XNTOK + 1024 * (t + 1)]
            dkeys = ak(R_XNTOK + 1024 * t, R_XNTOK + 1024 * (t + 1))
            if addxn:
                h = (t % 2) * 1024
                scr = c.scr1[:, h:h + 1024]
                skey = [SCR1[t % 2]]
                vop(lambda e, xt=xt, t=t, scr=scr: e.tensor_scalar(
                    out=scr, in0=xt, scalar1=c.lnM[:, t:t + 1], scalar2=c.lnR[:, t:t + 1],
                    op0=ALU.subtract, op1=ALU.mult), r=[f"x:{t}", "lnM", "lnR"], w=skey)
                vop(lambda e, xt=xt, scr=scr: e.tensor_add(out=xt, in0=xt, in1=scr),
                    r=skey + [f"x:{t}"], w=[f"x:{t}"])
                vop(lambda e, dst=dst, scr=scr: e.tensor_copy(out=dst, in_=scr),
                    r=skey, w=dkeys)
            else:
                vop(lambda e, xt=xt, t=t, dst=dst: e.tensor_scalar(
                    out=dst, in0=xt, scalar1=c.lnM[:, t:t + 1], scalar2=c.lnR[:, t:t + 1],
                    op0=ALU.subtract, op1=ALU.mult), r=[f"x:{t}", "lnM", "lnR"], w=dkeys)

    def build_xnT():
        for cc in range(NCH):
            for tg in range(2):
                bank = (cc * 2 + tg) % 4
                pview = pA[:, 512 * bank:512 * (bank + 1)].bitcast(BF16)
                bkey = [f"pA:{bank}"]
                for j in range(8):
                    t = tg * 8 + j
                    tr(pview[:, 128 * j:128 * (j + 1)],
                       R[:, R_XNTOK + 1024 * t + 128 * cc:R_XNTOK + 1024 * t + 128 * (cc + 1)],
                       c.idb,
                       r=ak(R_XNTOK + 1024 * t + 128 * cc, R_XNTOK + 1024 * t + 128 * (cc + 1)) + ["idf_b"],
                       w=bkey)
                dlo = 2048 * cc + 1024 * tg
                vop(lambda e, pv=pview, cc=cc, tg=tg: e.tensor_copy(
                    out=c.xnT[:, 2048 * cc + 1024 * tg: 2048 * cc + 1024 * tg + 1024], in_=pv),
                    r=bkey, w=xkk(dlo, dlo + 1024))

    def gn():
        for cc in range(NCH):
            ch = c.xnT[:, 2048 * cc:2048 * (cc + 1)]
            rk = xkk(2048 * cc, 2048 * (cc + 1))
            vop(lambda e, ch=ch, cc=cc: e.reduce_sum(out=c.gnSQf[:, 2 * cc:2 * cc + 1], in_=ch, axis=AX.X),
                r=rk, w=["gnSQf"])
            sop(lambda e, ch=ch, cc=cc: e.activation(out=c.scr1[:, :], in_=ch, func=AF.Square,
                                                     accum_out=c.gnSQf[:, 2 * cc + 1:2 * cc + 2]),
                r=rk, w=SCR1 + ["gnSQf"])
        vop(lambda e: e.tensor_copy(out=c.gnSQ[:, :], in_=c.gnSQf[:, :]), r=["gnSQf"], w=["gnSQ"])
        for cc in range(NCH):
            mm(pB[0:32, 0:2], c.egnb[:, 32 * cc:32 * (cc + 1)], c.gnSQ[:, 2 * cc:2 * cc + 2],
               cc == 0, cc == 7, r=["egn_b", "gnSQ"], w=["pB:0"])
        vop(lambda e: e.tensor_scalar(out=c.gst[:, 0:2], in0=pB[0:32, 0:2], scalar1=1.0 / 65536.0,
                                      scalar2=None, op0=ALU.mult), r=["pB:0"], w=["gst"])
        vop(lambda e: e.tensor_mul(out=c.gst[:, 2:3], in0=c.gst[:, 0:1], in1=c.gst[:, 0:1]),
            r=["gst"], w=["gst"])
        vop(lambda e: e.tensor_sub(out=c.gst[:, 3:4], in0=c.gst[:, 1:2], in1=c.gst[:, 2:3]),
            r=["gst"], w=["gst"])
        vop(lambda e: e.tensor_scalar(out=c.gst[:, 3:4], in0=c.gst[:, 3:4], scalar1=1e-6,
                                      scalar2=None, op0=ALU.add), r=["gst"], w=["gst"])
        sop(lambda e: e.activation(out=c.gst[:, 2:3], in_=c.gst[:, 3:4], func=AF.Sqrt),
            r=["gst"], w=["gst"])
        vop(lambda e: e.reciprocal(out=c.gst[:, 3:4], in_=c.gst[:, 2:3]), r=["gst"], w=["gst"])
        vop(lambda e: e.tensor_copy(out=c.gstb[:, 0:1], in_=c.gst[:, 3:4]), r=["gst"], w=["gstb"])
        vop(lambda e: e.tensor_copy(out=c.gstb[:, 1:2], in_=c.gst[:, 0:1]), r=["gst"], w=["gstb"])
        vop(lambda e: e.tensor_sub(out=c.gst[:, 1:2], in0=c.gst[:, 3:4], in1=c.gstb[:, 0:1]),
            r=["gst", "gstb"], w=["gst"])
        vop(lambda e: e.tensor_copy(out=c.gstb[:, 2:3], in_=c.gst[:, 1:2]), r=["gst"], w=["gstb"])
        vop(lambda e: e.tensor_sub(out=c.gst[:, 1:2], in0=c.gst[:, 0:1], in1=c.gstb[:, 1:2]),
            r=["gst", "gstb"], w=["gst"])
        vop(lambda e: e.tensor_copy(out=c.gstb[:, 3:4], in_=c.gst[:, 1:2]), r=["gst"], w=["gstb"])
        for cc in range(NCH):
            bank = 1 + (cc % 2)
            mm(pB[:, 512 * bank:512 * bank + 4], c.egnTb[:, 128 * cc:128 * (cc + 1)], c.gstb[:, 0:4],
               True, True, r=["egnT_b", "gstb"], w=[f"pB:{bank}"])
            vop(lambda e, cc=cc, bank=bank: e.tensor_copy(
                out=c.gnE[:, 4 * cc:4 * cc + 4], in_=pB[:, 512 * bank:512 * bank + 4]),
                r=[f"pB:{bank}"], w=["gnE"])
        for cc in range(NCH):
            vop(lambda e, cc=cc: e.tensor_add(out=c.gnA[:, cc:cc + 1], in0=c.gnE[:, 4 * cc:4 * cc + 1],
                                              in1=c.gnE[:, 4 * cc + 2:4 * cc + 3]), r=["gnE"], w=["gnA"])
            vop(lambda e, cc=cc: e.tensor_mul(out=c.gnA[:, cc:cc + 1], in0=c.gnA[:, cc:cc + 1],
                                              in1=c.gngs[:, cc:cc + 1]), r=["gnA", "gng_sb"], w=["gnA"])
            vop(lambda e, cc=cc: e.tensor_add(out=c.gnB[:, cc:cc + 1], in0=c.gnE[:, 4 * cc + 1:4 * cc + 2],
                                              in1=c.gnE[:, 4 * cc + 3:4 * cc + 4]), r=["gnE"], w=["gnB"])
            vop(lambda e, cc=cc: e.tensor_mul(out=c.gnB[:, cc:cc + 1], in0=c.gnB[:, cc:cc + 1],
                                              in1=c.gnA[:, cc:cc + 1]), r=["gnB", "gnA"], w=["gnB"])
            vop(lambda e, cc=cc: e.tensor_sub(out=c.gnB[:, cc:cc + 1], in0=c.gnbs[:, cc:cc + 1],
                                              in1=c.gnB[:, cc:cc + 1]), r=["gnb_sb", "gnB"], w=["gnB"])
        vop(lambda e: e.tensor_copy(out=c.gnAb[:, :], in_=c.gnA[:, :]), r=["gnA"], w=["gnAb"])
        vop(lambda e: e.tensor_copy(out=c.gnBb[:, :], in_=c.gnB[:, :]), r=["gnB"], w=["gnBb"])

    def l0_attn_bias():
        wq3 = dr["wqkv"].rearrange("p (n f) -> p n f", n=8)
        for os6 in range(6):
            gdma(lambda e, os6=os6: e.dma_start(
                out=c.ws[:, 0:4096].rearrange("p (n f) -> p n f", n=8),
                in_=wq3[:, :, 512 * os6:512 * (os6 + 1)]), r=(), w=WSK(0, 4096), tag="wb")
            for cc in range(NCH):
                mm(pA[0:1, 1536:2048], c.gnBb[:, cc:cc + 1], c.ws[:, 512 * cc:512 * (cc + 1)],
                   cc == 0, cc == 7, r=["gnBb"] + WSK(0, 4096), w=["pA:3"])
            vop(lambda e: e.tensor_copy(out=c.trow[0:1, 0:512], in_=pA[0:1, 1536:2048]),
                r=["pA:3"], w=["trow"])
            for q4 in range(4):
                och = os6 * 4 + q4
                tr(pB[:, 1536 + q4:1537 + q4], c.trow[0:1, 128 * q4:128 * (q4 + 1)],
                   c.idf[0:1, 0:1], r=["trow", "idf_sb"], w=["pB:3"])
            vop(lambda e, os6=os6: e.tensor_add(out=c.bcall[:, 4 * os6:4 * (os6 + 1)],
                                                in0=pB[:, 1536:1540],
                                                in1=c.bqkvc[:, 4 * os6:4 * (os6 + 1)]),
                r=["pB:3", "bqkv_sb"], w=["bcall"])
        for vch in range(8):
            tr(pA[0:1, 1536 + 128 * (vch % 4):1664 + 128 * (vch % 4)], c.bcall[:, 16 + vch:17 + vch],
               c.idf, r=["bcall", "idf_sb"], w=["pA:3"])
            vop(lambda e, vch=vch: e.tensor_copy(
                out=c.trow[0:1, 0:512][0:1, 128 * (vch % 4):128 * (vch % 4 + 1)],
                in_=pA[0:1, 1536 + 128 * (vch % 4):1664 + 128 * (vch % 4)]),
                r=["pA:3"], w=["trow"])
            if vch % 4 == 3:
                half = vch // 4
                for s2 in range(1):
                    mm(pB[:, 1024 + 512 * half:1024 + 512 * (half + 1)], c.onesf[0:1, 0:128],
                       c.trow[0:1, 0:512], True, True,
                       r=["onesf_sb", "trow"], w=pbk(1024 + 512 * half, 1024 + 512 * (half + 1)))
                vop(lambda e, half=half: e.tensor_copy(
                    out=c.botile[:, 512 * half:512 * (half + 1)],
                    in_=pB[:, 1024 + 512 * half:1024 + 512 * (half + 1)]),
                    r=pbk(1024 + 512 * half, 1024 + 512 * (half + 1)), w=["botile"])

    def l0_attn():
        l0_attn_bias()
        wq3 = dr["wqkv"].rearrange("p (n f) -> p n f", n=8)

        def emit_qk(p):
            qb, kb = 1024 * (p % 2), 2048 + 1024 * (p % 2)
            for base, osrc in ((qb, 128 * p), (kb, 1024 + 128 * p)):
                gdma(lambda e, base=base, osrc=osrc: e.dma_start(
                    out=c.ws[:, base:base + 1024].rearrange("p (n f) -> p n f", n=8),
                    in_=wq3[:, :, osrc:osrc + 128]),
                    r=(), w=WSK(base, base + 1024), tag="wqkv")
            for base in (qb, kb):
                w3 = c.ws[:, base:base + 1024].rearrange("p (n f) -> p n f", n=8)
                ab = c.gnAb[:, :].rearrange("p (n o) -> p n o", o=1).to_broadcast([128, 8, 128])
                vop(lambda e, w3=w3, ab=ab: e.tensor_mul(out=w3, in0=w3, in1=ab),
                    r=WSK(base, base + 1024) + ["gnAb"], w=WSK(base, base + 1024))
            for dstbase, wsb, bci in ((R_QP, qb, p), (R_KP, kb, 8 + p)):
                for s4 in range(4):
                    bank = s4 % 2
                    for cc in range(NCH):
                        mm(pA[:, 512 * bank:512 * (bank + 1)],
                           c.ws[:, wsb + 128 * cc:wsb + 128 * (cc + 1)],
                           c.xnT[:, 2048 * cc + 512 * s4:2048 * cc + 512 * (s4 + 1)],
                           cc == 0, cc == 7,
                           r=WSK(wsb, wsb + 1024) + xkk(2048 * cc + 512 * s4, 2048 * cc + 512 * (s4 + 1)),
                           w=[f"pA:{bank}"])
                    sop(lambda e, dstbase=dstbase, s4=s4, bank=bank, bci=bci: e.activation(
                        out=R[:, dstbase + 512 * s4:dstbase + 512 * (s4 + 1)],
                        in_=pA[:, 512 * bank:512 * (bank + 1)], func=AF.Identity,
                        bias=c.bcall[:, bci:bci + 1], scale=1.0),
                        r=[f"pA:{bank}", "bcall"], w=ak(dstbase + 512 * s4, dstbase + 512 * (s4 + 1)))

        def emit_vdma(p):
            vb = 4096 + 1024 * (p % 2)
            gdma(lambda e, p=p, vb=vb: e.dma_start(
                out=c.ws[:, vb:vb + 1024].rearrange("p (n f) -> p n f", n=8),
                in_=wq3[:, :, 2048 + 128 * p:2048 + 128 * (p + 1)]),
                r=(), w=WSK(vb, vb + 1024), tag="wv")
            w3 = c.ws[:, vb:vb + 1024].rearrange("p (n f) -> p n f", n=8)
            ab = c.gnAb[:, :].rearrange("p (n o) -> p n o", o=1).to_broadcast([128, 8, 128])
            vop(lambda e, w3=w3, ab=ab: e.tensor_mul(out=w3, in0=w3, in1=ab),
                r=WSK(vb, vb + 1024) + ["gnAb"], w=WSK(vb, vb + 1024))
            gop(lambda e: e.memset(R[:, R_VP:R_VP + 2080], 1.0), r=(), w=ak(R_VP, R_VP + 2080))

        def emit_v_tile(p, tt):
            vb = 4096 + 1024 * (p % 2)
            for cc in range(NCH):
                mm(pA[:, 1536:1664],
                   c.xnT[:, 2048 * cc + 128 * tt:2048 * cc + 128 * (tt + 1)],
                   c.ws[:, vb + 128 * cc:vb + 128 * (cc + 1)],
                   cc == 0, cc == 7,
                   r=WSK(vb, vb + 1024) + xkk(2048 * cc + 128 * tt, 2048 * cc + 128 * (tt + 1)),
                   w=["pA:3"])
            dst = R[:, R_VP + 130 * tt:R_VP + 130 * tt + 130].rearrange(
                "p (a b) -> p a b", a=2)[:, :, 0:64]
            src = pA[:, 1536:1664].rearrange("p (a b) -> p a b", a=2)
            vbt3 = c.botile[:, 128 * p:128 * (p + 1)].rearrange("p (a b) -> p a b", a=2)
            vop(lambda e, dst=dst, src=src, vbt3=vbt3: e.tensor_add(out=dst, in0=src, in1=vbt3),
                r=["pA:3", "botile"], w=ak(R_VP + 130 * tt, R_VP + 130 * tt + 130))

        def emit_mhat(p):
            for src, col in ((R_QP, 0), (R_KP, 1)):
                for half in range(2):
                    sop(lambda e, src=src, half=half: e.activation(
                        out=c.ET[:, 1024 * half:1024 * (half + 1)],
                        in_=R[:, src + 1024 * half:src + 1024 * (half + 1)], func=AF.Square),
                        r=ak(src + 1024 * half, src + 1024 * (half + 1)), w=[f"ET:{half}"], tag="mhat_sq")
                for s4 in range(4):
                    mm(pB[0:2, 512 * s4:512 * s4 + 512], c.h2b[:, :],
                       c.ET[:, 512 * s4:512 * (s4 + 1)], True, True,
                       r=["h2_b", "ET:0", "ET:1"], w=[f"pB:{s4}"], tag="mhat_mm")
                vop(lambda e, col=col: e.reduce_max(out=c.tmpc[0:2, col:col + 1], in_=pB[0:2, 0:2048],
                                                    axis=AX.X), r=pbk(0, 2048), w=["tmpc"], tag="mhat_red")
            vop(lambda e: e.tensor_mul(out=c.tmpc[0:2, 2:3], in0=c.tmpc[0:2, 0:1],
                                       in1=c.tmpc[0:2, 1:2]), r=["tmpc"], w=["tmpc"])
            sop(lambda e: e.activation(out=c.tmpc[0:2, 3:4], in_=c.tmpc[0:2, 2:3], func=AF.Sqrt),
                r=["tmpc"], w=["tmpc"])
            tr(pB[0:1, 1536:1538], c.tmpc[0:2, 3:4], c.idf[0:2, 0:2], r=["tmpc", "idf_sb"], w=["pB:3"])
            vop(lambda e: e.tensor_copy(out=c.trow[0:1, 424:426], in_=pB[0:1, 1536:1538]),
                r=["pB:3"], w=["trow"])
            for j in range(2):
                mm(pA[:, 1664 + j:1665 + j], c.onesf[0:1, 0:128], c.trow[0:1, 424 + j:425 + j],
                   True, True, r=["onesf_sb", "trow"], w=["pA:3"])
            vop(lambda e: e.tensor_scalar(out=c.mb2[:, 0:2], in0=pA[:, 1664:1666],
                                          scalar1=-0.125, scalar2=None, op0=ALU.mult),
                r=["pA:3"], w=["mb2"])

        def p2_okeys(qh):
            return pak(0, 1024) if qh == 0 else pak(1024, 2048)

        def p2_oview(qh):
            return pA[0:65, 0:1024] if qh == 0 else pA[0:65, 1024:2048]

        def emit_p2_st(p, j, qh, kt):
            r0 = 64 * j
            hb = kt % 2
            sv = pB[:, 1024 * hb:1024 * (hb + 1)]
            for s in range(2):
                mm(sv[:, 512 * s:512 * (s + 1)],
                   R[r0:r0 + 64, R_KP + 128 * kt:R_KP + 128 * (kt + 1)],
                   R[r0:r0 + 64, R_QP + 1024 * qh + 512 * s:R_QP + 1024 * qh + 512 * (s + 1)],
                   True, True,
                   r=ak(R_KP + 128 * kt, R_KP + 128 * (kt + 1)) + ak(R_QP + 1024 * qh + 512 * s, R_QP + 1024 * qh + 512 * (s + 1)),
                   w=pbk(1024 * hb + 512 * s, 1024 * hb + 512 * (s + 1)))
            sop(lambda e, sv=sv, hb=hb, j=j: e.activation(
                out=c.ET[:, 1024 * hb:1024 * (hb + 1)], in_=sv, func=AF.Exp,
                bias=c.mb2[:, j:j + 1], scale=0.125),
                r=pbk(1024 * hb, 1024 * (hb + 1)) + ["mb2"], w=[f"ET:{hb}"])

        def emit_p2_pv(p, j, qh, kt):
            hb = kt % 2
            ov = p2_oview(qh)
            for s in range(2):
                mm(ov[:, 512 * s:512 * (s + 1)],
                   R[:, R_VP + 130 * kt + 65 * j:R_VP + 130 * kt + 65 * (j + 1)],
                   c.ET[:, 1024 * hb + 512 * s:1024 * hb + 512 * (s + 1)],
                   kt == 0, kt == 15,
                   r=[f"ET:{hb}"] + ak(R_VP + 130 * kt, R_VP + 130 * (kt + 1)),
                   w=p2_okeys(qh))

        def emit_p2_norm(p, j, qh):
            okeys = p2_okeys(qh)
            ov = p2_oview(qh)
            vop(lambda e, ov=ov: e.tensor_copy(out=c.scr1[64:65, 0:1024], in_=ov[64:65, 0:1024]),
                r=okeys, w=[SCR1[0]])
            vop(lambda e: e.reciprocal(out=c.scr1[64:65, 1024:2048], in_=c.scr1[64:65, 0:1024]),
                r=[SCR1[0]], w=[SCR1[1]])
            for s in range(2):
                mm(pB[0:64, 0:512], c.onesf[64:65, 0:64],
                   c.scr1[64:65, 1024 + 512 * s:1024 + 512 * (s + 1)], True, True,
                   r=["onesf_sb", SCR1[1]], w=["pB:0"])
                vop(lambda e, s=s: e.tensor_copy(out=c.scr1[0:64, 512 * s:512 * (s + 1)],
                                                 in_=pB[0:64, 0:512]),
                    r=["pB:0"], w=[SCR1[0]])
            dst0 = R_OT + 2048 * p + 1024 * qh
            if j == 0:
                vop(lambda e, ov=ov, dst0=dst0: e.tensor_mul(
                    out=R[0:64, dst0:dst0 + 1024], in0=ov[0:64, 0:1024],
                    in1=c.scr1[0:64, 0:1024]),
                    r=okeys + [SCR1[0]], w=ak(dst0, dst0 + 1024))
            else:
                vop(lambda e, ov=ov: e.tensor_mul(
                    out=c.ET[0:64, 0:1024], in0=ov[0:64, 0:1024],
                    in1=c.scr1[0:64, 0:1024]),
                    r=okeys + [SCR1[0]], w=["ET:0"])
                for s in range(2):
                    mm(pB[:, 0:512], c.s64b[:, :], c.ET[0:64, 512 * s:512 * (s + 1)],
                       True, True, r=["s64_b", "ET:0"], w=["pB:0"])
                    vop(lambda e, s=s, dst0=dst0: e.tensor_copy(
                        out=R[64:128, dst0 + 512 * s:dst0 + 512 * (s + 1)],
                        in_=pB[64:128, 0:512]),
                        r=["pB:0"], w=ak(dst0 + 512 * s, dst0 + 512 * (s + 1)))

        for p in range(int(os.environ.get("K_NPAIRS", "8"))):
            emit_qk(p)
            emit_vdma(p)
            emit_mhat(p)
            for u in range(16):
                emit_v_tile(p, u)
            for j in range(2):
                for qh in range(2):
                    for kt in range(NT):
                        emit_p2_st(p, j, qh, kt)
                        emit_p2_pv(p, j, qh, kt)
                    emit_p2_norm(p, j, qh)

        sdma(lambda e: e.dma_start(out=c.scr1[0:1, 0:1024], in_=dr["bo"][:, :]), r=(), w=[SCR1[0]])
        for s in range(2):
            mm(pB[:, 512 * s:512 * (s + 1)], c.onesf[0:1, 0:128],
               c.scr1[0:1, 512 * s:512 * (s + 1)], True, True,
               r=["onesf_sb", SCR1[0]], w=pbk(512 * s, 512 * (s + 1)))
        vop(lambda e: e.tensor_copy(out=c.botile[:, :], in_=pB[:, 0:1024]), r=pbk(0, 1024), w=["botile"])
        wo3 = dr["wol"].rearrange("p (n f) -> p n f", n=8)
        for os_ in range(2):
            gdma(lambda e, os_=os_: e.dma_start(
                out=c.ws[:, 0:4096].rearrange("p (n f) -> p n f", n=8),
                in_=wo3[:, :, 512 * os_:512 * (os_ + 1)]),
                r=(), w=WSK(0, 4096), tag="wo")
            for tt in range(NT):
                bank = tt % 4
                for pch in range(8):
                    mm(pA[:, 512 * bank:512 * (bank + 1)],
                       R[:, R_OT + 2048 * pch + 128 * tt:R_OT + 2048 * pch + 128 * (tt + 1)],
                       c.ws[:, 512 * pch:512 * pch + 512],
                       pch == 0, pch == 7,
                       r=ak(R_OT + 2048 * pch + 128 * tt, R_OT + 2048 * pch + 128 * (tt + 1)) + WSK(512 * pch, 512 * pch + 512),
                       w=[f"pA:{bank}"])
                xs = c.x[:, 1024 * tt + 512 * os_:1024 * tt + 512 * (os_ + 1)]
                vop(lambda e, xs=xs, bank=bank: e.tensor_add(out=xs, in0=xs,
                                                             in1=pA[:, 512 * bank:512 * (bank + 1)]),
                    r=[f"pA:{bank}", f"x:{tt}"], w=[f"x:{tt}"])
                vop(lambda e, xs=xs, os_=os_: e.tensor_add(out=xs, in0=xs,
                                                           in1=c.botile[:, 512 * os_:512 * (os_ + 1)]),
                    r=["botile", f"x:{tt}"], w=[f"x:{tt}"])

    def l0_ffn():
        gdma(lambda e: e.dma_start(out=R[0:16, 28672:30720], in_=dr["sel16"][:, :]),
             r=(), w=ak(28672, 30720), tag="sel16")
        for tt in range(NT):
            bank = tt % 2
            pv = pA[:, 512 * bank:512 * bank + 16]
            for cc in range(NCH):
                mm(pv, c.xnT[:, 2048 * cc + 128 * tt:2048 * cc + 128 * (tt + 1)],
                   c.rwts[:, 16 * cc:16 * (cc + 1)], cc == 0, cc == 7,
                   r=["rwt_b"] + xkk(2048 * cc + 128 * tt, 2048 * cc + 128 * (tt + 1)),
                   w=[f"pA:{bank}"])
            vop(lambda e, pv=pv: e.reduce_max(out=c.tmpc[:, 2:3], in_=pv, axis=AX.X),
                r=[f"pA:{bank}"], w=["tmpc"])
            vop(lambda e: e.tensor_scalar(out=c.tmpc[:, 2:3], in0=c.tmpc[:, 2:3], scalar1=-1.0,
                                          scalar2=None, op0=ALU.mult), r=["tmpc"], w=["tmpc"])
            sop(lambda e, pv=pv: e.activation(out=c.scr0[:, 0:16], in_=pv, func=AF.Exp,
                                              bias=c.tmpc[:, 2:3], scale=1.0),
                r=[f"pA:{bank}", "tmpc"], w=["scr0"])
            vop(lambda e: e.reduce_sum(out=c.tmpc[:, 3:4], in_=c.scr0[:, 0:16], axis=AX.X),
                r=["scr0"], w=["tmpc"])
            vop(lambda e: e.reciprocal(out=c.tmpc[:, 3:4], in_=c.tmpc[:, 3:4]), r=["tmpc"], w=["tmpc"])
            vop(lambda e: e.max(out=c.scr0[:, 16:24], in_=c.scr0[:, 0:16]), r=["scr0"], w=["scr0"])
            vop(lambda e: e.tensor_scalar(out=c.scr0[:, 24:40], in0=c.scr0[:, 0:16],
                                          scalar1=c.scr0[:, 19:20], scalar2=0.0,
                                          op0=ALU.subtract, op1=ALU.is_ge), r=["scr0"], w=["scr0"])
            vop(lambda e: e.scalar_tensor_tensor(out=c.scr0[:, 40:56], in0=c.scr0[:, 24:40],
                                                 scalar=c.tmpc[:, 3:4], in1=c.scr0[:, 0:16],
                                                 op0=ALU.mult, op1=ALU.mult),
                r=["scr0", "tmpc"], w=["scr0"])
            tr(pB[0:16, 1024 + 128 * (tt % 2):1024 + 128 * (tt % 2) + 128], c.scr0[:, 40:56], c.idf,
               r=["scr0", "idf_sb"], w=["pB:2"])
            vop(lambda e, tt=tt: e.tensor_copy(
                out=R[0:16, R_GATET + 128 * tt:R_GATET + 128 * (tt + 1)],
                in_=pB[0:16, 1024 + 128 * (tt % 2):1024 + 128 * (tt % 2) + 128]),
                r=["pB:2"], w=ak(R_GATET + 128 * tt, R_GATET + 128 * (tt + 1)))

        EXP = [("sut", "sdt", 0, False), ("sut", "sdt", 1, False)] + \
              [("rut", "rdt", r, True) for r in range(16)]
        for g in range(9):
            for gi in range(2):
                usrc, dsrc, idx, routed = EXP[2 * g + gi]
                ub = R_U0 if gi == 0 else R_U1
                db = R_D0 if gi == 0 else R_D1
                gdma(lambda e, usrc=usrc, idx=idx, ub=ub: e.dma_start(
                    out=R[:, ub:ub + 4096], in_=dr[usrc][idx]), r=(), w=ak(ub, ub + 4096), tag="U")
                gdma(lambda e, dsrc=dsrc, idx=idx, db=db: e.dma_start(
                    out=R[:, db:db + 4096], in_=dr[dsrc][idx]), r=(), w=ak(db, db + 4096), tag="D")
            for th in range(2):
                for gi in range(2):
                    usrc, dsrc, idx, routed = EXP[2 * g + gi]
                    ub = R_U0 if gi == 0 else R_U1
                    ab = R_A0 if gi == 0 else R_A1
                    for ec in range(4):
                        for s in range(2):
                            bank = (ec * 2 + s) % 2
                            for cc in range(NCH):
                                mm(pA[:, 512 * bank:512 * (bank + 1)],
                                   R[:, ub + 512 * cc + 128 * ec:ub + 512 * cc + 128 * (ec + 1)],
                                   c.xnT[:, 2048 * cc + 1024 * th + 512 * s:2048 * cc + 1024 * th + 512 * (s + 1)],
                                   cc == 0, cc == 7,
                                   r=ak(ub, ub + 4096) + xkk(2048 * cc + 1024 * th + 512 * s, 2048 * cc + 1024 * th + 512 * (s + 1)),
                                   w=[f"pA:{bank}"])
                            sop(lambda e, ab=ab, ec=ec, s=s, bank=bank: e.activation(
                                out=R[:, ab + 1024 * ec + 512 * s:ab + 1024 * ec + 512 * (s + 1)],
                                in_=pA[:, 512 * bank:512 * (bank + 1)], func=AF.Gelu),
                                r=[f"pA:{bank}"], w=ak(ab + 1024 * ec + 512 * s, ab + 1024 * ec + 512 * (s + 1)))
                    if routed:
                        for s in range(2):
                            mm(pB[:, 512 * s:512 * (s + 1)], R[0:16, 28672 + 128 * idx:28672 + 128 * (idx + 1)],
                               R[0:16, R_GATET + 1024 * th + 512 * s:R_GATET + 1024 * th + 512 * (s + 1)],
                               True, True,
                               r=ak(28672, 30720) + ak(R_GATET + 1024 * th + 512 * s, R_GATET + 1024 * th + 512 * (s + 1)),
                               w=[f"pB:{s}"])
                        for ec in range(4):
                            a3 = R[:, ab + 1024 * ec:ab + 1024 * (ec + 1)]
                            vop(lambda e, a3=a3: e.tensor_mul(out=a3, in0=a3, in1=pB[:, 0:1024]),
                                r=["pB:0", "pB:1"] + ak(ab + 1024 * ec, ab + 1024 * (ec + 1)),
                                w=ak(ab + 1024 * ec, ab + 1024 * (ec + 1)))
                for tt8 in range(8):
                    for os_ in range(2):
                        bank = 2 + (tt8 * 2 + os_) % 2
                        nmm = 0
                        for gi in range(2):
                            ab = R_A0 if gi == 0 else R_A1
                            db = R_D0 if gi == 0 else R_D1
                            for ec in range(4):
                                mm(pB[:, 512 * bank:512 * (bank + 1)],
                                   R[:, ab + 1024 * ec + 128 * tt8:ab + 1024 * ec + 128 * (tt8 + 1)],
                                   R[:, db + 1024 * ec + 512 * os_:db + 1024 * ec + 512 * (os_ + 1)],
                                   nmm == 0, nmm == 7,
                                   r=ak(ab + 1024 * ec + 128 * tt8, ab + 1024 * ec + 128 * (tt8 + 1)) + ak(db, db + 4096),
                                   w=[f"pB:{bank}"])
                                nmm += 1
                        tglob = th * 8 + tt8
                        xs = c.x[:, 1024 * tglob + 512 * os_:1024 * tglob + 512 * (os_ + 1)]
                        vop(lambda e, xs=xs, bank=bank: e.tensor_add(
                            out=xs, in0=xs, in1=pB[:, 512 * bank:512 * (bank + 1)]),
                            r=[f"pB:{bank}", f"x:{tglob}"], w=[f"x:{tglob}"])

    def l1_attn():
        sdma(lambda e: e.dma_start(out=c.scr1[0:1, 0:1024], in_=dr["bo1"][:, :]), r=(), w=[SCR1[0]])
        for s in range(2):
            mm(pB[:, 512 * s:512 * (s + 1)], c.onesf[0:1, 0:128],
               c.scr1[0:1, 512 * s:512 * (s + 1)], True, True,
               r=["onesf_sb", SCR1[0]], w=pbk(512 * s, 512 * (s + 1)))
        vop(lambda e: e.tensor_copy(out=c.botile[:, :], in_=pB[:, 0:1024]), r=pbk(0, 1024), w=["botile"])
        wv3 = dr["wv1"].rearrange("p (n f) -> p n f", n=8)
        for ch in range(NCH):
            wb = 4096 + 1024 * (ch % 2)
            gdma(lambda e, ch=ch, wb=wb: e.dma_start(
                out=c.ws[:, wb:wb + 1024].rearrange("p (n f) -> p n f", n=8),
                in_=wv3[:, :, 128 * ch:128 * (ch + 1)]), r=(), w=WSK(wb, wb + 1024), tag="wv1")
            for s4 in range(4):
                bank = s4 % 2
                for cc in range(NCH):
                    mm(pB[:, 512 * bank:512 * (bank + 1)],
                       c.ws[:, wb + 128 * cc:wb + 128 * (cc + 1)],
                       c.xnT[:, 2048 * cc + 512 * s4:2048 * cc + 512 * (s4 + 1)],
                       cc == 0, cc == 7,
                       r=WSK(wb, wb + 1024) + xkk(2048 * cc + 512 * s4, 2048 * cc + 512 * (s4 + 1)),
                       w=[f"pB:{bank}"])
                sop(lambda e, ch=ch, s4=s4, bank=bank: e.activation(
                    out=R[:, R_VBUF + 2048 * ch + 512 * s4:R_VBUF + 2048 * ch + 512 * (s4 + 1)],
                    in_=pB[:, 512 * bank:512 * (bank + 1)], func=AF.Identity,
                    bias=c.bv1s[:, ch:ch + 1], scale=1.0),
                    r=[f"pB:{bank}", "bv1_sb"],
                    w=ak(R_VBUF + 2048 * ch + 512 * s4, R_VBUF + 2048 * ch + 512 * (s4 + 1)))
        kcv = c.scr1[:, :].bitcast(BF16)
        wq3l = dr["wq1"].rearrange("p (n f) -> p n f", n=8)
        wk3l = dr["wk1"].rearrange("p (n f) -> p n f", n=8)
        for ch in range(NCH):
            wbq = 1024 * (ch % 2)
            wbk = 2048 + 1024 * (ch % 2)
            gdma(lambda e, ch=ch, wbq=wbq: e.dma_start(
                out=c.ws[:, wbq:wbq + 1024].rearrange("p (n f) -> p n f", n=8),
                in_=wq3l[:, :, 128 * ch:128 * (ch + 1)]), r=(), w=WSK(wbq, wbq + 1024), tag="wq1")
            gdma(lambda e, ch=ch, wbk=wbk: e.dma_start(
                out=c.ws[:, wbk:wbk + 1024].rearrange("p (n f) -> p n f", n=8),
                in_=wk3l[:, :, 128 * ch:128 * (ch + 1)]), r=(), w=WSK(wbk, wbk + 1024), tag="wk1")
            for (wb, bsb, bkey, dst, dkey) in (
                    (wbq, c.bq1s, "bq1_sb", c.ET, ["ET:0", "ET:1"]),
                    (wbk, c.bk1s, "bk1_sb", kcv, SCR1)):
                for s4 in range(4):
                    bank = s4 % 2
                    for cc in range(NCH):
                        mm(pB[:, 512 * bank:512 * (bank + 1)],
                           c.ws[:, wb + 128 * cc:wb + 128 * (cc + 1)],
                           c.xnT[:, 2048 * cc + 512 * s4:2048 * cc + 512 * (s4 + 1)],
                           cc == 0, cc == 7,
                           r=WSK(wb, wb + 1024) + xkk(2048 * cc + 512 * s4, 2048 * cc + 512 * (s4 + 1)),
                           w=[f"pB:{bank}"])
                    sop(lambda e, ch=ch, bank=bank, s4=s4, bsb=bsb: e.activation(
                        out=c.scr0[:, 512 * (s4 % 2):512 * (s4 % 2) + 512],
                        in_=pB[:, 512 * bank:512 * (bank + 1)], func=AF.Square,
                        bias=bsb[:, ch:ch + 1], scale=1.0),
                        r=[f"pB:{bank}", bkey], w=["scr0"])
                    sop(lambda e, s4=s4, dst=dst: e.activation(
                        out=dst[:, 512 * s4:512 * (s4 + 1)],
                        in_=c.scr0[:, 512 * (s4 % 2):512 * (s4 % 2) + 512], func=AF.Exp, scale=-0.5),
                        r=["scr0"], w=dkey)
            vop(lambda e: e.tensor_mul(out=c.ET[:, :], in0=c.ET[:, :], in1=kcv[:, 0:2048]),
                r=["ET:0", "ET:1"] + SCR1, w=["ET:0", "ET:1"])
            for s4 in range(4):
                mm(pA[0:16, 512 * s4:512 * (s4 + 1)], c.eheadb[:, 16 * ch:16 * (ch + 1)],
                   c.ET[:, 512 * s4:512 * (s4 + 1)], ch == 0, ch == 7,
                   r=["ehead_b", "ET:0", "ET:1"], w=[f"pA:{s4}"])
        vop(lambda e: e.tensor_copy(out=c.ET[0:16, 0:2048], in_=pA[0:16, 0:2048]),
            r=pak(0, 2048), w=["ET:0", "ET:1"])
        for ch in range(NCH):
            for s4 in range(4):
                bank = s4 % 2
                mm(pB[:, 512 * bank:512 * (bank + 1)], c.ehead2b[:, 128 * ch:128 * (ch + 1)],
                   c.ET[0:16, 512 * s4:512 * (s4 + 1)], True, True,
                   r=["ehead2_b", "ET:0", "ET:1"], w=[f"pB:{bank}"])
                vv = R[:, R_VBUF + 2048 * ch + 512 * s4:R_VBUF + 2048 * ch + 512 * (s4 + 1)]
                vk = ak(R_VBUF + 2048 * ch + 512 * s4, R_VBUF + 2048 * ch + 512 * (s4 + 1))
                vop(lambda e, vv=vv, bank=bank: e.tensor_mul(
                    out=vv, in0=vv, in1=pB[:, 512 * bank:512 * (bank + 1)]),
                    r=[f"pB:{bank}"] + vk, w=vk)
        wo3l = dr["wo1"].rearrange("p (n f) -> p n f", n=8)
        for os_ in range(2):
            gdma(lambda e, os_=os_: e.dma_start(
                out=c.ws[:, 0:4096].rearrange("p (n f) -> p n f", n=8),
                in_=wo3l[:, :, 512 * os_:512 * (os_ + 1)]), r=(), w=WSK(0, 4096), tag="wo1")
            for tt in range(NT):
                bank = tt % 4
                for ch in range(NCH):
                    mm(pA[:, 512 * bank:512 * (bank + 1)],
                       R[:, R_VBUF + 2048 * ch + 128 * tt:R_VBUF + 2048 * ch + 128 * (tt + 1)],
                       c.ws[:, 512 * ch:512 * ch + 512], ch == 0, ch == 7,
                       r=ak(R_VBUF + 2048 * ch + 128 * tt, R_VBUF + 2048 * ch + 128 * (tt + 1)) + WSK(512 * ch, 512 * ch + 512),
                       w=[f"pA:{bank}"])
                xs = c.x[:, 1024 * tt + 512 * os_:1024 * tt + 512 * (os_ + 1)]
                vop(lambda e, xs=xs, bank=bank: e.tensor_add(out=xs, in0=xs,
                                                             in1=pA[:, 512 * bank:512 * (bank + 1)]),
                    r=[f"pA:{bank}", f"x:{tt}"], w=[f"x:{tt}"])
                vop(lambda e, xs=xs, os_=os_: e.tensor_add(out=xs, in0=xs,
                                                           in1=c.botile[:, 512 * os_:512 * (os_ + 1)]),
                    r=["botile", f"x:{tt}"], w=[f"x:{tt}"])

    def l1_ffn():
        for tt in range(NT):
            bank = tt % 2
            pv = pA[:, 512 * bank:512 * bank + 8]
            for cc in range(NCH):
                mm(pv, c.xnT[:, 2048 * cc + 128 * tt:2048 * cc + 128 * (tt + 1)],
                   c.gwls[:, 8 * cc:8 * (cc + 1)], cc == 0, cc == 7,
                   r=["gwl_b"] + xkk(2048 * cc + 128 * tt, 2048 * cc + 128 * (tt + 1)),
                   w=[f"pA:{bank}"])
            vop(lambda e, pv=pv: e.reduce_max(out=c.tmpc[:, 2:3], in_=pv, axis=AX.X),
                r=[f"pA:{bank}"], w=["tmpc"])
            vop(lambda e: e.tensor_scalar(out=c.tmpc[:, 2:3], in0=c.tmpc[:, 2:3], scalar1=-1.0,
                                          scalar2=None, op0=ALU.mult), r=["tmpc"], w=["tmpc"])
            sop(lambda e, pv=pv: e.activation(out=c.scr0[:, 0:8], in_=pv, func=AF.Exp,
                                              bias=c.tmpc[:, 2:3], scale=1.0),
                r=[f"pA:{bank}", "tmpc"], w=["scr0"])
            vop(lambda e: e.reduce_sum(out=c.tmpc[:, 3:4], in_=c.scr0[:, 0:8], axis=AX.X),
                r=["scr0"], w=["tmpc"])
            vop(lambda e: e.reciprocal(out=c.tmpc[:, 3:4], in_=c.tmpc[:, 3:4]), r=["tmpc"], w=["tmpc"])
            vop(lambda e, tt=tt: e.tensor_scalar(out=c.probs[:, 8 * tt:8 * (tt + 1)],
                                                 in0=c.scr0[:, 0:8], scalar1=c.tmpc[:, 3:4],
                                                 scalar2=None, op0=ALU.mult),
                r=["scr0", "tmpc"], w=["probs"])
        vop(lambda e: e.tensor_copy(out=c.probsb[:, :], in_=c.probs[:, :]), r=["probs"], w=["probsb"])
        vop(lambda e: e.memset(c.bis[:, 0:1], 0.0), r=(), w=["bis"])
        vop(lambda e: e.memset(c.bis[:, 1:2], 1.0), r=(), w=["bis"])

        def mask_from(colidx, final):
            tr(pB[0:1, 1536:1544], c.bis[:, colidx:colidx + 1], c.idf[0:8, 0:8], r=["bis", "idf_sb"], w=["pB:3"])
            vop(lambda e: e.tensor_copy(out=c.trow[0:1, 0:8], in_=pB[0:1, 1536:1544]),
                r=["pB:3"], w=["trow"])
            vop(lambda e: e.tensor_copy(
                out=c.trow[0:1, 128:256].rearrange("a (o f) -> a o f", o=16),
                in_=c.trow[0:1, 0:8].rearrange("a (o f) -> a o f", o=1).to_broadcast([1, 16, 8])),
                r=["trow"], w=["trow"])
            mm(pA[:, 1536:1664], c.onesf[0:1, 0:128], c.trow[0:1, 128:256], True, True,
               r=["onesf_sb", "trow"], w=["pA:3"])
            vop(lambda e: e.tensor_tensor(out=c.mask[:, :], in0=c.probs[:, :],
                                          in1=pA[:, 1536:1664], op=ALU.is_ge),
                r=["probs", "pA:3"], w=["mask"])
            if final:
                return
            mm(pB[:, 1664:1665], c.mask[:, :], c.onesb[:, 0:1], True, True,
               r=["mask", "onesf_b"], w=["pB:3"])
            vop(lambda e: e.tensor_copy(out=c.cnt1b[:, :], in_=pB[:, 1664:1665]),
                r=["pB:3"], w=["cnt1b"])
            mm(pA[0:8, 1664:1665], c.sel8b[:, :], c.cnt1b[:, :], True, True,
               r=["sel8_b", "cnt1b"], w=["pA:3"])
            vop(lambda e: e.tensor_scalar(out=c.bis[:, 3:4], in0=pA[0:8, 1664:1665],
                                          scalar1=511.5, scalar2=None, op0=ALU.is_ge),
                r=["pA:3"], w=["bis"])
            vop(lambda e: e.tensor_sub(out=c.bis[:, 4:5], in0=c.bis[:, 2:3], in1=c.bis[:, 0:1]),
                r=["bis"], w=["bis"])
            vop(lambda e: e.tensor_sub(out=c.bis[:, 5:6], in0=c.bis[:, 1:2], in1=c.bis[:, 2:3]),
                r=["bis"], w=["bis"])
            vop(lambda e: e.scalar_tensor_tensor(out=c.bis[:, 0:1], in0=c.bis[:, 4:5],
                                                 scalar=c.bis[:, 3:4], in1=c.bis[:, 0:1],
                                                 op0=ALU.mult, op1=ALU.add), r=["bis"], w=["bis"])
            vop(lambda e: e.scalar_tensor_tensor(out=c.bis[:, 1:2], in0=c.bis[:, 5:6],
                                                 scalar=c.bis[:, 3:4], in1=c.bis[:, 2:3],
                                                 op0=ALU.mult, op1=ALU.add), r=["bis"], w=["bis"])

        for it in range(30):
            vop(lambda e: e.tensor_add(out=c.bis[:, 2:3], in0=c.bis[:, 0:1], in1=c.bis[:, 1:2]),
                r=["bis"], w=["bis"])
            vop(lambda e: e.tensor_scalar(out=c.bis[:, 2:3], in0=c.bis[:, 2:3], scalar1=0.5,
                                          scalar2=None, op0=ALU.mult), r=["bis"], w=["bis"])
            mask_from(2, False)
        mask_from(0, True)
        vop(lambda e: e.tensor_copy(out=c.maskf[:, :], in_=c.mask[:, :]), r=["mask"], w=["maskf"])
        for tt in range(NT):
            bank = tt % 2
            mm(pB[:, 512 * bank:512 * bank + 8], c.trilb[:, :], c.mask[:, 8 * tt:8 * (tt + 1)],
               True, True, r=["tril_b", "mask"], w=[f"pB:{bank}"])
            vop(lambda e, tt=tt, bank=bank: e.tensor_copy(
                out=c.cum[:, 8 * tt:8 * (tt + 1)], in_=pB[:, 512 * bank:512 * bank + 8]),
                r=[f"pB:{bank}"], w=["cum"])
            mm(pB[0:1, 1024 + 8 * tt:1024 + 8 * (tt + 1)], c.onesb[:, 0:1], c.mask[:, 8 * tt:8 * (tt + 1)],
               True, True, r=["onesf_b", "mask"], w=["pB:2"])
            vop(lambda e, tt=tt: e.tensor_copy(out=c.totrow[0:1, 8 * tt:8 * (tt + 1)],
                                               in_=pB[0:1, 1024 + 8 * tt:1024 + 8 * (tt + 1)]),
                r=["pB:2"], w=["totrow"])
        tr(pB[:, 1536:1537], c.totrow[0:1, 0:128], c.idf[0:1, 0:1], r=["totrow", "idf_sb"], w=["pB:3"])
        vop(lambda e: e.tensor_copy(out=c.totb[:, :], in_=pB[:, 1536:1537]), r=["pB:3"], w=["totb"])
        mm(pB[:, 1537:1538], c.trieb[:, :], c.totb[:, :], True, True,
           r=["trie_b", "totb"], w=["pB:3"])
        vop(lambda e: e.tensor_copy(out=c.tmpc[:, 1:2], in_=pB[:, 1537:1538]), r=["pB:3"], w=["tmpc"])
        tr(pB[0:1, 1544:1672], c.tmpc[:, 1:2], c.idf, r=["tmpc", "idf_sb"], w=["pB:3"])
        vop(lambda e: e.tensor_copy(out=c.trow[0:1, 256:384], in_=pB[0:1, 1544:1672]),
            r=["pB:3"], w=["trow"])
        mm(pA[:, 1536:1664], c.onesf[0:1, 0:128], c.trow[0:1, 256:384], True, True,
           r=["onesf_sb", "trow"], w=["pA:3"])
        vop(lambda e: e.tensor_copy(out=c.offsb[:, :], in_=pA[:, 1536:1664]), r=["pA:3"], w=["offsb"])
        for e_ in range(8):
            for tt in range(NT):
                vop(lambda e, e_=e_, tt=tt: e.tensor_scalar(
                    out=c.slots[:, tt:tt + 1], in0=c.cum[:, 8 * tt + e_:8 * tt + e_ + 1],
                    scalar1=c.offsb[:, 8 * tt + e_:8 * tt + e_ + 1], scalar2=32767.0,
                    op0=ALU.add, op1=ALU.add), r=["cum", "offsb"], w=["slots"])
                vop(lambda e, e_=e_, tt=tt: e.tensor_scalar(
                    out=c.slots[:, tt:tt + 1], in0=c.slots[:, tt:tt + 1],
                    scalar1=c.maskf[:, 8 * tt + e_:8 * tt + e_ + 1], scalar2=32768.0,
                    op0=ALU.mult, op1=ALU.subtract), r=["slots", "maskf"], w=["slots"])
            for tt in range(NT):
                vop(lambda e, tt=tt: e.tensor_scalar(
                    out=R[:, R_SEL + 512 * tt:R_SEL + 512 * (tt + 1)], in0=c.iota512[:, :],
                    scalar1=c.slots[:, tt:tt + 1], scalar2=0.0,
                    op0=ALU.subtract, op1=ALU.is_equal),
                    r=["iota512_sb", "slots"], w=ak(R_SEL + 512 * tt, R_SEL + 512 * (tt + 1)))
            for cc in range(NCH):
                bank = cc % 2
                for tt in range(NT):
                    mm(pA[:, 512 * bank:512 * (bank + 1)],
                       R[:, R_XNTOK + 1024 * tt + 128 * cc:R_XNTOK + 1024 * tt + 128 * (cc + 1)],
                       R[:, R_SEL + 512 * tt:R_SEL + 512 * (tt + 1)],
                       tt == 0, tt == 15,
                       r=ak(R_XNTOK + 1024 * tt + 128 * cc, R_XNTOK + 1024 * tt + 128 * (cc + 1)) + ak(R_SEL + 512 * tt, R_SEL + 512 * (tt + 1)),
                       w=[f"pA:{bank}"])
                vop(lambda e, cc=cc, bank=bank: e.tensor_copy(
                    out=R[:, R_XIN + 512 * cc:R_XIN + 512 * (cc + 1)],
                    in_=pA[:, 512 * bank:512 * (bank + 1)]),
                    r=[f"pA:{bank}"], w=ak(R_XIN + 512 * cc, R_XIN + 512 * (cc + 1)))
            for tt in range(NT):
                mm(pB[0:1, 1024:1536], c.probsb[:, 8 * tt + e_:8 * tt + e_ + 1],
                   R[:, R_SEL + 512 * tt:R_SEL + 512 * (tt + 1)], tt == 0, tt == 15,
                   r=["probsb"] + ak(R_SEL + 512 * tt, R_SEL + 512 * (tt + 1)), w=["pB:2"])
            vop(lambda e: e.tensor_copy(out=c.trow[0:1, 0:512], in_=pB[0:1, 1024:1536]),
                r=["pB:2"], w=["trow"])
            for sc in range(4):
                tr(pB[:, 1536 + sc:1537 + sc], c.trow[0:1, 128 * sc:128 * (sc + 1)], c.idf[0:1, 0:1],
                   r=["trow", "idf_sb"], w=["pB:3"])
            vop(lambda e: e.tensor_copy(out=c.pcols[:, 0:4], in_=pB[:, 1536:1540]),
                r=["pB:3"], w=["pcols_sb"])
            w13 = dr["w1l"][e_].rearrange("p (n f) -> p n f", n=8)
            for hg in range(16):
                wb = 2048 * (hg % 2)
                gdma(lambda e, hg=hg, wb=wb: e.dma_start(
                    out=c.ws[:, wb:wb + 2048].rearrange("p (n f) -> p n f", n=8),
                    in_=w13[:, :, 256 * hg:256 * (hg + 1)]), r=(), w=WSK(wb, wb + 2048), tag="w1")
                for hj in range(2):
                    hc = 2 * hg + hj
                    bank = hc % 2
                    for cc in range(NCH):
                        mm(pA[:, 512 * bank:512 * (bank + 1)],
                           c.ws[:, wb + 256 * cc + 128 * hj:wb + 256 * cc + 128 * (hj + 1)],
                           R[:, R_XIN + 512 * cc:R_XIN + 512 * (cc + 1)],
                           cc == 0, cc == 7,
                           r=WSK(wb, wb + 2048) + ak(R_XIN + 512 * cc, R_XIN + 512 * (cc + 1)),
                           w=[f"pA:{bank}"])
                    sop(lambda e, hc=hc, bank=bank: e.activation(
                        out=c.xnT[:, 512 * hc:512 * (hc + 1)],
                        in_=pA[:, 512 * bank:512 * (bank + 1)], func=AF.Gelu),
                        r=[f"pA:{bank}"], w=xkk(512 * hc, 512 * (hc + 1)))
            w23 = dr["w2l"][e_].rearrange("p (n f) -> p n f", n=32)
            for cs in range(2):
                for hg in range(8):
                    wb = 2048 * (hg % 2)
                    gdma(lambda e, hg=hg, cs=cs, wb=wb: e.dma_start(
                        out=c.ws[:, wb:wb + 2048].rearrange("p (n f) -> p n f", n=4),
                        in_=w23[:, 4 * hg:4 * (hg + 1), 512 * cs:512 * (cs + 1)]),
                        r=(), w=WSK(wb, wb + 2048), tag="w2")
                    for hj in range(4):
                        hc = 4 * hg + hj
                        for st in range(4):
                            mm(pB[:, 512 * st:512 * (st + 1)],
                               c.xnT[:, 512 * hc + 128 * st:512 * hc + 128 * (st + 1)],
                               c.ws[:, wb + 512 * hj:wb + 512 * (hj + 1)],
                               hc == 0, hc == 31,
                               r=WSK(wb, wb + 2048) + xkk(512 * hc + 128 * st, 512 * hc + 128 * (st + 1)),
                               w=[f"pB:{st}"])
                for st in range(4):
                    vop(lambda e, st=st, cs=cs: e.tensor_scalar(
                        out=R[:, R_Y + 1024 * st + 512 * cs:R_Y + 1024 * st + 512 * (cs + 1)],
                        in0=pB[:, 512 * st:512 * (st + 1)], scalar1=c.pcols[:, st:st + 1],
                        scalar2=None, op0=ALU.mult),
                        r=[f"pB:{st}", "pcols_sb"], w=ak(R_Y + 1024 * st + 512 * cs, R_Y + 1024 * st + 512 * (cs + 1)))
            for tt in range(NT):
                tr(pB[0:1, 1024 + 128 * (tt % 2):1024 + 128 * (tt % 2) + 128],
                   c.slots[:, tt:tt + 1], c.idf, r=["slots", "idf_sb"], w=["pB:2"])
                vop(lambda e, tt=tt: e.tensor_copy(
                    out=c.scr1[0:1, 128 * tt:128 * (tt + 1)],
                    in_=pB[0:1, 1024 + 128 * (tt % 2):1024 + 128 * (tt % 2) + 128]),
                    r=["pB:2"], w=[SCR1[tt // 8]])
            for s in range(4):
                mm(pA[:, 512 * s:512 * (s + 1)], c.onesf[0:1, 0:128],
                   c.scr1[0:1, 512 * s:512 * (s + 1)], True, True,
                   r=["onesf_sb"] + SCR1, w=[f"pA:{s}"])
            for s in range(4):
                vop(lambda e, s=s: e.tensor_copy(out=c.scr1[:, 512 * s:512 * (s + 1)],
                                                 in_=pA[:, 512 * s:512 * (s + 1)]),
                    r=[f"pA:{s}"], w=[SCR1[s // 2]])
            for sc in range(4):
                vop(lambda e, sc=sc: e.tensor_scalar(
                    out=R[:, R_SEL + 2048 * sc:R_SEL + 2048 * (sc + 1)], in0=c.scr1[:, :],
                    scalar1=c.iotacol[:, sc:sc + 1], scalar2=0.0,
                    op0=ALU.subtract, op1=ALU.is_equal),
                    r=SCR1 + ["iotacol_sb"], w=ak(R_SEL + 2048 * sc, R_SEL + 2048 * (sc + 1)))
            for tt in range(NT):
                for cs in range(2):
                    bank = (tt * 2 + cs) % 2
                    for sc in range(4):
                        mm(pA[:, 512 * bank:512 * (bank + 1)],
                           R[:, R_SEL + 2048 * sc + 128 * tt:R_SEL + 2048 * sc + 128 * (tt + 1)],
                           R[:, R_Y + 1024 * sc + 512 * cs:R_Y + 1024 * sc + 512 * (cs + 1)],
                           sc == 0, sc == 3,
                           r=ak(R_SEL + 2048 * sc + 128 * tt, R_SEL + 2048 * sc + 128 * (tt + 1)) + ak(R_Y + 1024 * sc + 512 * cs, R_Y + 1024 * sc + 512 * (cs + 1)),
                           w=[f"pA:{bank}"])
                    xs = c.x[:, 1024 * tt + 512 * cs:1024 * tt + 512 * (cs + 1)]
                    vop(lambda e, xs=xs, bank=bank: e.tensor_add(
                        out=xs, in0=xs, in1=pA[:, 512 * bank:512 * (bank + 1)]),
                        r=[f"pA:{bank}", f"x:{tt}"], w=[f"x:{tt}"])

    def store():
        sdma(lambda e: e.dma_start(
            out=c.out.rearrange("(n p) f -> p n f", p=128),
            in_=c.x[:, :].rearrange("p (n f) -> p n f", n=NT)), r=XT(), w=(), tag="stx")

    stages = ["ln0", "xnt0", "gn", "attn0", "ffn0", "ln1", "xnt1", "attn1", "ffn1"]
    stop = len(stages) if upto == "full" else stages.index(upto) + 1
    si = 0
    ln(0); si += 1
    if si < stop:
        build_xnT(); si += 1
    if si < stop:
        gn(); si += 1
    if si < stop:
        l0_attn(); si += 1
    if si < stop:
        l0_ffn(); si += 1
    if si < stop:
        ln(1); si += 1
    if si < stop:
        build_xnT(); si += 1
    if si < stop:
        l1_attn(); si += 1
    if si < stop:
        l1_ffn(); si += 1
    if "xntok" in taps:
        tap("xntok", R[:, R_XNTOK:R_XNTOK + 16384], ak(R_XNTOK, R_XNTOK + 16384), (128, 16384), BF16)
    if "xnt" in taps:
        tap("xnt", c.xnT[:, :], xkk(0, 16384), (128, 16384), BF16)
    if "gnab" in taps:
        tap("gnab", c.gnA[:, :], ["gnA"], (128, 8), F32)
        tap("gnbb", c.gnB[:, :], ["gnB"], (128, 8), F32)
    if "dbg1" in taps:
        tap("dbg1", c.offsb[0:1, :], ["offsb"], (1, 128), F32)
        tap("dbg2", c.offsb[:, :], ["offsb"], (128, 128), F32)
        tap("bqkvd", c.bqkvc[:, :], ["bqkv_sb"], (128, 24), F32)
        tap("gnbbd", c.gnBb[:, :], ["gnBb"], (128, 8), BF16)
        tap("tmpcd", c.tmpc[:, :], ["tmpc"], (128, 4), F32)
        tap("etd", c.ET[:, :], ["ET:0", "ET:1"], (128, 2048), BF16)
    if "qkv" in taps:
        tap("qp", R[:, R_QP:R_QP + 2048], ak(R_QP, R_QP + 2048), (128, 2048), BF16)
        tap("kp", R[:, R_KP:R_KP + 2048], ak(R_KP, R_KP + 2048), (128, 2048), BF16)
        tap("vp", R[:, R_VP:R_VP + 2080], ak(R_VP, R_VP + 2080), (128, 2080), BF16)
        tap("mb2", c.mb2[:, :], ["mb2"], (128, 2), F32)
        tap("bcall", c.bcall[:, :], ["bcall"], (128, 24), F32)
    if "ot" in taps:
        tap("ot", R[:, R_OT:R_OT + 16384], ak(R_OT, R_OT + 16384), (128, 16384), BF16)
    if "gatet" in taps:
        tap("gatet", R[0:16, R_GATET:R_GATET + 2048], ak(R_GATET, R_GATET + 2048), (16, 2048), BF16)
    if "probs" in taps:
        tap("probs", c.probs[:, :], ["probs"], (128, 128), F32)
    if "mask" in taps:
        tap("mask", c.mask[:, :], ["mask"], (128, 128), BF16)
    if "cum" in taps:
        tap("cum", c.cum[:, :], ["cum"], (128, 128), F32)
        tap("offsb", c.offsb[:, :], ["offsb"], (128, 128), F32)
    if "coefb" in taps:
        tap("coefb", c.ET[0:16, :], ["ET:0", "ET:1"], (16, 2048), BF16)
    store()


_NC_CACHE = {}


def kernel(**inputs):
    host = host_prep(inputs)
    x_full = np.asarray(inputs["L0_attn_x"], dtype=np.float32)
    if "full" not in _NC_CACHE:
        _NC_CACHE["full"] = build_program()
    nc = _NC_CACHE["full"]
    in_maps = []
    for b in range(8):
        m = dict(host)
        m["x"] = np.ascontiguousarray(x_full[b])
        in_maps.append(m)
    res = run_bass_kernel_spmd(nc, in_maps, core_ids=list(range(8)))
    out = np.stack([np.asarray(res.results[i]["out"]) for i in range(8)])
    return out.astype(np.float32)


# version 04
# speedup vs baseline: 1.4085x; 1.1686x over previous
import math
import os
import numpy as np
import ml_dtypes
from contextlib import ExitStack

import concourse.bass as bass
import concourse.mybir as mybir
from concourse.bass_utils import run_bass_kernel_spmd

F32 = mybir.dt.float32
BF16 = mybir.dt.bfloat16
AF = mybir.ActivationFunctionType
ALU = mybir.AluOpType
AX = mybir.AxisListType

S = 2048
C = 1024
NT = S // 128
NCH = C // 128
NHEAD = 16
DH = 64

ENGINES = ("tensor", "vector", "scalar", "gpsimd", "sync")


class Em:
    def __init__(self, n_dma_sems=24):
        self.evs = []
        self.eng_ops = {e: [] for e in ENGINES}
        self.lw = {}
        self.rd = {}
        self.n_dma = n_dma_sems
        self.dma_use = [0] * n_dma_sems
        self.dma_rr = 0
        self.emitted = []

    @staticmethod
    def _keys(x):
        if x is None:
            return []
        if isinstance(x, str):
            return [x]
        out = []
        for e in x:
            if isinstance(e, str):
                out.append(e)
            else:
                out.extend(e)
        return out

    def _deps(self, reads, writes):
        deps = set()
        for k in reads:
            if k in self.lw:
                deps.add(self.lw[k])
        for k in writes:
            if k in self.lw:
                deps.add(self.lw[k])
            for r in self.rd.get(k, ()):
                deps.add(r)
        return deps

    def _record(self, kind, eng, fn, reads, writes, tag, extra):
        reads = self._keys(reads)
        writes = self._keys(writes)
        evid = len(self.evs)
        ev = dict(kind=kind, eng=eng, fn=fn, deps=self._deps(reads, writes),
                  marked=False, tag=tag)
        ev.update(extra)
        self.evs.append(ev)
        self.eng_ops[eng].append(evid)
        for k in reads:
            self.rd.setdefault(k, []).append(evid)
        for k in writes:
            self.lw[k] = evid
            self.rd[k] = []
        return evid

    def op(self, eng, fn, reads=(), writes=(), tag=""):
        return self._record("op", eng, fn, reads, writes, tag, {})

    def dma(self, eng, fn, reads=(), writes=(), tag=""):
        q = self.dma_rr
        self.dma_rr = (self.dma_rr + 1) % self.n_dma
        self.dma_use[q] += 1
        return self._record("dma", eng, fn, reads, writes, tag,
                            dict(q=q, use=self.dma_use[q]))

    def build(self, nc, es):
        for ev in self.evs:
            for d in ev["deps"]:
                dev = self.evs[d]
                if dev["kind"] == "op":
                    dev["marked"] = True
        inc = {e: 0 for e in ENGINES}
        for e in ENGINES:
            for evid in self.eng_ops[e]:
                ev = self.evs[evid]
                if ev["kind"] == "op" and ev["marked"]:
                    inc[e] += 1
                ev["incval"] = inc[e]
        esem = {e: es.enter_context(nc.semaphore(f"sem_{e}")) for e in ENGINES}
        dsem = [es.enter_context(nc.semaphore(f"dma{i}")) for i in range(self.n_dma)]
        self._esem, self._dsem = esem, dsem

        emr = self

        def body(eng_name):
            def run(engobj):
                wm = {}
                for evid in emr.eng_ops[eng_name]:
                    ev = emr.evs[evid]
                    waits = []
                    for d in ev["deps"]:
                        dev = emr.evs[d]
                        if dev["kind"] == "dma":
                            waits.append((("d", dev["q"]), 16 * dev["use"], dsem[dev["q"]]))
                        else:
                            pe = dev["eng"]
                            if pe == eng_name and eng_name == "tensor":
                                continue
                            waits.append((("e", pe), dev["incval"], esem[pe]))
                    if ev["kind"] == "dma" and ev["use"] > 1:
                        waits.append((("d", ev["q"]), 16 * (ev["use"] - 1), dsem[ev["q"]]))
                    wlog = []
                    for key, val, sem in sorted(waits, key=lambda w: (str(w[0]), w[1])):
                        if wm.get(key, 0) < val:
                            engobj.wait_ge(sem, val)
                            wm[key] = val
                            wlog.append((key, val))
                    emr.emitted.append((eng_name, evid, ev["tag"], wlog,
                                        ev["incval"] if ev.get("marked") else None))
                    inst = ev["fn"](engobj)
                    if ev["kind"] == "dma":
                        inst.then_inc(dsem[ev["q"]], 16)
                    elif ev["marked"]:
                        inst.then_inc(esem[eng_name], 1)
                if eng_name == "sync":
                    for i in range(emr.n_dma):
                        if emr.dma_use[i]:
                            engobj.wait_ge(dsem[i], 16 * emr.dma_use[i])
            return run

        with nc.Block() as block:
            block.sync(body("sync"))
            block.gpsimd(body("gpsimd"))
            block.vector(body("vector"))
            block.scalar(body("scalar"))
            block.tensor(body("tensor"))



def _chunked(w):
    n, rem = divmod(w.shape[0], 128)
    assert rem == 0
    return np.ascontiguousarray(
        w.reshape(n, 128, w.shape[1]).transpose(1, 0, 2).reshape(128, n * w.shape[1]))


def _pcols(v):
    n = v.shape[0] // 128
    return np.ascontiguousarray(v.reshape(n, 128).T)


def _row(v):
    return np.ascontiguousarray(v.reshape(1, -1))


def host_consts():
    c = {}
    c["onesf"] = np.ones((128, 128), np.float32)
    c["idf"] = np.eye(128, dtype=np.float32)
    s64 = np.zeros((64, 128), np.float32)
    for d in range(64):
        s64[d, 64 + d] = 1.0
    c["s64"] = s64
    eh = np.zeros((128, 16 * NCH), np.float32)
    for ch in range(NCH):
        for p in range(128):
            eh[p, 16 * ch + 2 * ch + p // 64] = 1.0
    c["ehead"] = eh
    eh2 = np.zeros((16, 128 * NCH), np.float32)
    for ch in range(NCH):
        for p in range(128):
            eh2[2 * ch + p // 64, 128 * ch + p] = 1.0
    c["ehead2"] = eh2
    s16 = np.zeros((16, 128 * 16), np.float32)
    for r in range(16):
        s16[r, 128 * r:128 * (r + 1)] = 1.0
    c["sel16"] = s16
    egn = np.zeros((128, 32 * NCH), np.float32)
    egnT = np.zeros((32, 128 * NCH), np.float32)
    for ch in range(NCH):
        for p in range(128):
            g = (128 * ch + p) // 32
            egn[p, 32 * ch + g] = 1.0
            egnT[g, 128 * ch + p] = 1.0
    c["egn"] = egn
    c["egnT"] = egnT
    tril = np.tril(np.ones((128, 128), np.float32)).T
    c["tril"] = np.ascontiguousarray(tril)
    trie = np.zeros((128, 128), np.float32)
    for p2 in range(128):
        for p in range(128):
            if (p2 % 8) == (p % 8) and (p2 // 8) < (p // 8):
                trie[p2, p] = 1.0
    c["trie"] = trie
    c["iota512"] = np.broadcast_to(np.arange(512, dtype=np.float32), (128, 512)).copy()
    ic = np.zeros((128, 4), np.float32)
    for sc in range(4):
        ic[:, sc] = 128 * sc + np.arange(128)
    c["iotacol"] = ic
    c["epsrow"] = np.full((128, 1), 1e-5, np.float32)
    h2 = np.zeros((128, 2), np.float32)
    h2[0:64, 0] = 1.0
    h2[64:128, 1] = 1.0
    c["h2"] = h2
    s8 = np.zeros((128, 8), np.float32)
    for f in range(128):
        s8[f, f % 8] = 1.0
    c["sel8"] = s8
    return c


def host_prep(inputs):
    f = np.float32
    d = {}
    d["wqkv"] = _chunked(inputs["L0_attn_W_qkv"].astype(f, copy=False))
    d["bqkv"] = _pcols(inputs["L0_attn_b_qkv"])
    d["gng"] = _pcols(inputs["L0_attn_gn_gamma"])
    d["gnb"] = _pcols(inputs["L0_attn_gn_beta"])
    d["wol"] = _chunked(inputs["L0_attn_W_o"])
    d["bo"] = _row(inputs["L0_attn_b_o"])
    su = inputs["L0_ffn_shared_up"]
    sd = inputs["L0_ffn_shared_down"]
    ru = inputs["L0_ffn_routed_up"]
    rdn = inputs["L0_ffn_routed_down"]
    d["sut"] = np.stack([_chunked(np.ascontiguousarray(su[i].T)) for i in range(2)])
    d["sdt"] = np.stack([_chunked(np.ascontiguousarray(sd[i].T)) for i in range(2)])
    d["rut"] = np.stack([_chunked(np.ascontiguousarray(ru[i].T)) for i in range(16)])
    d["rdt"] = np.stack([_chunked(np.ascontiguousarray(rdn[i].T)) for i in range(16)])
    d["rwt"] = _chunked(np.ascontiguousarray(inputs["L0_ffn_router_w"].T))
    d["wq1"] = _chunked(inputs["L1_attn_Wq"])
    d["wk1"] = _chunked(inputs["L1_attn_Wk"])
    d["wv1"] = _chunked(inputs["L1_attn_Wv"])
    d["wo1"] = _chunked(inputs["L1_attn_Wo"])
    d["bq1"] = _pcols(inputs["L1_attn_bq"])
    d["bk1"] = _pcols(inputs["L1_attn_bk"])
    d["bv1"] = _pcols(inputs["L1_attn_bv"])
    d["bo1"] = _row(inputs["L1_attn_bo"])
    d["gwl"] = _chunked(inputs["L1_ffn_gate_w"])
    w1 = inputs["L1_ffn_w1"]
    w2 = inputs["L1_ffn_w2"]
    d["w1l"] = np.stack([_chunked(w1[e]) for e in range(8)])
    d["w2l"] = np.stack([_chunked(w2[e]) for e in range(8)])
    d.update(host_consts())
    return d



class Ctx:
    pass


def ak(lo, hi):
    return [f"R:{g}" for g in range(lo // 512, (hi + 511) // 512)]


def xkk(lo, hi):
    return [f"xnT:{g}" for g in range(lo // 512, (hi + 511) // 512)]


def pak(lo, hi):
    return [f"pA:{b}" for b in range(lo // 512, (hi + 511) // 512)]


def pbk(lo, hi):
    return [f"pB:{b}" for b in range(lo // 512, (hi + 511) // 512)]


XT = lambda: [f"x:{t}" for t in range(NT)]
XNT_ALL = lambda: [f"xnT:{g}" for g in range(32)]

R_XNTOK = 0
R_QP = 0
R_KP = 2048
R_VP = 4096
R_OT = 16384
R_A0 = 0
R_A1 = 4096
R_GATEB = 8192
R_GATET = 10240
R_U0 = 12288
R_U1 = 16384
R_D0 = 20480
R_D1 = 24576
R_VBUF = 16384
R_SEL = 16384
R_Y = 24576
R_XIN = 28672


def build_program(taps=(), upto="full"):
    nc = bass.Bass("TRN2", target_bir_lowering=False)
    em = Em()
    c = Ctx()
    c.nc, c.em, c.taps = nc, em, {}
    c.fill = []

    dr = {}
    def din(name, shape, dt=F32):
        dr[name] = nc.dram_tensor(name, list(shape), dt, kind="ExternalInput").ap()
    din("x", (S, C))
    din("wqkv", (128, 8 * 3072)); din("bqkv", (128, 24))
    din("gng", (128, 8)); din("gnb", (128, 8))
    din("wol", (128, 8 * 1024)); din("bo", (1, 1024))
    din("sut", (2, 128, 4096)); din("sdt", (2, 128, 4096))
    din("rut", (16, 128, 4096)); din("rdt", (16, 128, 4096))
    din("rwt", (128, 128))
    din("wq1", (128, 8192)); din("wk1", (128, 8192)); din("wv1", (128, 8192))
    din("wo1", (128, 8192))
    din("bq1", (128, 8)); din("bk1", (128, 8)); din("bv1", (128, 8)); din("bo1", (1, 1024))
    din("gwl", (128, 64))
    din("w1l", (8, 128, 32768)); din("w2l", (8, 128, 32768))
    for n, sh in (("onesf", (128, 128)), ("idf", (128, 128)), ("s64", (64, 128)),
                  ("ehead", (128, 128)), ("ehead2", (16, 1024)), ("sel16", (16, 2048)),
                  ("egn", (128, 256)), ("egnT", (32, 1024)), ("tril", (128, 128)),
                  ("trie", (128, 128)), ("iota512", (128, 512)), ("iotacol", (128, 4)),
                  ("epsrow", (128, 1)), ("sel8", (128, 8)), ("h2", (128, 2))):
        din(n, sh)
    out = nc.dram_tensor("out", [S, C], F32, kind="ExternalOutput").ap()
    c.dr, c.out = dr, out

    es = ExitStack()
    sb = lambda n, sh, dt: es.enter_context(nc.sbuf_tensor(n, sh, dt))
    with es:
        c.x = sb("x_sb", [128, NT * 1024], F32)
        c.xnT = sb("xnT_sb", [128, 16384], BF16)
        c.R = sb("reg64", [128, 32768], BF16)
        c.ET = sb("ET", [128, 2048], BF16)
        c.ws = sb("wstream", [128, 6144], BF16)
        c.scr0 = sb("scr0", [128, 1024], F32)
        c.scr1 = sb("scr1", [128, 2048], F32)
        c.onesf = sb("onesf_sb", [128, 128], F32)
        c.onesb = sb("onesb_sb", [128, 128], BF16)
        c.idf = sb("idf_sb", [128, 128], F32)
        c.idb = sb("idb_sb", [128, 128], BF16)
        c.s64b = sb("s64b_sb", [64, 128], BF16)
        c.eheadb = sb("eheadb_sb", [128, 128], BF16)
        c.egnb = sb("egnb_sb", [128, 256], BF16)
        c.egnTb = sb("egnTb_sb", [32, 1024], BF16)
        c.trilb = sb("trilb_sb", [128, 128], BF16)
        c.trieb = sb("trieb_sb", [128, 128], BF16)
        c.sel8b = sb("sel8b_sb", [128, 8], BF16)
        c.iota512 = sb("iota512_sb", [128, 512], F32)
        c.iotacol = sb("iotacol_sb", [128, 4], F32)
        c.lnS = sb("lnS", [128, 16], F32)
        c.lnQ = sb("lnQ", [128, 16], F32)
        c.lnM = sb("lnM", [128, 16], F32)
        c.lnV = sb("lnV", [128, 16], F32)
        c.lnR = sb("lnR", [128, 16], F32)
        c.gnSQ = sb("gnSQ", [128, 16], BF16)
        c.gnSQf = sb("gnSQf", [128, 16], F32)
        c.gst = sb("gst", [32, 4], F32)
        c.gstb = sb("gstb", [32, 4], BF16)
        c.gnE = sb("gnE", [128, 32], F32)
        c.gnA = sb("gnA", [128, 8], F32)
        c.gnAb = sb("gnAb", [128, 8], BF16)
        c.gnB = sb("gnB", [128, 8], F32)
        c.gnBb = sb("gnBb", [128, 8], BF16)
        c.bcall = sb("bcall", [128, 24], F32)
        c.mcol2 = sb("mcol2", [128, 4], F32)
        c.trow = sb("trow", [1, 512], F32)
        c.ehead2b = sb("ehead2b_sb", [16, 1024], BF16)
        c.tmpc = sb("tmpc", [128, 4], F32)
        c.gngs = sb("gng_s", [128, 8], F32)
        c.gnbs = sb("gnb_s", [128, 8], F32)
        c.bqkvc = sb("bqkv_s", [128, 24], F32)
        c.bq1s = sb("bq1_s", [128, 8], F32)
        c.bk1s = sb("bk1_s", [128, 8], F32)
        c.bv1s = sb("bv1_s", [128, 8], F32)
        c.rwts = sb("rwt_s", [128, 128], BF16)
        c.gwls = sb("gwl_s", [128, 64], BF16)
        c.mcol = sb("mcol", [128, 16], F32)
        c.mbias = sb("mbias", [128, 1], F32)
        c.mb2 = sb("mb2", [128, 2], F32)
        c.h2b = sb("h2_s", [128, 2], BF16)
        c.botile = sb("botile", [128, 1024], BF16)
        c.probs = sb("probs128", [128, 128], F32)
        c.probsb = sb("probsb", [128, 128], BF16)
        c.mask = sb("mask128", [128, 128], BF16)
        c.maskf = sb("maskf", [128, 128], F32)
        c.cum = sb("cum128", [128, 128], F32)
        c.offsb = sb("offsb", [128, 128], F32)
        c.totrow = sb("totrow", [1, 128], F32)
        c.bis = sb("bis", [8, 8], F32)
        c.cnt1b = sb("cnt1b", [128, 1], BF16)
        c.totb = sb("totb", [128, 1], BF16)
        c.slots = sb("slots", [128, 16], F32)
        c.neg9 = sb("neg9", [128, 1], F32)
        c.pcols = sb("pcols_sb", [128, 4], F32)
        c.epsc = sb("epsc", [128, 1], F32)
        c.pA = es.enter_context(nc.psum_tensor("pA", [128, 2048], F32))
        c.pB = es.enter_context(nc.psum_tensor("pB", [128, 2048], F32))

        emit_all(c, upto, taps)
        em.build(nc, es)
    return nc


SCR1 = ["scr1:0", "scr1:1"]
WSK = lambda lo, hi: [f"ws:{g}" for g in range(lo // 512, (hi + 511) // 512)]


def emit_all(c, upto, taps):
    em, dr, nc = c.em, c.dr, c.nc
    R, pA, pB = c.R, c.pA, c.pB

    def top(fn, r=(), w=(), tag=""):
        em.op("tensor", fn, r, w, tag)

    def vop(fn, r=(), w=(), tag=""):
        em.op("vector", fn, r, w, tag)

    def sop(fn, r=(), w=(), tag=""):
        em.op("scalar", fn, r, w, tag)

    def gop(fn, r=(), w=(), tag=""):
        em.op("gpsimd", fn, r, w, tag)

    def sdma(fn, r=(), w=(), tag=""):
        em.dma("sync", fn, r, w, tag)

    def gdma(fn, r=(), w=(), tag=""):
        em.dma("gpsimd", fn, r, w, tag)

    tap_stage = {}

    def tap(name, ap, rkeys, shape, dt):
        t = nc.dram_tensor("tap_" + name, list(shape), dt, kind="ExternalOutput").ap()
        sdma(lambda e, t=t, ap=ap: e.dma_start(out=t[:, :], in_=ap), r=rkeys, w=(), tag="tap")

    sdma(lambda e: e.dma_start(
        out=c.x[:, :].rearrange("p (n f) -> p n f", n=NT),
        in_=dr["x"].rearrange("(n p) f -> p n f", p=128)), r=(), w=XT(), tag="ldx")
    for name, dst in (("onesf", c.onesf), ("idf", c.idf), ("iota512", c.iota512),
                      ("iotacol", c.iotacol), ("epsrow", c.epsc), ("gng", c.gngs),
                      ("gnb", c.gnbs), ("bqkv", c.bqkvc), ("bq1", c.bq1s),
                      ("bk1", c.bk1s), ("bv1", c.bv1s)):
        sdma(lambda e, d=dr[name], t=dst: e.dma_start(out=t[:, :], in_=d[:, :]),
             r=(), w=[name + "_sb"], tag="ldc")
    for name, dst in (("onesf", c.onesb), ("idf", c.idb), ("s64", c.s64b),
                      ("ehead", c.eheadb), ("ehead2", c.ehead2b),
                      ("egn", c.egnb), ("egnT", c.egnTb), ("tril", c.trilb),
                      ("trie", c.trieb), ("sel8", c.sel8b), ("h2", c.h2b), ("rwt", c.rwts),
                      ("gwl", c.gwls)):
        gdma(lambda e, d=dr[name], t=dst: e.dma_start(out=t[:, :], in_=d[:, :]),
             r=(), w=[name + "_b"], tag="ldcb")
    vop(lambda e: e.memset(c.neg9[:, :], -1.0e9), r=(), w=["neg9"])

    def mm(out, lhsT, rhs, st, sp, r, w, tag=""):
        top(lambda e, o=out, l=lhsT, rr=rhs, st=st, sp=sp: e.matmul(o, l, rr, start=st, stop=sp), r=r, w=w, tag=tag)

    def tr(out, in_, ident, r, w):
        idap = ident if isinstance(ident, bass.AP) else ident[:, :]
        top(lambda e, o=out, i=in_, d=idap: e.transpose(o, i, d), r=r, w=w)

    def ln(layer):
        addxn = (layer == 0)
        for t in range(NT):
            xt = c.x[:, 1024 * t:1024 * (t + 1)]
            vop(lambda e, xt=xt, t=t: e.reduce_sum(out=c.lnS[:, t:t + 1], in_=xt, axis=AX.X),
                r=[f"x:{t}"], w=["lnS"])
            sop(lambda e, xt=xt, t=t: e.activation(out=c.scr0[:, :], in_=xt, func=AF.Square,
                                                   accum_out=c.lnQ[:, t:t + 1]),
                r=[f"x:{t}"], w=["scr0", "lnQ"])
        vop(lambda e: e.tensor_scalar(out=c.lnM[:, :], in0=c.lnS[:, :], scalar1=1.0 / C,
                                      scalar2=None, op0=ALU.mult), r=["lnS"], w=["lnM"])
        vop(lambda e: e.tensor_mul(out=c.lnV[:, :], in0=c.lnM[:, :], in1=c.lnM[:, :]),
            r=["lnM"], w=["lnV"])
        vop(lambda e: e.scalar_tensor_tensor(out=c.lnV[:, :], in0=c.lnQ[:, :], scalar=1.0 / C,
                                             in1=c.lnV[:, :], op0=ALU.mult, op1=ALU.subtract),
            r=["lnQ", "lnV"], w=["lnV"])
        vop(lambda e: e.tensor_scalar(out=c.lnV[:, :], in0=c.lnV[:, :], scalar1=1e-5,
                                      scalar2=None, op0=ALU.add), r=["lnV"], w=["lnV"])
        sop(lambda e: e.activation(out=c.lnR[:, :], in_=c.lnV[:, :], func=AF.Sqrt),
            r=["lnV"], w=["lnR"])
        vop(lambda e: e.reciprocal(out=c.lnR[:, :], in_=c.lnR[:, :]), r=["lnR"], w=["lnR"])
        for t in range(NT):
            xt = c.x[:, 1024 * t:1024 * (t + 1)]
            dst = R[:, R_XNTOK + 1024 * t:R_XNTOK + 1024 * (t + 1)]
            dkeys = ak(R_XNTOK + 1024 * t, R_XNTOK + 1024 * (t + 1))
            if addxn:
                h = (t % 2) * 1024
                scr = c.scr1[:, h:h + 1024]
                skey = [SCR1[t % 2]]
                vop(lambda e, xt=xt, t=t, scr=scr: e.tensor_scalar(
                    out=scr, in0=xt, scalar1=c.lnM[:, t:t + 1], scalar2=c.lnR[:, t:t + 1],
                    op0=ALU.subtract, op1=ALU.mult), r=[f"x:{t}", "lnM", "lnR"], w=skey)
                vop(lambda e, xt=xt, scr=scr: e.tensor_add(out=xt, in0=xt, in1=scr),
                    r=skey + [f"x:{t}"], w=[f"x:{t}"])
                vop(lambda e, dst=dst, scr=scr: e.tensor_copy(out=dst, in_=scr),
                    r=skey, w=dkeys)
            else:
                vop(lambda e, xt=xt, t=t, dst=dst: e.tensor_scalar(
                    out=dst, in0=xt, scalar1=c.lnM[:, t:t + 1], scalar2=c.lnR[:, t:t + 1],
                    op0=ALU.subtract, op1=ALU.mult), r=[f"x:{t}", "lnM", "lnR"], w=dkeys)

    def build_xnT():
        for cc in range(NCH):
            for tg in range(2):
                bank = (cc * 2 + tg) % 4
                pview = pA[:, 512 * bank:512 * (bank + 1)].bitcast(BF16)
                bkey = [f"pA:{bank}"]
                for j in range(8):
                    t = tg * 8 + j
                    tr(pview[:, 128 * j:128 * (j + 1)],
                       R[:, R_XNTOK + 1024 * t + 128 * cc:R_XNTOK + 1024 * t + 128 * (cc + 1)],
                       c.idb,
                       r=ak(R_XNTOK + 1024 * t + 128 * cc, R_XNTOK + 1024 * t + 128 * (cc + 1)) + ["idf_b"],
                       w=bkey)
                dlo = 2048 * cc + 1024 * tg
                vop(lambda e, pv=pview, cc=cc, tg=tg: e.tensor_copy(
                    out=c.xnT[:, 2048 * cc + 1024 * tg: 2048 * cc + 1024 * tg + 1024], in_=pv),
                    r=bkey, w=xkk(dlo, dlo + 1024))

    def gn():
        for cc in range(NCH):
            ch = c.xnT[:, 2048 * cc:2048 * (cc + 1)]
            rk = xkk(2048 * cc, 2048 * (cc + 1))
            vop(lambda e, ch=ch, cc=cc: e.reduce_sum(out=c.gnSQf[:, 2 * cc:2 * cc + 1], in_=ch, axis=AX.X),
                r=rk, w=["gnSQf"])
            sop(lambda e, ch=ch, cc=cc: e.activation(out=c.scr1[:, :], in_=ch, func=AF.Square,
                                                     accum_out=c.gnSQf[:, 2 * cc + 1:2 * cc + 2]),
                r=rk, w=SCR1 + ["gnSQf"])
        vop(lambda e: e.tensor_copy(out=c.gnSQ[:, :], in_=c.gnSQf[:, :]), r=["gnSQf"], w=["gnSQ"])
        for cc in range(NCH):
            mm(pB[0:32, 0:2], c.egnb[:, 32 * cc:32 * (cc + 1)], c.gnSQ[:, 2 * cc:2 * cc + 2],
               cc == 0, cc == 7, r=["egn_b", "gnSQ"], w=["pB:0"])
        vop(lambda e: e.tensor_scalar(out=c.gst[:, 0:2], in0=pB[0:32, 0:2], scalar1=1.0 / 65536.0,
                                      scalar2=None, op0=ALU.mult), r=["pB:0"], w=["gst"])
        vop(lambda e: e.tensor_mul(out=c.gst[:, 2:3], in0=c.gst[:, 0:1], in1=c.gst[:, 0:1]),
            r=["gst"], w=["gst"])
        vop(lambda e: e.tensor_sub(out=c.gst[:, 3:4], in0=c.gst[:, 1:2], in1=c.gst[:, 2:3]),
            r=["gst"], w=["gst"])
        vop(lambda e: e.tensor_scalar(out=c.gst[:, 3:4], in0=c.gst[:, 3:4], scalar1=1e-6,
                                      scalar2=None, op0=ALU.add), r=["gst"], w=["gst"])
        sop(lambda e: e.activation(out=c.gst[:, 2:3], in_=c.gst[:, 3:4], func=AF.Sqrt),
            r=["gst"], w=["gst"])
        vop(lambda e: e.reciprocal(out=c.gst[:, 3:4], in_=c.gst[:, 2:3]), r=["gst"], w=["gst"])
        vop(lambda e: e.tensor_copy(out=c.gstb[:, 0:1], in_=c.gst[:, 3:4]), r=["gst"], w=["gstb"])
        vop(lambda e: e.tensor_copy(out=c.gstb[:, 1:2], in_=c.gst[:, 0:1]), r=["gst"], w=["gstb"])
        vop(lambda e: e.tensor_sub(out=c.gst[:, 1:2], in0=c.gst[:, 3:4], in1=c.gstb[:, 0:1]),
            r=["gst", "gstb"], w=["gst"])
        vop(lambda e: e.tensor_copy(out=c.gstb[:, 2:3], in_=c.gst[:, 1:2]), r=["gst"], w=["gstb"])
        vop(lambda e: e.tensor_sub(out=c.gst[:, 1:2], in0=c.gst[:, 0:1], in1=c.gstb[:, 1:2]),
            r=["gst", "gstb"], w=["gst"])
        vop(lambda e: e.tensor_copy(out=c.gstb[:, 3:4], in_=c.gst[:, 1:2]), r=["gst"], w=["gstb"])
        for cc in range(NCH):
            bank = 1 + (cc % 2)
            mm(pB[:, 512 * bank:512 * bank + 4], c.egnTb[:, 128 * cc:128 * (cc + 1)], c.gstb[:, 0:4],
               True, True, r=["egnT_b", "gstb"], w=[f"pB:{bank}"])
            vop(lambda e, cc=cc, bank=bank: e.tensor_copy(
                out=c.gnE[:, 4 * cc:4 * cc + 4], in_=pB[:, 512 * bank:512 * bank + 4]),
                r=[f"pB:{bank}"], w=["gnE"])
        for cc in range(NCH):
            vop(lambda e, cc=cc: e.tensor_add(out=c.gnA[:, cc:cc + 1], in0=c.gnE[:, 4 * cc:4 * cc + 1],
                                              in1=c.gnE[:, 4 * cc + 2:4 * cc + 3]), r=["gnE"], w=["gnA"])
            vop(lambda e, cc=cc: e.tensor_mul(out=c.gnA[:, cc:cc + 1], in0=c.gnA[:, cc:cc + 1],
                                              in1=c.gngs[:, cc:cc + 1]), r=["gnA", "gng_sb"], w=["gnA"])
            vop(lambda e, cc=cc: e.tensor_add(out=c.gnB[:, cc:cc + 1], in0=c.gnE[:, 4 * cc + 1:4 * cc + 2],
                                              in1=c.gnE[:, 4 * cc + 3:4 * cc + 4]), r=["gnE"], w=["gnB"])
            vop(lambda e, cc=cc: e.tensor_mul(out=c.gnB[:, cc:cc + 1], in0=c.gnB[:, cc:cc + 1],
                                              in1=c.gnA[:, cc:cc + 1]), r=["gnB", "gnA"], w=["gnB"])
            vop(lambda e, cc=cc: e.tensor_sub(out=c.gnB[:, cc:cc + 1], in0=c.gnbs[:, cc:cc + 1],
                                              in1=c.gnB[:, cc:cc + 1]), r=["gnb_sb", "gnB"], w=["gnB"])
        vop(lambda e: e.tensor_copy(out=c.gnAb[:, :], in_=c.gnA[:, :]), r=["gnA"], w=["gnAb"])
        vop(lambda e: e.tensor_copy(out=c.gnBb[:, :], in_=c.gnB[:, :]), r=["gnB"], w=["gnBb"])

    def l0_attn_bias():
        wq3 = dr["wqkv"].rearrange("p (n f) -> p n f", n=8)
        for os6 in range(6):
            gdma(lambda e, os6=os6: e.dma_start(
                out=c.ws[:, 0:4096].rearrange("p (n f) -> p n f", n=8),
                in_=wq3[:, :, 512 * os6:512 * (os6 + 1)]), r=(), w=WSK(0, 4096), tag="wb")
            for cc in range(NCH):
                mm(pA[0:1, 1536:2048], c.gnBb[:, cc:cc + 1], c.ws[:, 512 * cc:512 * (cc + 1)],
                   cc == 0, cc == 7, r=["gnBb"] + WSK(0, 4096), w=["pA:3"])
            vop(lambda e: e.tensor_copy(out=c.trow[0:1, 0:512], in_=pA[0:1, 1536:2048]),
                r=["pA:3"], w=["trow"])
            for q4 in range(4):
                och = os6 * 4 + q4
                tr(pB[:, 1536 + q4:1537 + q4], c.trow[0:1, 128 * q4:128 * (q4 + 1)],
                   c.idf[0:1, 0:1], r=["trow", "idf_sb"], w=["pB:3"])
            vop(lambda e, os6=os6: e.tensor_add(out=c.bcall[:, 4 * os6:4 * (os6 + 1)],
                                                in0=pB[:, 1536:1540],
                                                in1=c.bqkvc[:, 4 * os6:4 * (os6 + 1)]),
                r=["pB:3", "bqkv_sb"], w=["bcall"])
        for vch in range(8):
            tr(pA[0:1, 1536 + 128 * (vch % 4):1664 + 128 * (vch % 4)], c.bcall[:, 16 + vch:17 + vch],
               c.idf, r=["bcall", "idf_sb"], w=["pA:3"])
            vop(lambda e, vch=vch: e.tensor_copy(
                out=c.trow[0:1, 0:512][0:1, 128 * (vch % 4):128 * (vch % 4 + 1)],
                in_=pA[0:1, 1536 + 128 * (vch % 4):1664 + 128 * (vch % 4)]),
                r=["pA:3"], w=["trow"])
            if vch % 4 == 3:
                half = vch // 4
                for s2 in range(1):
                    mm(pB[:, 1024 + 512 * half:1024 + 512 * (half + 1)], c.onesf[0:1, 0:128],
                       c.trow[0:1, 0:512], True, True,
                       r=["onesf_sb", "trow"], w=pbk(1024 + 512 * half, 1024 + 512 * (half + 1)))
                vop(lambda e, half=half: e.tensor_copy(
                    out=c.botile[:, 512 * half:512 * (half + 1)],
                    in_=pB[:, 1024 + 512 * half:1024 + 512 * (half + 1)]),
                    r=pbk(1024 + 512 * half, 1024 + 512 * (half + 1)), w=["botile"])

    def l0_attn():
        l0_attn_bias()
        wq3 = dr["wqkv"].rearrange("p (n f) -> p n f", n=8)

        def emit_qk(p):
            qb, kb = 1024 * (p % 2), 2048 + 1024 * (p % 2)
            for base, osrc in ((qb, 128 * p), (kb, 1024 + 128 * p)):
                gdma(lambda e, base=base, osrc=osrc: e.dma_start(
                    out=c.ws[:, base:base + 1024].rearrange("p (n f) -> p n f", n=8),
                    in_=wq3[:, :, osrc:osrc + 128]),
                    r=(), w=WSK(base, base + 1024), tag="wqkv")
            for base in (qb, kb):
                w3 = c.ws[:, base:base + 1024].rearrange("p (n f) -> p n f", n=8)
                ab = c.gnAb[:, :].rearrange("p (n o) -> p n o", o=1).to_broadcast([128, 8, 128])
                vop(lambda e, w3=w3, ab=ab: e.tensor_mul(out=w3, in0=w3, in1=ab),
                    r=WSK(base, base + 1024) + ["gnAb"], w=WSK(base, base + 1024))
            for dstbase, wsb, bci in ((R_QP, qb, p), (R_KP, kb, 8 + p)):
                for s4 in range(4):
                    bank = s4 % 2
                    for cc in range(NCH):
                        mm(pA[:, 512 * bank:512 * (bank + 1)],
                           c.ws[:, wsb + 128 * cc:wsb + 128 * (cc + 1)],
                           c.xnT[:, 2048 * cc + 512 * s4:2048 * cc + 512 * (s4 + 1)],
                           cc == 0, cc == 7,
                           r=WSK(wsb, wsb + 1024) + xkk(2048 * cc + 512 * s4, 2048 * cc + 512 * (s4 + 1)),
                           w=[f"pA:{bank}"])
                    sop(lambda e, dstbase=dstbase, s4=s4, bank=bank, bci=bci: e.activation(
                        out=R[:, dstbase + 512 * s4:dstbase + 512 * (s4 + 1)],
                        in_=pA[:, 512 * bank:512 * (bank + 1)], func=AF.Identity,
                        bias=c.bcall[:, bci:bci + 1], scale=1.0),
                        r=[f"pA:{bank}", "bcall"], w=ak(dstbase + 512 * s4, dstbase + 512 * (s4 + 1)))

        def emit_vdma(p):
            vb = 4096 + 1024 * (p % 2)
            gdma(lambda e, p=p, vb=vb: e.dma_start(
                out=c.ws[:, vb:vb + 1024].rearrange("p (n f) -> p n f", n=8),
                in_=wq3[:, :, 2048 + 128 * p:2048 + 128 * (p + 1)]),
                r=(), w=WSK(vb, vb + 1024), tag="wv")
            w3 = c.ws[:, vb:vb + 1024].rearrange("p (n f) -> p n f", n=8)
            ab = c.gnAb[:, :].rearrange("p (n o) -> p n o", o=1).to_broadcast([128, 8, 128])
            vop(lambda e, w3=w3, ab=ab: e.tensor_mul(out=w3, in0=w3, in1=ab),
                r=WSK(vb, vb + 1024) + ["gnAb"], w=WSK(vb, vb + 1024))
            gop(lambda e: e.memset(R[:, R_VP:R_VP + 2080], 1.0), r=(), w=ak(R_VP, R_VP + 2080))

        def emit_v_tile(p, tt):
            vb = 4096 + 1024 * (p % 2)
            for cc in range(NCH):
                mm(pA[:, 1536:1664],
                   c.xnT[:, 2048 * cc + 128 * tt:2048 * cc + 128 * (tt + 1)],
                   c.ws[:, vb + 128 * cc:vb + 128 * (cc + 1)],
                   cc == 0, cc == 7,
                   r=WSK(vb, vb + 1024) + xkk(2048 * cc + 128 * tt, 2048 * cc + 128 * (tt + 1)),
                   w=["pA:3"])
            dst = R[:, R_VP + 130 * tt:R_VP + 130 * tt + 130].rearrange(
                "p (a b) -> p a b", a=2)[:, :, 0:64]
            src = pA[:, 1536:1664].rearrange("p (a b) -> p a b", a=2)
            vbt3 = c.botile[:, 128 * p:128 * (p + 1)].rearrange("p (a b) -> p a b", a=2)
            vop(lambda e, dst=dst, src=src, vbt3=vbt3: e.tensor_add(out=dst, in0=src, in1=vbt3),
                r=["pA:3", "botile"], w=ak(R_VP + 130 * tt, R_VP + 130 * tt + 130))

        def emit_mhat(p):
            for src, col in ((R_QP, 0), (R_KP, 1)):
                for half in range(2):
                    sop(lambda e, src=src, half=half: e.activation(
                        out=c.ET[:, 1024 * half:1024 * (half + 1)],
                        in_=R[:, src + 1024 * half:src + 1024 * (half + 1)], func=AF.Square),
                        r=ak(src + 1024 * half, src + 1024 * (half + 1)), w=[f"ET:{half}"], tag="mhat_sq")
                for s4 in range(4):
                    mm(pB[0:2, 512 * s4:512 * s4 + 512], c.h2b[:, :],
                       c.ET[:, 512 * s4:512 * (s4 + 1)], True, True,
                       r=["h2_b", "ET:0", "ET:1"], w=[f"pB:{s4}"], tag="mhat_mm")
                vop(lambda e, col=col: e.reduce_max(out=c.tmpc[0:2, col:col + 1], in_=pB[0:2, 0:2048],
                                                    axis=AX.X), r=pbk(0, 2048), w=["tmpc"], tag="mhat_red")
            vop(lambda e: e.tensor_mul(out=c.tmpc[0:2, 2:3], in0=c.tmpc[0:2, 0:1],
                                       in1=c.tmpc[0:2, 1:2]), r=["tmpc"], w=["tmpc"])
            sop(lambda e: e.activation(out=c.tmpc[0:2, 3:4], in_=c.tmpc[0:2, 2:3], func=AF.Sqrt),
                r=["tmpc"], w=["tmpc"])
            tr(pB[0:1, 1536:1538], c.tmpc[0:2, 3:4], c.idf[0:2, 0:2], r=["tmpc", "idf_sb"], w=["pB:3"])
            vop(lambda e: e.tensor_copy(out=c.trow[0:1, 424:426], in_=pB[0:1, 1536:1538]),
                r=["pB:3"], w=["trow"])
            for j in range(2):
                mm(pA[:, 1664 + j:1665 + j], c.onesf[0:1, 0:128], c.trow[0:1, 424 + j:425 + j],
                   True, True, r=["onesf_sb", "trow"], w=["pA:3"])
            vop(lambda e: e.tensor_scalar(out=c.mb2[:, 0:2], in0=pA[:, 1664:1666],
                                          scalar1=-0.125, scalar2=None, op0=ALU.mult),
                r=["pA:3"], w=["mb2"])

        def p2_okeys(qh):
            return pak(0, 1024) if qh == 0 else pak(1024, 2048)

        def p2_oview(qh):
            return pA[0:65, 0:1024] if qh == 0 else pA[0:65, 1024:2048]

        def emit_p2_st(p, j, qh, kt):
            r0 = 64 * j
            hb = kt % 2
            sv = pB[:, 1024 * hb:1024 * (hb + 1)]
            for s in range(2):
                mm(sv[:, 512 * s:512 * (s + 1)],
                   R[r0:r0 + 64, R_KP + 128 * kt:R_KP + 128 * (kt + 1)],
                   R[r0:r0 + 64, R_QP + 1024 * qh + 512 * s:R_QP + 1024 * qh + 512 * (s + 1)],
                   True, True,
                   r=ak(R_KP + 128 * kt, R_KP + 128 * (kt + 1)) + ak(R_QP + 1024 * qh + 512 * s, R_QP + 1024 * qh + 512 * (s + 1)),
                   w=pbk(1024 * hb + 512 * s, 1024 * hb + 512 * (s + 1)))
            sop(lambda e, sv=sv, hb=hb, j=j: e.activation(
                out=c.ET[:, 1024 * hb:1024 * (hb + 1)], in_=sv, func=AF.Exp,
                bias=c.mb2[:, j:j + 1], scale=0.125),
                r=pbk(1024 * hb, 1024 * (hb + 1)) + ["mb2"], w=[f"ET:{hb}"])

        def emit_p2_pv(p, j, qh, kt):
            hb = kt % 2
            ov = p2_oview(qh)
            for s in range(2):
                mm(ov[:, 512 * s:512 * (s + 1)],
                   R[:, R_VP + 130 * kt + 65 * j:R_VP + 130 * kt + 65 * (j + 1)],
                   c.ET[:, 1024 * hb + 512 * s:1024 * hb + 512 * (s + 1)],
                   kt == 0, kt == 15,
                   r=[f"ET:{hb}"] + ak(R_VP + 130 * kt, R_VP + 130 * (kt + 1)),
                   w=p2_okeys(qh))

        def emit_p2_norm(p, j, qh):
            okeys = p2_okeys(qh)
            ov = p2_oview(qh)
            vop(lambda e, ov=ov: e.tensor_copy(out=c.scr1[64:65, 0:1024], in_=ov[64:65, 0:1024]),
                r=okeys, w=[SCR1[0]])
            vop(lambda e: e.reciprocal(out=c.scr1[64:65, 1024:2048], in_=c.scr1[64:65, 0:1024]),
                r=[SCR1[0]], w=[SCR1[1]])
            for s in range(2):
                mm(pB[0:64, 0:512], c.onesf[64:65, 0:64],
                   c.scr1[64:65, 1024 + 512 * s:1024 + 512 * (s + 1)], True, True,
                   r=["onesf_sb", SCR1[1]], w=["pB:0"])
                vop(lambda e, s=s: e.tensor_copy(out=c.scr1[0:64, 512 * s:512 * (s + 1)],
                                                 in_=pB[0:64, 0:512]),
                    r=["pB:0"], w=[SCR1[0]])
            dst0 = R_OT + 2048 * p + 1024 * qh
            if j == 0:
                vop(lambda e, ov=ov, dst0=dst0: e.tensor_mul(
                    out=R[0:64, dst0:dst0 + 1024], in0=ov[0:64, 0:1024],
                    in1=c.scr1[0:64, 0:1024]),
                    r=okeys + [SCR1[0]], w=ak(dst0, dst0 + 1024))
            else:
                vop(lambda e, ov=ov: e.tensor_mul(
                    out=c.ET[0:64, 0:1024], in0=ov[0:64, 0:1024],
                    in1=c.scr1[0:64, 0:1024]),
                    r=okeys + [SCR1[0]], w=["ET:0"])
                for s in range(2):
                    mm(pB[:, 0:512], c.s64b[:, :], c.ET[0:64, 512 * s:512 * (s + 1)],
                       True, True, r=["s64_b", "ET:0"], w=["pB:0"])
                    vop(lambda e, s=s, dst0=dst0: e.tensor_copy(
                        out=R[64:128, dst0 + 512 * s:dst0 + 512 * (s + 1)],
                        in_=pB[64:128, 0:512]),
                        r=["pB:0"], w=ak(dst0 + 512 * s, dst0 + 512 * (s + 1)))

        for p in range(int(os.environ.get("K_NPAIRS", "8"))):
            emit_qk(p)
            emit_vdma(p)
            emit_mhat(p)
            for u in range(16):
                emit_v_tile(p, u)
            for j in range(2):
                for qh in range(2):
                    for kt in range(NT):
                        emit_p2_st(p, j, qh, kt)
                        if kt > 0:
                            emit_p2_pv(p, j, qh, kt - 1)
                    emit_p2_pv(p, j, qh, NT - 1)
                    emit_p2_norm(p, j, qh)

        sdma(lambda e: e.dma_start(out=c.scr1[0:1, 0:1024], in_=dr["bo"][:, :]), r=(), w=[SCR1[0]])
        for s in range(2):
            mm(pB[:, 512 * s:512 * (s + 1)], c.onesf[0:1, 0:128],
               c.scr1[0:1, 512 * s:512 * (s + 1)], True, True,
               r=["onesf_sb", SCR1[0]], w=pbk(512 * s, 512 * (s + 1)))
        vop(lambda e: e.tensor_copy(out=c.botile[:, :], in_=pB[:, 0:1024]), r=pbk(0, 1024), w=["botile"])
        wo3 = dr["wol"].rearrange("p (n f) -> p n f", n=8)
        for os_ in range(2):
            gdma(lambda e, os_=os_: e.dma_start(
                out=c.ws[:, 0:4096].rearrange("p (n f) -> p n f", n=8),
                in_=wo3[:, :, 512 * os_:512 * (os_ + 1)]),
                r=(), w=WSK(0, 4096), tag="wo")
            for tt in range(NT):
                bank = tt % 4
                for pch in range(8):
                    mm(pA[:, 512 * bank:512 * (bank + 1)],
                       R[:, R_OT + 2048 * pch + 128 * tt:R_OT + 2048 * pch + 128 * (tt + 1)],
                       c.ws[:, 512 * pch:512 * pch + 512],
                       pch == 0, pch == 7,
                       r=ak(R_OT + 2048 * pch + 128 * tt, R_OT + 2048 * pch + 128 * (tt + 1)) + WSK(512 * pch, 512 * pch + 512),
                       w=[f"pA:{bank}"])
                xs = c.x[:, 1024 * tt + 512 * os_:1024 * tt + 512 * (os_ + 1)]
                vop(lambda e, xs=xs, bank=bank: e.tensor_add(out=xs, in0=xs,
                                                             in1=pA[:, 512 * bank:512 * (bank + 1)]),
                    r=[f"pA:{bank}", f"x:{tt}"], w=[f"x:{tt}"])
                vop(lambda e, xs=xs, os_=os_: e.tensor_add(out=xs, in0=xs,
                                                           in1=c.botile[:, 512 * os_:512 * (os_ + 1)]),
                    r=["botile", f"x:{tt}"], w=[f"x:{tt}"])

    def l0_ffn():
        gdma(lambda e: e.dma_start(out=R[0:16, 28672:30720], in_=dr["sel16"][:, :]),
             r=(), w=ak(28672, 30720), tag="sel16")
        for tt in range(NT):
            bank = tt % 2
            pv = pA[:, 512 * bank:512 * bank + 16]
            for cc in range(NCH):
                mm(pv, c.xnT[:, 2048 * cc + 128 * tt:2048 * cc + 128 * (tt + 1)],
                   c.rwts[:, 16 * cc:16 * (cc + 1)], cc == 0, cc == 7,
                   r=["rwt_b"] + xkk(2048 * cc + 128 * tt, 2048 * cc + 128 * (tt + 1)),
                   w=[f"pA:{bank}"])
            vop(lambda e, pv=pv: e.reduce_max(out=c.tmpc[:, 2:3], in_=pv, axis=AX.X),
                r=[f"pA:{bank}"], w=["tmpc"])
            vop(lambda e: e.tensor_scalar(out=c.tmpc[:, 2:3], in0=c.tmpc[:, 2:3], scalar1=-1.0,
                                          scalar2=None, op0=ALU.mult), r=["tmpc"], w=["tmpc"])
            sop(lambda e, pv=pv: e.activation(out=c.scr0[:, 0:16], in_=pv, func=AF.Exp,
                                              bias=c.tmpc[:, 2:3], scale=1.0),
                r=[f"pA:{bank}", "tmpc"], w=["scr0"])
            vop(lambda e: e.reduce_sum(out=c.tmpc[:, 3:4], in_=c.scr0[:, 0:16], axis=AX.X),
                r=["scr0"], w=["tmpc"])
            vop(lambda e: e.reciprocal(out=c.tmpc[:, 3:4], in_=c.tmpc[:, 3:4]), r=["tmpc"], w=["tmpc"])
            vop(lambda e: e.max(out=c.scr0[:, 16:24], in_=c.scr0[:, 0:16]), r=["scr0"], w=["scr0"])
            vop(lambda e: e.tensor_scalar(out=c.scr0[:, 24:40], in0=c.scr0[:, 0:16],
                                          scalar1=c.scr0[:, 19:20], scalar2=0.0,
                                          op0=ALU.subtract, op1=ALU.is_ge), r=["scr0"], w=["scr0"])
            vop(lambda e: e.scalar_tensor_tensor(out=c.scr0[:, 40:56], in0=c.scr0[:, 24:40],
                                                 scalar=c.tmpc[:, 3:4], in1=c.scr0[:, 0:16],
                                                 op0=ALU.mult, op1=ALU.mult),
                r=["scr0", "tmpc"], w=["scr0"])
            tr(pB[0:16, 1024 + 128 * (tt % 2):1024 + 128 * (tt % 2) + 128], c.scr0[:, 40:56], c.idf,
               r=["scr0", "idf_sb"], w=["pB:2"])
            vop(lambda e, tt=tt: e.tensor_copy(
                out=R[0:16, R_GATET + 128 * tt:R_GATET + 128 * (tt + 1)],
                in_=pB[0:16, 1024 + 128 * (tt % 2):1024 + 128 * (tt % 2) + 128]),
                r=["pB:2"], w=ak(R_GATET + 128 * tt, R_GATET + 128 * (tt + 1)))

        EXP = [("sut", "sdt", 0, False), ("sut", "sdt", 1, False)] + \
              [("rut", "rdt", r, True) for r in range(16)]
        for g in range(9):
            for gi in range(2):
                usrc, dsrc, idx, routed = EXP[2 * g + gi]
                ub = R_U0 if gi == 0 else R_U1
                db = R_D0 if gi == 0 else R_D1
                gdma(lambda e, usrc=usrc, idx=idx, ub=ub: e.dma_start(
                    out=R[:, ub:ub + 4096], in_=dr[usrc][idx]), r=(), w=ak(ub, ub + 4096), tag="U")
                gdma(lambda e, dsrc=dsrc, idx=idx, db=db: e.dma_start(
                    out=R[:, db:db + 4096], in_=dr[dsrc][idx]), r=(), w=ak(db, db + 4096), tag="D")
            for th in range(2):
                for gi in range(2):
                    usrc, dsrc, idx, routed = EXP[2 * g + gi]
                    ub = R_U0 if gi == 0 else R_U1
                    ab = R_A0 if gi == 0 else R_A1
                    for ec in range(4):
                        for s in range(2):
                            bank = (ec * 2 + s) % 2
                            for cc in range(NCH):
                                mm(pA[:, 512 * bank:512 * (bank + 1)],
                                   R[:, ub + 512 * cc + 128 * ec:ub + 512 * cc + 128 * (ec + 1)],
                                   c.xnT[:, 2048 * cc + 1024 * th + 512 * s:2048 * cc + 1024 * th + 512 * (s + 1)],
                                   cc == 0, cc == 7,
                                   r=ak(ub, ub + 4096) + xkk(2048 * cc + 1024 * th + 512 * s, 2048 * cc + 1024 * th + 512 * (s + 1)),
                                   w=[f"pA:{bank}"])
                            sop(lambda e, ab=ab, ec=ec, s=s, bank=bank: e.activation(
                                out=R[:, ab + 1024 * ec + 512 * s:ab + 1024 * ec + 512 * (s + 1)],
                                in_=pA[:, 512 * bank:512 * (bank + 1)], func=AF.Gelu),
                                r=[f"pA:{bank}"], w=ak(ab + 1024 * ec + 512 * s, ab + 1024 * ec + 512 * (s + 1)))
                    if routed:
                        for s in range(2):
                            mm(pB[:, 512 * s:512 * (s + 1)], R[0:16, 28672 + 128 * idx:28672 + 128 * (idx + 1)],
                               R[0:16, R_GATET + 1024 * th + 512 * s:R_GATET + 1024 * th + 512 * (s + 1)],
                               True, True,
                               r=ak(28672, 30720) + ak(R_GATET + 1024 * th + 512 * s, R_GATET + 1024 * th + 512 * (s + 1)),
                               w=[f"pB:{s}"])
                        for ec in range(4):
                            a3 = R[:, ab + 1024 * ec:ab + 1024 * (ec + 1)]
                            vop(lambda e, a3=a3: e.tensor_mul(out=a3, in0=a3, in1=pB[:, 0:1024]),
                                r=["pB:0", "pB:1"] + ak(ab + 1024 * ec, ab + 1024 * (ec + 1)),
                                w=ak(ab + 1024 * ec, ab + 1024 * (ec + 1)))
                for tt8 in range(8):
                    for os_ in range(2):
                        bank = 2 + (tt8 * 2 + os_) % 2
                        nmm = 0
                        for gi in range(2):
                            ab = R_A0 if gi == 0 else R_A1
                            db = R_D0 if gi == 0 else R_D1
                            for ec in range(4):
                                mm(pB[:, 512 * bank:512 * (bank + 1)],
                                   R[:, ab + 1024 * ec + 128 * tt8:ab + 1024 * ec + 128 * (tt8 + 1)],
                                   R[:, db + 1024 * ec + 512 * os_:db + 1024 * ec + 512 * (os_ + 1)],
                                   nmm == 0, nmm == 7,
                                   r=ak(ab + 1024 * ec + 128 * tt8, ab + 1024 * ec + 128 * (tt8 + 1)) + ak(db, db + 4096),
                                   w=[f"pB:{bank}"])
                                nmm += 1
                        tglob = th * 8 + tt8
                        xs = c.x[:, 1024 * tglob + 512 * os_:1024 * tglob + 512 * (os_ + 1)]
                        vop(lambda e, xs=xs, bank=bank: e.tensor_add(
                            out=xs, in0=xs, in1=pB[:, 512 * bank:512 * (bank + 1)]),
                            r=[f"pB:{bank}", f"x:{tglob}"], w=[f"x:{tglob}"])

    def l1_attn():
        def take_fill(n=1):
            for _ in range(n):
                if c.fill:
                    c.fill.pop(0)()
        sdma(lambda e: e.dma_start(out=c.scr1[0:1, 0:1024], in_=dr["bo1"][:, :]), r=(), w=[SCR1[0]])
        for s in range(2):
            mm(pB[:, 512 * s:512 * (s + 1)], c.onesf[0:1, 0:128],
               c.scr1[0:1, 512 * s:512 * (s + 1)], True, True,
               r=["onesf_sb", SCR1[0]], w=pbk(512 * s, 512 * (s + 1)))
        vop(lambda e: e.tensor_copy(out=c.botile[:, :], in_=pB[:, 0:1024]), r=pbk(0, 1024), w=["botile"])
        wv3 = dr["wv1"].rearrange("p (n f) -> p n f", n=8)
        for ch in range(NCH):
            wb = 4096 + 1024 * (ch % 2)
            gdma(lambda e, ch=ch, wb=wb: e.dma_start(
                out=c.ws[:, wb:wb + 1024].rearrange("p (n f) -> p n f", n=8),
                in_=wv3[:, :, 128 * ch:128 * (ch + 1)]), r=(), w=WSK(wb, wb + 1024), tag="wv1")
            for s4 in range(4):
                bank = s4 % 2
                for cc in range(NCH):
                    mm(pB[:, 512 * bank:512 * (bank + 1)],
                       c.ws[:, wb + 128 * cc:wb + 128 * (cc + 1)],
                       c.xnT[:, 2048 * cc + 512 * s4:2048 * cc + 512 * (s4 + 1)],
                       cc == 0, cc == 7,
                       r=WSK(wb, wb + 1024) + xkk(2048 * cc + 512 * s4, 2048 * cc + 512 * (s4 + 1)),
                       w=[f"pB:{bank}"])
                sop(lambda e, ch=ch, s4=s4, bank=bank: e.activation(
                    out=R[:, R_VBUF + 2048 * ch + 512 * s4:R_VBUF + 2048 * ch + 512 * (s4 + 1)],
                    in_=pB[:, 512 * bank:512 * (bank + 1)], func=AF.Identity,
                    bias=c.bv1s[:, ch:ch + 1], scale=1.0),
                    r=[f"pB:{bank}", "bv1_sb"],
                    w=ak(R_VBUF + 2048 * ch + 512 * s4, R_VBUF + 2048 * ch + 512 * (s4 + 1)))
                take_fill(1)
        kcv = c.scr1[:, :].bitcast(BF16)
        wq3l = dr["wq1"].rearrange("p (n f) -> p n f", n=8)
        wk3l = dr["wk1"].rearrange("p (n f) -> p n f", n=8)
        for ch in range(NCH):
            wbq = 1024 * (ch % 2)
            wbk = 2048 + 1024 * (ch % 2)
            gdma(lambda e, ch=ch, wbq=wbq: e.dma_start(
                out=c.ws[:, wbq:wbq + 1024].rearrange("p (n f) -> p n f", n=8),
                in_=wq3l[:, :, 128 * ch:128 * (ch + 1)]), r=(), w=WSK(wbq, wbq + 1024), tag="wq1")
            gdma(lambda e, ch=ch, wbk=wbk: e.dma_start(
                out=c.ws[:, wbk:wbk + 1024].rearrange("p (n f) -> p n f", n=8),
                in_=wk3l[:, :, 128 * ch:128 * (ch + 1)]), r=(), w=WSK(wbk, wbk + 1024), tag="wk1")
            for (wb, bsb, bkey, dst, dkey) in (
                    (wbq, c.bq1s, "bq1_sb", c.ET, ["ET:0", "ET:1"]),
                    (wbk, c.bk1s, "bk1_sb", kcv, SCR1)):
                for s4 in range(4):
                    bank = s4 % 2
                    for cc in range(NCH):
                        mm(pB[:, 512 * bank:512 * (bank + 1)],
                           c.ws[:, wb + 128 * cc:wb + 128 * (cc + 1)],
                           c.xnT[:, 2048 * cc + 512 * s4:2048 * cc + 512 * (s4 + 1)],
                           cc == 0, cc == 7,
                           r=WSK(wb, wb + 1024) + xkk(2048 * cc + 512 * s4, 2048 * cc + 512 * (s4 + 1)),
                           w=[f"pB:{bank}"])
                    sop(lambda e, ch=ch, bank=bank, s4=s4, bsb=bsb: e.activation(
                        out=c.scr0[:, 512 * (s4 % 2):512 * (s4 % 2) + 512],
                        in_=pB[:, 512 * bank:512 * (bank + 1)], func=AF.Square,
                        bias=bsb[:, ch:ch + 1], scale=1.0),
                        r=[f"pB:{bank}", bkey], w=["scr0"])
                    sop(lambda e, s4=s4, dst=dst: e.activation(
                        out=dst[:, 512 * s4:512 * (s4 + 1)],
                        in_=c.scr0[:, 512 * (s4 % 2):512 * (s4 % 2) + 512], func=AF.Exp, scale=-0.5),
                        r=["scr0"], w=dkey)
            vop(lambda e: e.tensor_mul(out=c.ET[:, :], in0=c.ET[:, :], in1=kcv[:, 0:2048]),
                r=["ET:0", "ET:1"] + SCR1, w=["ET:0", "ET:1"])
            for s4 in range(4):
                mm(pA[0:16, 512 * s4:512 * (s4 + 1)], c.eheadb[:, 16 * ch:16 * (ch + 1)],
                   c.ET[:, 512 * s4:512 * (s4 + 1)], ch == 0, ch == 7,
                   r=["ehead_b", "ET:0", "ET:1"], w=[f"pA:{s4}"])
        vop(lambda e: e.tensor_copy(out=c.ET[0:16, 0:2048], in_=pA[0:16, 0:2048]),
            r=pak(0, 2048), w=["ET:0", "ET:1"])
        for ch in range(NCH):
            for s4 in range(4):
                bank = s4 % 2
                mm(pB[:, 512 * bank:512 * (bank + 1)], c.ehead2b[:, 128 * ch:128 * (ch + 1)],
                   c.ET[0:16, 512 * s4:512 * (s4 + 1)], True, True,
                   r=["ehead2_b", "ET:0", "ET:1"], w=[f"pB:{bank}"])
                vv = R[:, R_VBUF + 2048 * ch + 512 * s4:R_VBUF + 2048 * ch + 512 * (s4 + 1)]
                vk = ak(R_VBUF + 2048 * ch + 512 * s4, R_VBUF + 2048 * ch + 512 * (s4 + 1))
                vop(lambda e, vv=vv, bank=bank: e.tensor_mul(
                    out=vv, in0=vv, in1=pB[:, 512 * bank:512 * (bank + 1)]),
                    r=[f"pB:{bank}"] + vk, w=vk)
                take_fill(1)
        wo3l = dr["wo1"].rearrange("p (n f) -> p n f", n=8)
        for os_ in range(2):
            gdma(lambda e, os_=os_: e.dma_start(
                out=c.ws[:, 0:4096].rearrange("p (n f) -> p n f", n=8),
                in_=wo3l[:, :, 512 * os_:512 * (os_ + 1)]), r=(), w=WSK(0, 4096), tag="wo1")
            for tt in range(NT):
                bank = tt % 4
                for ch in range(NCH):
                    mm(pA[:, 512 * bank:512 * (bank + 1)],
                       R[:, R_VBUF + 2048 * ch + 128 * tt:R_VBUF + 2048 * ch + 128 * (tt + 1)],
                       c.ws[:, 512 * ch:512 * ch + 512], ch == 0, ch == 7,
                       r=ak(R_VBUF + 2048 * ch + 128 * tt, R_VBUF + 2048 * ch + 128 * (tt + 1)) + WSK(512 * ch, 512 * ch + 512),
                       w=[f"pA:{bank}"])
                xs = c.x[:, 1024 * tt + 512 * os_:1024 * tt + 512 * (os_ + 1)]
                vop(lambda e, xs=xs, bank=bank: e.tensor_add(out=xs, in0=xs,
                                                             in1=pA[:, 512 * bank:512 * (bank + 1)]),
                    r=[f"pA:{bank}", f"x:{tt}"], w=[f"x:{tt}"])
                vop(lambda e, xs=xs, os_=os_: e.tensor_add(out=xs, in0=xs,
                                                           in1=c.botile[:, 512 * os_:512 * (os_ + 1)]),
                    r=["botile", f"x:{tt}"], w=[f"x:{tt}"])

    def l1_route_scores():
        for tt in range(NT):
            bank = tt % 2
            pv = pA[:, 512 * bank:512 * bank + 8]
            for cc in range(NCH):
                mm(pv, c.xnT[:, 2048 * cc + 128 * tt:2048 * cc + 128 * (tt + 1)],
                   c.gwls[:, 8 * cc:8 * (cc + 1)], cc == 0, cc == 7,
                   r=["gwl_b"] + xkk(2048 * cc + 128 * tt, 2048 * cc + 128 * (tt + 1)),
                   w=[f"pA:{bank}"])
            vop(lambda e, pv=pv: e.reduce_max(out=c.tmpc[:, 2:3], in_=pv, axis=AX.X),
                r=[f"pA:{bank}"], w=["tmpc"])
            vop(lambda e: e.tensor_scalar(out=c.tmpc[:, 2:3], in0=c.tmpc[:, 2:3], scalar1=-1.0,
                                          scalar2=None, op0=ALU.mult), r=["tmpc"], w=["tmpc"])
            sop(lambda e, pv=pv: e.activation(out=c.scr0[:, 0:8], in_=pv, func=AF.Exp,
                                              bias=c.tmpc[:, 2:3], scale=1.0),
                r=[f"pA:{bank}", "tmpc"], w=["scr0"])
            vop(lambda e: e.reduce_sum(out=c.tmpc[:, 3:4], in_=c.scr0[:, 0:8], axis=AX.X),
                r=["scr0"], w=["tmpc"])
            vop(lambda e: e.reciprocal(out=c.tmpc[:, 3:4], in_=c.tmpc[:, 3:4]), r=["tmpc"], w=["tmpc"])
            vop(lambda e, tt=tt: e.tensor_scalar(out=c.probs[:, 8 * tt:8 * (tt + 1)],
                                                 in0=c.scr0[:, 0:8], scalar1=c.tmpc[:, 3:4],
                                                 scalar2=None, op0=ALU.mult),
                r=["scr0", "tmpc"], w=["probs"])
        vop(lambda e: e.tensor_copy(out=c.probsb[:, :], in_=c.probs[:, :]), r=["probs"], w=["probsb"])

    def l1_route_fill_steps():
        steps = []

        def mask_from(colidx, final):
            tr(pB[0:1, 1536:1544], c.bis[:, colidx:colidx + 1], c.idf[0:8, 0:8], r=["bis", "idf_sb"], w=["pB:3"])
            vop(lambda e: e.tensor_copy(out=c.trow[0:1, 0:8], in_=pB[0:1, 1536:1544]),
                r=["pB:3"], w=["trow"])
            vop(lambda e: e.tensor_copy(
                out=c.trow[0:1, 128:256].rearrange("a (o f) -> a o f", o=16),
                in_=c.trow[0:1, 0:8].rearrange("a (o f) -> a o f", o=1).to_broadcast([1, 16, 8])),
                r=["trow"], w=["trow"])
            mm(pA[:, 1536:1664], c.onesf[0:1, 0:128], c.trow[0:1, 128:256], True, True,
               r=["onesf_sb", "trow"], w=["pA:3"])
            vop(lambda e: e.tensor_tensor(out=c.mask[:, :], in0=c.probs[:, :],
                                          in1=pA[:, 1536:1664], op=ALU.is_ge),
                r=["probs", "pA:3"], w=["mask"])
            if final:
                return
            mm(pB[:, 1664:1665], c.mask[:, :], c.onesb[:, 0:1], True, True,
               r=["mask", "onesf_b"], w=["pB:3"])
            vop(lambda e: e.tensor_copy(out=c.cnt1b[:, :], in_=pB[:, 1664:1665]),
                r=["pB:3"], w=["cnt1b"])
            mm(pA[0:8, 1664:1665], c.sel8b[:, :], c.cnt1b[:, :], True, True,
               r=["sel8_b", "cnt1b"], w=["pA:3"])
            vop(lambda e: e.tensor_scalar(out=c.bis[:, 3:4], in0=pA[0:8, 1664:1665],
                                          scalar1=511.5, scalar2=None, op0=ALU.is_ge),
                r=["pA:3"], w=["bis"])
            vop(lambda e: e.tensor_sub(out=c.bis[:, 4:5], in0=c.bis[:, 2:3], in1=c.bis[:, 0:1]),
                r=["bis"], w=["bis"])
            vop(lambda e: e.tensor_sub(out=c.bis[:, 5:6], in0=c.bis[:, 1:2], in1=c.bis[:, 2:3]),
                r=["bis"], w=["bis"])
            vop(lambda e: e.scalar_tensor_tensor(out=c.bis[:, 0:1], in0=c.bis[:, 4:5],
                                                 scalar=c.bis[:, 3:4], in1=c.bis[:, 0:1],
                                                 op0=ALU.mult, op1=ALU.add), r=["bis"], w=["bis"])
            vop(lambda e: e.scalar_tensor_tensor(out=c.bis[:, 1:2], in0=c.bis[:, 5:6],
                                                 scalar=c.bis[:, 3:4], in1=c.bis[:, 2:3],
                                                 op0=ALU.mult, op1=ALU.add), r=["bis"], w=["bis"])

        def init():
            vop(lambda e: e.memset(c.bis[:, 0:1], 0.0), r=(), w=["bis"])
            vop(lambda e: e.memset(c.bis[:, 1:2], 1.0), r=(), w=["bis"])
        steps.append(init)

        def one_iter():
            vop(lambda e: e.tensor_add(out=c.bis[:, 2:3], in0=c.bis[:, 0:1], in1=c.bis[:, 1:2]),
                r=["bis"], w=["bis"])
            vop(lambda e: e.tensor_scalar(out=c.bis[:, 2:3], in0=c.bis[:, 2:3], scalar1=0.5,
                                          scalar2=None, op0=ALU.mult), r=["bis"], w=["bis"])
            mask_from(2, False)
        for it in range(30):
            steps.append(one_iter)

        def final_mask():
            mask_from(0, True)
            vop(lambda e: e.tensor_copy(out=c.maskf[:, :], in_=c.mask[:, :]), r=["mask"], w=["maskf"])
        steps.append(final_mask)

        def cum_tt(tt):
            bank = tt % 2
            mm(pB[:, 512 * bank:512 * bank + 8], c.trilb[:, :], c.mask[:, 8 * tt:8 * (tt + 1)],
               True, True, r=["tril_b", "mask"], w=[f"pB:{bank}"])
            vop(lambda e, tt=tt, bank=bank: e.tensor_copy(
                out=c.cum[:, 8 * tt:8 * (tt + 1)], in_=pB[:, 512 * bank:512 * bank + 8]),
                r=[f"pB:{bank}"], w=["cum"])
            mm(pB[0:1, 1024 + 8 * tt:1024 + 8 * (tt + 1)], c.onesb[:, 0:1], c.mask[:, 8 * tt:8 * (tt + 1)],
               True, True, r=["onesf_b", "mask"], w=["pB:2"])
            vop(lambda e, tt=tt: e.tensor_copy(out=c.totrow[0:1, 8 * tt:8 * (tt + 1)],
                                               in_=pB[0:1, 1024 + 8 * tt:1024 + 8 * (tt + 1)]),
                r=["pB:2"], w=["totrow"])
        for tt in range(NT):
            steps.append(lambda tt=tt: cum_tt(tt))

        def offs():
            tr(pB[:, 1536:1537], c.totrow[0:1, 0:128], c.idf[0:1, 0:1], r=["totrow", "idf_sb"], w=["pB:3"])
            vop(lambda e: e.tensor_copy(out=c.totb[:, :], in_=pB[:, 1536:1537]), r=["pB:3"], w=["totb"])
            mm(pB[:, 1537:1538], c.trieb[:, :], c.totb[:, :], True, True,
               r=["trie_b", "totb"], w=["pB:3"])
            vop(lambda e: e.tensor_copy(out=c.tmpc[:, 1:2], in_=pB[:, 1537:1538]), r=["pB:3"], w=["tmpc"])
            tr(pB[0:1, 1544:1672], c.tmpc[:, 1:2], c.idf, r=["tmpc", "idf_sb"], w=["pB:3"])
            vop(lambda e: e.tensor_copy(out=c.trow[0:1, 256:384], in_=pB[0:1, 1544:1672]),
                r=["pB:3"], w=["trow"])
            mm(pA[:, 1536:1664], c.onesf[0:1, 0:128], c.trow[0:1, 256:384], True, True,
               r=["onesf_sb", "trow"], w=["pA:3"])
            vop(lambda e: e.tensor_copy(out=c.offsb[:, :], in_=pA[:, 1536:1664]), r=["pA:3"], w=["offsb"])
        steps.append(offs)
        return steps

    def l1_ffn():
        for st_ in c.fill:
            st_()
        c.fill = []
        for e_ in range(8):
            for tt in range(NT):
                vop(lambda e, e_=e_, tt=tt: e.tensor_scalar(
                    out=c.slots[:, tt:tt + 1], in0=c.cum[:, 8 * tt + e_:8 * tt + e_ + 1],
                    scalar1=c.offsb[:, 8 * tt + e_:8 * tt + e_ + 1], scalar2=32767.0,
                    op0=ALU.add, op1=ALU.add), r=["cum", "offsb"], w=["slots"])
                vop(lambda e, e_=e_, tt=tt: e.tensor_scalar(
                    out=c.slots[:, tt:tt + 1], in0=c.slots[:, tt:tt + 1],
                    scalar1=c.maskf[:, 8 * tt + e_:8 * tt + e_ + 1], scalar2=32768.0,
                    op0=ALU.mult, op1=ALU.subtract), r=["slots", "maskf"], w=["slots"])
            for tt in range(NT):
                vop(lambda e, tt=tt: e.tensor_scalar(
                    out=R[:, R_SEL + 512 * tt:R_SEL + 512 * (tt + 1)], in0=c.iota512[:, :],
                    scalar1=c.slots[:, tt:tt + 1], scalar2=0.0,
                    op0=ALU.subtract, op1=ALU.is_equal),
                    r=["iota512_sb", "slots"], w=ak(R_SEL + 512 * tt, R_SEL + 512 * (tt + 1)))
            for cc in range(NCH):
                bank = cc % 2
                for tt in range(NT):
                    mm(pA[:, 512 * bank:512 * (bank + 1)],
                       R[:, R_XNTOK + 1024 * tt + 128 * cc:R_XNTOK + 1024 * tt + 128 * (cc + 1)],
                       R[:, R_SEL + 512 * tt:R_SEL + 512 * (tt + 1)],
                       tt == 0, tt == 15,
                       r=ak(R_XNTOK + 1024 * tt + 128 * cc, R_XNTOK + 1024 * tt + 128 * (cc + 1)) + ak(R_SEL + 512 * tt, R_SEL + 512 * (tt + 1)),
                       w=[f"pA:{bank}"])
                vop(lambda e, cc=cc, bank=bank: e.tensor_copy(
                    out=R[:, R_XIN + 512 * cc:R_XIN + 512 * (cc + 1)],
                    in_=pA[:, 512 * bank:512 * (bank + 1)]),
                    r=[f"pA:{bank}"], w=ak(R_XIN + 512 * cc, R_XIN + 512 * (cc + 1)))
            for tt in range(NT):
                mm(pB[0:1, 1024:1536], c.probsb[:, 8 * tt + e_:8 * tt + e_ + 1],
                   R[:, R_SEL + 512 * tt:R_SEL + 512 * (tt + 1)], tt == 0, tt == 15,
                   r=["probsb"] + ak(R_SEL + 512 * tt, R_SEL + 512 * (tt + 1)), w=["pB:2"])
            vop(lambda e: e.tensor_copy(out=c.trow[0:1, 0:512], in_=pB[0:1, 1024:1536]),
                r=["pB:2"], w=["trow"])
            for sc in range(4):
                tr(pB[:, 1536 + sc:1537 + sc], c.trow[0:1, 128 * sc:128 * (sc + 1)], c.idf[0:1, 0:1],
                   r=["trow", "idf_sb"], w=["pB:3"])
            vop(lambda e: e.tensor_copy(out=c.pcols[:, 0:4], in_=pB[:, 1536:1540]),
                r=["pB:3"], w=["pcols_sb"])
            w13 = dr["w1l"][e_].rearrange("p (n f) -> p n f", n=8)
            for hg in range(16):
                wb = 2048 * (hg % 3)
                gdma(lambda e, hg=hg, wb=wb: e.dma_start(
                    out=c.ws[:, wb:wb + 2048].rearrange("p (n f) -> p n f", n=8),
                    in_=w13[:, :, 256 * hg:256 * (hg + 1)]), r=(), w=WSK(wb, wb + 2048), tag="w1")
                for hj in range(2):
                    hc = 2 * hg + hj
                    bank = hc % 2
                    for cc in range(NCH):
                        mm(pA[:, 512 * bank:512 * (bank + 1)],
                           c.ws[:, wb + 256 * cc + 128 * hj:wb + 256 * cc + 128 * (hj + 1)],
                           R[:, R_XIN + 512 * cc:R_XIN + 512 * (cc + 1)],
                           cc == 0, cc == 7,
                           r=WSK(wb, wb + 2048) + ak(R_XIN + 512 * cc, R_XIN + 512 * (cc + 1)),
                           w=[f"pA:{bank}"])
                    sop(lambda e, hc=hc, bank=bank: e.activation(
                        out=c.xnT[:, 512 * hc:512 * (hc + 1)],
                        in_=pA[:, 512 * bank:512 * (bank + 1)], func=AF.Gelu),
                        r=[f"pA:{bank}"], w=xkk(512 * hc, 512 * (hc + 1)))
            w23 = dr["w2l"][e_].rearrange("p (n f) -> p n f", n=32)
            for cs in range(2):
                for hg in range(8):
                    wb = 2048 * (hg % 3)
                    gdma(lambda e, hg=hg, cs=cs, wb=wb: e.dma_start(
                        out=c.ws[:, wb:wb + 2048].rearrange("p (n f) -> p n f", n=4),
                        in_=w23[:, 4 * hg:4 * (hg + 1), 512 * cs:512 * (cs + 1)]),
                        r=(), w=WSK(wb, wb + 2048), tag="w2")
                    for hj in range(4):
                        hc = 4 * hg + hj
                        for st in range(4):
                            mm(pB[:, 512 * st:512 * (st + 1)],
                               c.xnT[:, 512 * hc + 128 * st:512 * hc + 128 * (st + 1)],
                               c.ws[:, wb + 512 * hj:wb + 512 * (hj + 1)],
                               hc == 0, hc == 31,
                               r=WSK(wb, wb + 2048) + xkk(512 * hc + 128 * st, 512 * hc + 128 * (st + 1)),
                               w=[f"pB:{st}"])
                for st in range(4):
                    vop(lambda e, st=st, cs=cs: e.tensor_scalar(
                        out=R[:, R_Y + 1024 * st + 512 * cs:R_Y + 1024 * st + 512 * (cs + 1)],
                        in0=pB[:, 512 * st:512 * (st + 1)], scalar1=c.pcols[:, st:st + 1],
                        scalar2=None, op0=ALU.mult),
                        r=[f"pB:{st}", "pcols_sb"], w=ak(R_Y + 1024 * st + 512 * cs, R_Y + 1024 * st + 512 * (cs + 1)))
            for tt in range(NT):
                tr(pB[0:1, 1024 + 128 * (tt % 2):1024 + 128 * (tt % 2) + 128],
                   c.slots[:, tt:tt + 1], c.idf, r=["slots", "idf_sb"], w=["pB:2"])
                vop(lambda e, tt=tt: e.tensor_copy(
                    out=c.scr1[0:1, 128 * tt:128 * (tt + 1)],
                    in_=pB[0:1, 1024 + 128 * (tt % 2):1024 + 128 * (tt % 2) + 128]),
                    r=["pB:2"], w=[SCR1[tt // 8]])
            for s in range(4):
                mm(pA[:, 512 * s:512 * (s + 1)], c.onesf[0:1, 0:128],
                   c.scr1[0:1, 512 * s:512 * (s + 1)], True, True,
                   r=["onesf_sb"] + SCR1, w=[f"pA:{s}"])
            for s in range(4):
                vop(lambda e, s=s: e.tensor_copy(out=c.scr1[:, 512 * s:512 * (s + 1)],
                                                 in_=pA[:, 512 * s:512 * (s + 1)]),
                    r=[f"pA:{s}"], w=[SCR1[s // 2]])
            for sc in range(4):
                vop(lambda e, sc=sc: e.tensor_scalar(
                    out=R[:, R_SEL + 2048 * sc:R_SEL + 2048 * (sc + 1)], in0=c.scr1[:, :],
                    scalar1=c.iotacol[:, sc:sc + 1], scalar2=0.0,
                    op0=ALU.subtract, op1=ALU.is_equal),
                    r=SCR1 + ["iotacol_sb"], w=ak(R_SEL + 2048 * sc, R_SEL + 2048 * (sc + 1)))
            for tt in range(NT):
                for cs in range(2):
                    bank = (tt * 2 + cs) % 2
                    for sc in range(4):
                        mm(pA[:, 512 * bank:512 * (bank + 1)],
                           R[:, R_SEL + 2048 * sc + 128 * tt:R_SEL + 2048 * sc + 128 * (tt + 1)],
                           R[:, R_Y + 1024 * sc + 512 * cs:R_Y + 1024 * sc + 512 * (cs + 1)],
                           sc == 0, sc == 3,
                           r=ak(R_SEL + 2048 * sc + 128 * tt, R_SEL + 2048 * sc + 128 * (tt + 1)) + ak(R_Y + 1024 * sc + 512 * cs, R_Y + 1024 * sc + 512 * (cs + 1)),
                           w=[f"pA:{bank}"])
                    xs = c.x[:, 1024 * tt + 512 * cs:1024 * tt + 512 * (cs + 1)]
                    vop(lambda e, xs=xs, bank=bank: e.tensor_add(
                        out=xs, in0=xs, in1=pA[:, 512 * bank:512 * (bank + 1)]),
                        r=[f"pA:{bank}", f"x:{tt}"], w=[f"x:{tt}"])

    def store():
        sdma(lambda e: e.dma_start(
            out=c.out.rearrange("(n p) f -> p n f", p=128),
            in_=c.x[:, :].rearrange("p (n f) -> p n f", n=NT)), r=XT(), w=(), tag="stx")

    stages = ["ln0", "xnt0", "gn", "attn0", "ffn0", "ln1", "xnt1", "attn1", "ffn1"]
    stop = len(stages) if upto == "full" else stages.index(upto) + 1
    si = 0
    ln(0); si += 1
    if si < stop:
        build_xnT(); si += 1
    if si < stop:
        gn(); si += 1
    if si < stop:
        l0_attn(); si += 1
    if si < stop:
        l0_ffn(); si += 1
    if si < stop:
        ln(1); si += 1
    if si < stop:
        build_xnT(); si += 1
    if si < stop:
        l1_route_scores()
        c.fill = l1_route_fill_steps()
        l1_attn(); si += 1
    if si < stop:
        l1_ffn(); si += 1
    if "xntok" in taps:
        tap("xntok", R[:, R_XNTOK:R_XNTOK + 16384], ak(R_XNTOK, R_XNTOK + 16384), (128, 16384), BF16)
    if "xnt" in taps:
        tap("xnt", c.xnT[:, :], xkk(0, 16384), (128, 16384), BF16)
    if "gnab" in taps:
        tap("gnab", c.gnA[:, :], ["gnA"], (128, 8), F32)
        tap("gnbb", c.gnB[:, :], ["gnB"], (128, 8), F32)
    if "dbg1" in taps:
        tap("dbg1", c.offsb[0:1, :], ["offsb"], (1, 128), F32)
        tap("dbg2", c.offsb[:, :], ["offsb"], (128, 128), F32)
        tap("bqkvd", c.bqkvc[:, :], ["bqkv_sb"], (128, 24), F32)
        tap("gnbbd", c.gnBb[:, :], ["gnBb"], (128, 8), BF16)
        tap("tmpcd", c.tmpc[:, :], ["tmpc"], (128, 4), F32)
        tap("etd", c.ET[:, :], ["ET:0", "ET:1"], (128, 2048), BF16)
    if "qkv" in taps:
        tap("qp", R[:, R_QP:R_QP + 2048], ak(R_QP, R_QP + 2048), (128, 2048), BF16)
        tap("kp", R[:, R_KP:R_KP + 2048], ak(R_KP, R_KP + 2048), (128, 2048), BF16)
        tap("vp", R[:, R_VP:R_VP + 2080], ak(R_VP, R_VP + 2080), (128, 2080), BF16)
        tap("mb2", c.mb2[:, :], ["mb2"], (128, 2), F32)
        tap("bcall", c.bcall[:, :], ["bcall"], (128, 24), F32)
    if "ot" in taps:
        tap("ot", R[:, R_OT:R_OT + 16384], ak(R_OT, R_OT + 16384), (128, 16384), BF16)
    if "gatet" in taps:
        tap("gatet", R[0:16, R_GATET:R_GATET + 2048], ak(R_GATET, R_GATET + 2048), (16, 2048), BF16)
    if "probs" in taps:
        tap("probs", c.probs[:, :], ["probs"], (128, 128), F32)
    if "mask" in taps:
        tap("mask", c.mask[:, :], ["mask"], (128, 128), BF16)
    if "cum" in taps:
        tap("cum", c.cum[:, :], ["cum"], (128, 128), F32)
        tap("offsb", c.offsb[:, :], ["offsb"], (128, 128), F32)
    if "coefb" in taps:
        tap("coefb", c.ET[0:16, :], ["ET:0", "ET:1"], (16, 2048), BF16)
    store()


_NC_CACHE = {}


def kernel(**inputs):
    host = host_prep(inputs)
    x_full = np.asarray(inputs["L0_attn_x"], dtype=np.float32)
    if "full" not in _NC_CACHE:
        _NC_CACHE["full"] = build_program()
    nc = _NC_CACHE["full"]
    in_maps = []
    for b in range(8):
        m = dict(host)
        m["x"] = np.ascontiguousarray(x_full[b])
        in_maps.append(m)
    res = run_bass_kernel_spmd(nc, in_maps, core_ids=list(range(8)))
    out = np.stack([np.asarray(res.results[i]["out"]) for i in range(8)])
    return out.astype(np.float32)


# version 07 (final)
# speedup vs baseline: 1.4546x; 1.0327x over previous
import math
import os
import numpy as np
import ml_dtypes
from contextlib import ExitStack

import concourse.bass as bass
import concourse.mybir as mybir
from concourse.bass_utils import run_bass_kernel_spmd

F32 = mybir.dt.float32
BF16 = mybir.dt.bfloat16
AF = mybir.ActivationFunctionType
ALU = mybir.AluOpType
AX = mybir.AxisListType

S = 2048
C = 1024
NT = S // 128
NCH = C // 128
NHEAD = 16
DH = 64

ENGINES = ("tensor", "vector", "scalar", "gpsimd", "sync")


class Em:
    def __init__(self, n_dma_sems=24):
        self.evs = []
        self.eng_ops = {e: [] for e in ENGINES}
        self.lw = {}
        self.rd = {}
        self.n_dma = n_dma_sems
        self.dma_use = [0] * n_dma_sems
        self.dma_rr = 0
        self.emitted = []

    @staticmethod
    def _keys(x):
        if x is None:
            return []
        if isinstance(x, str):
            return [x]
        out = []
        for e in x:
            if isinstance(e, str):
                out.append(e)
            else:
                out.extend(e)
        return out

    def _deps(self, reads, writes):
        deps = set()
        for k in reads:
            if k in self.lw:
                deps.add(self.lw[k])
        for k in writes:
            if k in self.lw:
                deps.add(self.lw[k])
            for r in self.rd.get(k, ()):
                deps.add(r)
        return deps

    def _record(self, kind, eng, fn, reads, writes, tag, extra):
        reads = self._keys(reads)
        writes = self._keys(writes)
        evid = len(self.evs)
        ev = dict(kind=kind, eng=eng, fn=fn, deps=self._deps(reads, writes),
                  marked=False, tag=tag)
        ev.update(extra)
        self.evs.append(ev)
        self.eng_ops[eng].append(evid)
        for k in reads:
            self.rd.setdefault(k, []).append(evid)
        for k in writes:
            self.lw[k] = evid
            self.rd[k] = []
        return evid

    def op(self, eng, fn, reads=(), writes=(), tag=""):
        return self._record("op", eng, fn, reads, writes, tag, {})

    def dma(self, eng, fn, reads=(), writes=(), tag=""):
        q = self.dma_rr
        self.dma_rr = (self.dma_rr + 1) % self.n_dma
        self.dma_use[q] += 1
        return self._record("dma", eng, fn, reads, writes, tag,
                            dict(q=q, use=self.dma_use[q]))

    def build(self, nc, es):
        for ev in self.evs:
            for d in ev["deps"]:
                dev = self.evs[d]
                if dev["kind"] == "op":
                    dev["marked"] = True
        inc = {e: 0 for e in ENGINES}
        for e in ENGINES:
            for evid in self.eng_ops[e]:
                ev = self.evs[evid]
                if ev["kind"] == "op" and ev["marked"]:
                    inc[e] += 1
                ev["incval"] = inc[e]
        esem = {e: es.enter_context(nc.semaphore(f"sem_{e}")) for e in ENGINES}
        dsem = [es.enter_context(nc.semaphore(f"dma{i}")) for i in range(self.n_dma)]
        self._esem, self._dsem = esem, dsem

        emr = self

        def body(eng_name):
            def run(engobj):
                wm = {}
                for evid in emr.eng_ops[eng_name]:
                    ev = emr.evs[evid]
                    waits = []
                    for d in ev["deps"]:
                        dev = emr.evs[d]
                        if dev["kind"] == "dma":
                            waits.append((("d", dev["q"]), 16 * dev["use"], dsem[dev["q"]]))
                        else:
                            pe = dev["eng"]
                            if pe == eng_name and eng_name == "tensor":
                                continue
                            waits.append((("e", pe), dev["incval"], esem[pe]))
                    if ev["kind"] == "dma" and ev["use"] > 1:
                        waits.append((("d", ev["q"]), 16 * (ev["use"] - 1), dsem[ev["q"]]))
                    wlog = []
                    for key, val, sem in sorted(waits, key=lambda w: (str(w[0]), w[1])):
                        if wm.get(key, 0) < val:
                            engobj.wait_ge(sem, val)
                            wm[key] = val
                            wlog.append((key, val))
                    emr.emitted.append((eng_name, evid, ev["tag"], wlog,
                                        ev["incval"] if ev.get("marked") else None))
                    inst = ev["fn"](engobj)
                    if ev["kind"] == "dma":
                        inst.then_inc(dsem[ev["q"]], 16)
                    elif ev["marked"]:
                        inst.then_inc(esem[eng_name], 1)
                if eng_name == "sync":
                    for i in range(emr.n_dma):
                        if emr.dma_use[i]:
                            engobj.wait_ge(dsem[i], 16 * emr.dma_use[i])
            return run

        with nc.Block() as block:
            block.sync(body("sync"))
            block.gpsimd(body("gpsimd"))
            block.vector(body("vector"))
            block.scalar(body("scalar"))
            block.tensor(body("tensor"))



def _chunked(w):
    n, rem = divmod(w.shape[0], 128)
    assert rem == 0
    return np.ascontiguousarray(
        w.reshape(n, 128, w.shape[1]).transpose(1, 0, 2).reshape(128, n * w.shape[1]))


def _pcols(v):
    n = v.shape[0] // 128
    return np.ascontiguousarray(v.reshape(n, 128).T)


def _row(v):
    return np.ascontiguousarray(v.reshape(1, -1))


def host_consts():
    c = {}
    c["onesf"] = np.ones((128, 128), np.float32)
    c["idf"] = np.eye(128, dtype=np.float32)
    s64 = np.zeros((64, 128), np.float32)
    for d in range(64):
        s64[d, 64 + d] = 1.0
    c["s64"] = s64
    eh = np.zeros((128, 16 * NCH), np.float32)
    for ch in range(NCH):
        for p in range(128):
            eh[p, 16 * ch + 2 * ch + p // 64] = 1.0
    c["ehead"] = eh
    eh2 = np.zeros((16, 128 * NCH), np.float32)
    for ch in range(NCH):
        for p in range(128):
            eh2[2 * ch + p // 64, 128 * ch + p] = 1.0
    c["ehead2"] = eh2
    s16 = np.zeros((16, 128 * 16), np.float32)
    for r in range(16):
        s16[r, 128 * r:128 * (r + 1)] = 1.0
    c["sel16"] = s16
    egn = np.zeros((128, 32 * NCH), np.float32)
    egnT = np.zeros((32, 128 * NCH), np.float32)
    for ch in range(NCH):
        for p in range(128):
            g = (128 * ch + p) // 32
            egn[p, 32 * ch + g] = 1.0
            egnT[g, 128 * ch + p] = 1.0
    c["egn"] = egn
    c["egnT"] = egnT
    tril = np.tril(np.ones((128, 128), np.float32)).T
    c["tril"] = np.ascontiguousarray(tril)
    trie = np.zeros((128, 128), np.float32)
    for p2 in range(128):
        for p in range(128):
            if (p2 % 8) == (p % 8) and (p2 // 8) < (p // 8):
                trie[p2, p] = 1.0
    c["trie"] = trie
    c["iota512"] = np.broadcast_to(np.arange(512, dtype=np.float32), (128, 512)).copy()
    ic = np.zeros((128, 4), np.float32)
    for sc in range(4):
        ic[:, sc] = 128 * sc + np.arange(128)
    c["iotacol"] = ic
    c["epsrow"] = np.full((128, 1), 1e-5, np.float32)
    h2 = np.zeros((128, 2), np.float32)
    h2[0:64, 0] = 1.0
    h2[64:128, 1] = 1.0
    c["h2"] = h2
    s8 = np.zeros((128, 8), np.float32)
    for f in range(128):
        s8[f, f % 8] = 1.0
    c["sel8"] = s8
    return c


def host_prep(inputs):
    f = np.float32
    d = {}
    d["wqkv"] = _chunked(inputs["L0_attn_W_qkv"].astype(f, copy=False))
    d["bqkv"] = _pcols(inputs["L0_attn_b_qkv"])
    d["gng"] = _pcols(inputs["L0_attn_gn_gamma"])
    d["gnb"] = _pcols(inputs["L0_attn_gn_beta"])
    d["wol"] = _chunked(inputs["L0_attn_W_o"])
    d["bo"] = _row(inputs["L0_attn_b_o"])
    su = inputs["L0_ffn_shared_up"]
    sd = inputs["L0_ffn_shared_down"]
    ru = inputs["L0_ffn_routed_up"]
    rdn = inputs["L0_ffn_routed_down"]
    d["sut"] = np.stack([_chunked(np.ascontiguousarray(su[i].T)) for i in range(2)])
    d["sdt"] = np.stack([_chunked(np.ascontiguousarray(sd[i].T)) for i in range(2)])
    d["rut"] = np.stack([_chunked(np.ascontiguousarray(ru[i].T)) for i in range(16)])
    d["rdt"] = np.stack([_chunked(np.ascontiguousarray(rdn[i].T)) for i in range(16)])
    d["rwt"] = _chunked(np.ascontiguousarray(inputs["L0_ffn_router_w"].T))
    d["wq1"] = _chunked(inputs["L1_attn_Wq"])
    d["wk1"] = _chunked(inputs["L1_attn_Wk"])
    d["wv1"] = _chunked(inputs["L1_attn_Wv"])
    d["wo1"] = _chunked(inputs["L1_attn_Wo"])
    d["bq1"] = _pcols(inputs["L1_attn_bq"])
    d["bk1"] = _pcols(inputs["L1_attn_bk"])
    d["bv1"] = _pcols(inputs["L1_attn_bv"])
    d["bo1"] = _row(inputs["L1_attn_bo"])
    d["gwl"] = _chunked(inputs["L1_ffn_gate_w"])
    w1 = inputs["L1_ffn_w1"]
    w2 = inputs["L1_ffn_w2"]
    d["w1l"] = np.stack([_chunked(w1[e]) for e in range(8)])
    d["w2l"] = np.stack([_chunked(w2[e]) for e in range(8)])
    d.update(host_consts())
    return d



class Ctx:
    pass


def ak(lo, hi):
    return [f"R:{g}" for g in range(lo // 512, (hi + 511) // 512)]


def xkk(lo, hi):
    return [f"xnT:{g}" for g in range(lo // 512, (hi + 511) // 512)]


def pak(lo, hi):
    return [f"pA:{b}" for b in range(lo // 512, (hi + 511) // 512)]


def pbk(lo, hi):
    return [f"pB:{b}" for b in range(lo // 512, (hi + 511) // 512)]


XT = lambda: [f"x:{t}" for t in range(NT)]
XNT_ALL = lambda: [f"xnT:{g}" for g in range(32)]

R_XNTOK = 0
R_QP = 0
R_KP = 2048
R_VP = 4096
R_OT = 16384
R_A0 = 0
R_A1 = 4096
R_GATEB = 8192
R_GATET = 10240
R_U0 = 12288
R_U1 = 16384
R_D0 = 20480
R_D1 = 24576
R_VBUF = 16384
R_SEL = 16384
R_Y = 24576
R_XIN = 28672


def build_program(taps=(), upto="full"):
    nc = bass.Bass("TRN2", target_bir_lowering=False)
    em = Em()
    c = Ctx()
    c.nc, c.em, c.taps = nc, em, {}
    c.fill = []
    c.att_fill = []

    dr = {}
    def din(name, shape, dt=F32):
        dr[name] = nc.dram_tensor(name, list(shape), dt, kind="ExternalInput").ap()
    din("x", (S, C))
    din("wqkv", (128, 8 * 3072)); din("bqkv", (128, 24))
    din("gng", (128, 8)); din("gnb", (128, 8))
    din("wol", (128, 8 * 1024)); din("bo", (1, 1024))
    din("sut", (2, 128, 4096)); din("sdt", (2, 128, 4096))
    din("rut", (16, 128, 4096)); din("rdt", (16, 128, 4096))
    din("rwt", (128, 128))
    din("wq1", (128, 8192)); din("wk1", (128, 8192)); din("wv1", (128, 8192))
    din("wo1", (128, 8192))
    din("bq1", (128, 8)); din("bk1", (128, 8)); din("bv1", (128, 8)); din("bo1", (1, 1024))
    din("gwl", (128, 64))
    din("w1l", (8, 128, 32768)); din("w2l", (8, 128, 32768))
    for n, sh in (("onesf", (128, 128)), ("idf", (128, 128)), ("s64", (64, 128)),
                  ("ehead", (128, 128)), ("ehead2", (16, 1024)), ("sel16", (16, 2048)),
                  ("egn", (128, 256)), ("egnT", (32, 1024)), ("tril", (128, 128)),
                  ("trie", (128, 128)), ("iota512", (128, 512)), ("iotacol", (128, 4)),
                  ("epsrow", (128, 1)), ("sel8", (128, 8)), ("h2", (128, 2))):
        din(n, sh)
    out = nc.dram_tensor("out", [S, C], F32, kind="ExternalOutput").ap()
    c.dr, c.out = dr, out

    es = ExitStack()
    sb = lambda n, sh, dt: es.enter_context(nc.sbuf_tensor(n, sh, dt))
    with es:
        c.x = sb("x_sb", [128, NT * 1024], F32)
        c.xnT = sb("xnT_sb", [128, 16384], BF16)
        c.R = sb("reg64", [128, 32768], BF16)
        c.ET = sb("ET", [128, 2048], BF16)
        c.ws = sb("wstream", [128, 6144], BF16)
        c.scr0 = sb("scr0", [128, 1024], F32)
        c.scr1 = sb("scr1", [128, 2048], F32)
        c.onesf = sb("onesf_sb", [128, 128], F32)
        c.onesb = sb("onesb_sb", [128, 128], BF16)
        c.idf = sb("idf_sb", [128, 128], F32)
        c.idb = sb("idb_sb", [128, 128], BF16)
        c.s64b = sb("s64b_sb", [64, 128], BF16)
        c.eheadb = sb("eheadb_sb", [128, 128], BF16)
        c.egnb = sb("egnb_sb", [128, 256], BF16)
        c.egnTb = sb("egnTb_sb", [32, 1024], BF16)
        c.trilb = sb("trilb_sb", [128, 128], BF16)
        c.trieb = sb("trieb_sb", [128, 128], BF16)
        c.sel8b = sb("sel8b_sb", [128, 8], BF16)
        c.iota512 = sb("iota512_sb", [128, 512], F32)
        c.iotacol = sb("iotacol_sb", [128, 4], F32)
        c.lnS = sb("lnS", [128, 16], F32)
        c.lnQ = sb("lnQ", [128, 16], F32)
        c.lnM = sb("lnM", [128, 16], F32)
        c.lnV = sb("lnV", [128, 16], F32)
        c.lnR = sb("lnR", [128, 16], F32)
        c.gnSQ = sb("gnSQ", [128, 16], BF16)
        c.gnSQf = sb("gnSQf", [128, 16], F32)
        c.gst = sb("gst", [32, 4], F32)
        c.gstb = sb("gstb", [32, 4], BF16)
        c.gnE = sb("gnE", [128, 32], F32)
        c.gnA = sb("gnA", [128, 8], F32)
        c.gnAb = sb("gnAb", [128, 8], BF16)
        c.gnB = sb("gnB", [128, 8], F32)
        c.gnBb = sb("gnBb", [128, 8], BF16)
        c.bcall = sb("bcall", [128, 24], F32)
        c.trow = sb("trow", [1, 512], F32)
        c.ehead2b = sb("ehead2b_sb", [16, 1024], BF16)
        c.tmpc = sb("tmpc", [128, 4], F32)
        c.gngs = sb("gng_s", [128, 8], F32)
        c.gnbs = sb("gnb_s", [128, 8], F32)
        c.bqkvc = sb("bqkv_s", [128, 24], F32)
        c.bq1s = sb("bq1_s", [128, 8], F32)
        c.bk1s = sb("bk1_s", [128, 8], F32)
        c.bv1s = sb("bv1_s", [128, 8], F32)
        c.rwts = sb("rwt_s", [128, 128], BF16)
        c.gwls = sb("gwl_s", [128, 64], BF16)
        c.mb2 = sb("mb2", [128, 2], F32)
        c.h2b = sb("h2_s", [128, 2], BF16)
        c.otmp = sb("otmp", [64, 1024], BF16)
        c.botile = sb("botile", [128, 1024], BF16)
        c.probs = sb("probs128", [128, 128], F32)
        c.probsb = sb("probsb", [128, 128], BF16)
        c.mask = sb("mask128", [128, 128], BF16)
        c.maskf = sb("maskf", [128, 128], F32)
        c.cum = sb("cum128", [128, 128], F32)
        c.offsb = sb("offsb", [128, 128], F32)
        c.totrow = sb("totrow", [1, 128], F32)
        c.bis = sb("bis", [8, 8], F32)
        c.cnt1b = sb("cnt1b", [128, 1], BF16)
        c.totb = sb("totb", [128, 1], BF16)
        c.slots = sb("slots", [128, 16], F32)
        c.neg9 = sb("neg9", [128, 1], F32)
        c.pcols = sb("pcols_sb", [128, 4], F32)
        c.epsc = sb("epsc", [128, 1], F32)
        c.pA = es.enter_context(nc.psum_tensor("pA", [128, 2048], F32))
        c.pB = es.enter_context(nc.psum_tensor("pB", [128, 2048], F32))

        emit_all(c, upto, taps)
        em.build(nc, es)
    return nc


SCR1 = ["scr1:0", "scr1:1"]
WSK = lambda lo, hi: [f"ws:{g}" for g in range(lo // 512, (hi + 511) // 512)]


def emit_all(c, upto, taps):
    em, dr, nc = c.em, c.dr, c.nc
    R, pA, pB = c.R, c.pA, c.pB

    def top(fn, r=(), w=(), tag=""):
        em.op("tensor", fn, r, w, tag)

    def vop(fn, r=(), w=(), tag=""):
        em.op("vector", fn, r, w, tag)

    def sop(fn, r=(), w=(), tag=""):
        em.op("scalar", fn, r, w, tag)

    def gop(fn, r=(), w=(), tag=""):
        em.op("gpsimd", fn, r, w, tag)

    def sdma(fn, r=(), w=(), tag=""):
        em.dma("sync", fn, r, w, tag)

    def gdma(fn, r=(), w=(), tag=""):
        em.dma("gpsimd", fn, r, w, tag)

    tap_stage = {}

    def tap(name, ap, rkeys, shape, dt):
        t = nc.dram_tensor("tap_" + name, list(shape), dt, kind="ExternalOutput").ap()
        sdma(lambda e, t=t, ap=ap: e.dma_start(out=t[:, :], in_=ap), r=rkeys, w=(), tag="tap")

    sdma(lambda e: e.dma_start(
        out=c.x[:, :].rearrange("p (n f) -> p n f", n=NT),
        in_=dr["x"].rearrange("(n p) f -> p n f", p=128)), r=(), w=XT(), tag="ldx")
    for name, dst in (("onesf", c.onesf), ("idf", c.idf), ("iota512", c.iota512),
                      ("iotacol", c.iotacol), ("epsrow", c.epsc), ("gng", c.gngs),
                      ("gnb", c.gnbs), ("bqkv", c.bqkvc), ("bq1", c.bq1s),
                      ("bk1", c.bk1s), ("bv1", c.bv1s)):
        sdma(lambda e, d=dr[name], t=dst: e.dma_start(out=t[:, :], in_=d[:, :]),
             r=(), w=[name + "_sb"], tag="ldc")
    for name, dst in (("onesf", c.onesb), ("idf", c.idb), ("s64", c.s64b),
                      ("ehead", c.eheadb), ("ehead2", c.ehead2b),
                      ("egn", c.egnb), ("egnT", c.egnTb), ("tril", c.trilb),
                      ("trie", c.trieb), ("sel8", c.sel8b), ("h2", c.h2b), ("rwt", c.rwts),
                      ("gwl", c.gwls)):
        gdma(lambda e, d=dr[name], t=dst: e.dma_start(out=t[:, :], in_=d[:, :]),
             r=(), w=[name + "_b"], tag="ldcb")
    vop(lambda e: e.memset(c.neg9[:, :], -1.0e9), r=(), w=["neg9"])

    def mm(out, lhsT, rhs, st, sp, r, w, tag=""):
        top(lambda e, o=out, l=lhsT, rr=rhs, st=st, sp=sp: e.matmul(o, l, rr, start=st, stop=sp), r=r, w=w, tag=tag)

    def tr(out, in_, ident, r, w):
        idap = ident if isinstance(ident, bass.AP) else ident[:, :]
        top(lambda e, o=out, i=in_, d=idap: e.transpose(o, i, d), r=r, w=w)

    def ln(layer):
        part = int(os.environ.get("K_LN1PART", "3")) if layer == 1 else 3
        addxn = (layer == 0)
        for t in range(NT):
            xt = c.x[:, 1024 * t:1024 * (t + 1)]
            vop(lambda e, xt=xt, t=t: e.reduce_sum(out=c.lnS[:, t:t + 1], in_=xt, axis=AX.X),
                r=[f"x:{t}"], w=["lnS"])
            sop(lambda e, xt=xt, t=t: e.activation(out=c.scr0[:, :], in_=xt, func=AF.Square,
                                                   accum_out=c.lnQ[:, t:t + 1]),
                r=[f"x:{t}"], w=["scr0", "lnQ"])
        if part < 2:
            return
        vop(lambda e: e.tensor_scalar(out=c.lnM[:, :], in0=c.lnS[:, :], scalar1=1.0 / C,
                                      scalar2=None, op0=ALU.mult), r=["lnS"], w=["lnM"])
        vop(lambda e: e.tensor_mul(out=c.lnV[:, :], in0=c.lnM[:, :], in1=c.lnM[:, :]),
            r=["lnM"], w=["lnV"])
        vop(lambda e: e.scalar_tensor_tensor(out=c.lnV[:, :], in0=c.lnQ[:, :], scalar=1.0 / C,
                                             in1=c.lnV[:, :], op0=ALU.mult, op1=ALU.subtract),
            r=["lnQ", "lnV"], w=["lnV"])
        vop(lambda e: e.tensor_scalar(out=c.lnV[:, :], in0=c.lnV[:, :], scalar1=1e-5,
                                      scalar2=None, op0=ALU.add), r=["lnV"], w=["lnV"])
        sop(lambda e: e.activation(out=c.lnR[:, :], in_=c.lnV[:, :], func=AF.Sqrt),
            r=["lnV"], w=["lnR"])
        vop(lambda e: e.reciprocal(out=c.lnR[:, :], in_=c.lnR[:, :]), r=["lnR"], w=["lnR"])
        if part < 3:
            return
        for t in range(NT):
            xt = c.x[:, 1024 * t:1024 * (t + 1)]
            dst = R[:, R_XNTOK + 1024 * t:R_XNTOK + 1024 * (t + 1)]
            dkeys = ak(R_XNTOK + 1024 * t, R_XNTOK + 1024 * (t + 1))
            if addxn:
                h = (t % 2) * 1024
                scr = c.scr1[:, h:h + 1024]
                skey = [SCR1[t % 2]]
                vop(lambda e, xt=xt, t=t, scr=scr: e.tensor_scalar(
                    out=scr, in0=xt, scalar1=c.lnM[:, t:t + 1], scalar2=c.lnR[:, t:t + 1],
                    op0=ALU.subtract, op1=ALU.mult), r=[f"x:{t}", "lnM", "lnR"], w=skey)
                vop(lambda e, xt=xt, scr=scr: e.tensor_add(out=xt, in0=xt, in1=scr),
                    r=skey + [f"x:{t}"], w=[f"x:{t}"])
                vop(lambda e, dst=dst, scr=scr: e.tensor_copy(out=dst, in_=scr),
                    r=skey, w=dkeys)
            else:
                vop(lambda e, xt=xt, t=t, dst=dst: e.tensor_scalar(
                    out=dst, in0=xt, scalar1=c.lnM[:, t:t + 1], scalar2=c.lnR[:, t:t + 1],
                    op0=ALU.subtract, op1=ALU.mult), r=[f"x:{t}", "lnM", "lnR"], w=dkeys)

    def build_xnT():
        for cc in range(NCH):
            for tg in range(2):
                bank = (cc * 2 + tg) % 4
                pview = pA[:, 512 * bank:512 * (bank + 1)].bitcast(BF16)
                bkey = [f"pA:{bank}"]
                for j in range(8):
                    t = tg * 8 + j
                    tr(pview[:, 128 * j:128 * (j + 1)],
                       R[:, R_XNTOK + 1024 * t + 128 * cc:R_XNTOK + 1024 * t + 128 * (cc + 1)],
                       c.idb,
                       r=ak(R_XNTOK + 1024 * t + 128 * cc, R_XNTOK + 1024 * t + 128 * (cc + 1)) + ["idf_b"],
                       w=bkey)
                dlo = 2048 * cc + 1024 * tg
                vop(lambda e, pv=pview, cc=cc, tg=tg: e.tensor_copy(
                    out=c.xnT[:, 2048 * cc + 1024 * tg: 2048 * cc + 1024 * tg + 1024], in_=pv),
                    r=bkey, w=xkk(dlo, dlo + 1024))

    def gn():
        for cc in range(NCH):
            ch = c.xnT[:, 2048 * cc:2048 * (cc + 1)]
            rk = xkk(2048 * cc, 2048 * (cc + 1))
            vop(lambda e, ch=ch, cc=cc: e.reduce_sum(out=c.gnSQf[:, 2 * cc:2 * cc + 1], in_=ch, axis=AX.X),
                r=rk, w=["gnSQf"])
            sop(lambda e, ch=ch, cc=cc: e.activation(out=c.scr1[:, :], in_=ch, func=AF.Square,
                                                     accum_out=c.gnSQf[:, 2 * cc + 1:2 * cc + 2]),
                r=rk, w=SCR1 + ["gnSQf"])
        vop(lambda e: e.tensor_copy(out=c.gnSQ[:, :], in_=c.gnSQf[:, :]), r=["gnSQf"], w=["gnSQ"])
        for cc in range(NCH):
            mm(pB[0:32, 0:2], c.egnb[:, 32 * cc:32 * (cc + 1)], c.gnSQ[:, 2 * cc:2 * cc + 2],
               cc == 0, cc == 7, r=["egn_b", "gnSQ"], w=["pB:0"])
        vop(lambda e: e.tensor_scalar(out=c.gst[:, 0:2], in0=pB[0:32, 0:2], scalar1=1.0 / 65536.0,
                                      scalar2=None, op0=ALU.mult), r=["pB:0"], w=["gst"])
        vop(lambda e: e.tensor_mul(out=c.gst[:, 2:3], in0=c.gst[:, 0:1], in1=c.gst[:, 0:1]),
            r=["gst"], w=["gst"])
        vop(lambda e: e.tensor_sub(out=c.gst[:, 3:4], in0=c.gst[:, 1:2], in1=c.gst[:, 2:3]),
            r=["gst"], w=["gst"])
        vop(lambda e: e.tensor_scalar(out=c.gst[:, 3:4], in0=c.gst[:, 3:4], scalar1=1e-6,
                                      scalar2=None, op0=ALU.add), r=["gst"], w=["gst"])
        sop(lambda e: e.activation(out=c.gst[:, 2:3], in_=c.gst[:, 3:4], func=AF.Sqrt),
            r=["gst"], w=["gst"])
        vop(lambda e: e.reciprocal(out=c.gst[:, 3:4], in_=c.gst[:, 2:3]), r=["gst"], w=["gst"])
        vop(lambda e: e.tensor_copy(out=c.gstb[:, 0:1], in_=c.gst[:, 3:4]), r=["gst"], w=["gstb"])
        vop(lambda e: e.tensor_copy(out=c.gstb[:, 1:2], in_=c.gst[:, 0:1]), r=["gst"], w=["gstb"])
        vop(lambda e: e.tensor_sub(out=c.gst[:, 1:2], in0=c.gst[:, 3:4], in1=c.gstb[:, 0:1]),
            r=["gst", "gstb"], w=["gst"])
        vop(lambda e: e.tensor_copy(out=c.gstb[:, 2:3], in_=c.gst[:, 1:2]), r=["gst"], w=["gstb"])
        vop(lambda e: e.tensor_sub(out=c.gst[:, 1:2], in0=c.gst[:, 0:1], in1=c.gstb[:, 1:2]),
            r=["gst", "gstb"], w=["gst"])
        vop(lambda e: e.tensor_copy(out=c.gstb[:, 3:4], in_=c.gst[:, 1:2]), r=["gst"], w=["gstb"])
        for cc in range(NCH):
            bank = 1 + (cc % 2)
            mm(pB[:, 512 * bank:512 * bank + 4], c.egnTb[:, 128 * cc:128 * (cc + 1)], c.gstb[:, 0:4],
               True, True, r=["egnT_b", "gstb"], w=[f"pB:{bank}"])
            vop(lambda e, cc=cc, bank=bank: e.tensor_copy(
                out=c.gnE[:, 4 * cc:4 * cc + 4], in_=pB[:, 512 * bank:512 * bank + 4]),
                r=[f"pB:{bank}"], w=["gnE"])
        for cc in range(NCH):
            vop(lambda e, cc=cc: e.tensor_add(out=c.gnA[:, cc:cc + 1], in0=c.gnE[:, 4 * cc:4 * cc + 1],
                                              in1=c.gnE[:, 4 * cc + 2:4 * cc + 3]), r=["gnE"], w=["gnA"])
            vop(lambda e, cc=cc: e.tensor_mul(out=c.gnA[:, cc:cc + 1], in0=c.gnA[:, cc:cc + 1],
                                              in1=c.gngs[:, cc:cc + 1]), r=["gnA", "gng_sb"], w=["gnA"])
            vop(lambda e, cc=cc: e.tensor_add(out=c.gnB[:, cc:cc + 1], in0=c.gnE[:, 4 * cc + 1:4 * cc + 2],
                                              in1=c.gnE[:, 4 * cc + 3:4 * cc + 4]), r=["gnE"], w=["gnB"])
            vop(lambda e, cc=cc: e.tensor_mul(out=c.gnB[:, cc:cc + 1], in0=c.gnB[:, cc:cc + 1],
                                              in1=c.gnA[:, cc:cc + 1]), r=["gnB", "gnA"], w=["gnB"])
            vop(lambda e, cc=cc: e.tensor_sub(out=c.gnB[:, cc:cc + 1], in0=c.gnbs[:, cc:cc + 1],
                                              in1=c.gnB[:, cc:cc + 1]), r=["gnb_sb", "gnB"], w=["gnB"])
        vop(lambda e: e.tensor_copy(out=c.gnAb[:, :], in_=c.gnA[:, :]), r=["gnA"], w=["gnAb"])
        vop(lambda e: e.tensor_copy(out=c.gnBb[:, :], in_=c.gnB[:, :]), r=["gnB"], w=["gnBb"])

    def l0_attn_bias():
        wq3 = dr["wqkv"].rearrange("p (n f) -> p n f", n=8)
        for os6 in range(6):
            gdma(lambda e, os6=os6: e.dma_start(
                out=c.ws[:, 0:4096].rearrange("p (n f) -> p n f", n=8),
                in_=wq3[:, :, 512 * os6:512 * (os6 + 1)]), r=(), w=WSK(0, 4096), tag="wb")
            for cc in range(NCH):
                mm(pA[0:1, 1536:2048], c.gnBb[:, cc:cc + 1], c.ws[:, 512 * cc:512 * (cc + 1)],
                   cc == 0, cc == 7, r=["gnBb"] + WSK(0, 4096), w=["pA:3"])
            vop(lambda e: e.tensor_copy(out=c.trow[0:1, 0:512], in_=pA[0:1, 1536:2048]),
                r=["pA:3"], w=["trow"])
            for q4 in range(4):
                och = os6 * 4 + q4
                tr(pB[:, 1536 + q4:1537 + q4], c.trow[0:1, 128 * q4:128 * (q4 + 1)],
                   c.idf[0:1, 0:1], r=["trow", "idf_sb"], w=["pB:3"])
            vop(lambda e, os6=os6: e.tensor_add(out=c.bcall[:, 4 * os6:4 * (os6 + 1)],
                                                in0=pB[:, 1536:1540],
                                                in1=c.bqkvc[:, 4 * os6:4 * (os6 + 1)]),
                r=["pB:3", "bqkv_sb"], w=["bcall"])
        for vch in range(8):
            tr(pA[0:1, 1536 + 128 * (vch % 4):1664 + 128 * (vch % 4)], c.bcall[:, 16 + vch:17 + vch],
               c.idf, r=["bcall", "idf_sb"], w=["pA:3"])
            vop(lambda e, vch=vch: e.tensor_copy(
                out=c.trow[0:1, 0:512][0:1, 128 * (vch % 4):128 * (vch % 4 + 1)],
                in_=pA[0:1, 1536 + 128 * (vch % 4):1664 + 128 * (vch % 4)]),
                r=["pA:3"], w=["trow"])
            if vch % 4 == 3:
                half = vch // 4
                for s2 in range(1):
                    mm(pB[:, 1024 + 512 * half:1024 + 512 * (half + 1)], c.onesf[0:1, 0:128],
                       c.trow[0:1, 0:512], True, True,
                       r=["onesf_sb", "trow"], w=pbk(1024 + 512 * half, 1024 + 512 * (half + 1)))
                vop(lambda e, half=half: e.tensor_copy(
                    out=c.botile[:, 512 * half:512 * (half + 1)],
                    in_=pB[:, 1024 + 512 * half:1024 + 512 * (half + 1)]),
                    r=pbk(1024 + 512 * half, 1024 + 512 * (half + 1)), w=["botile"])

    def l0_attn():
        l0_attn_bias()
        wq3 = dr["wqkv"].rearrange("p (n f) -> p n f", n=8)

        def emit_qk(p):
            qb, kb = 1024 * (p % 2), 2048 + 1024 * (p % 2)
            for base, osrc in ((qb, 128 * p), (kb, 1024 + 128 * p)):
                gdma(lambda e, base=base, osrc=osrc: e.dma_start(
                    out=c.ws[:, base:base + 1024].rearrange("p (n f) -> p n f", n=8),
                    in_=wq3[:, :, osrc:osrc + 128]),
                    r=(), w=WSK(base, base + 1024), tag="wqkv")
            for base in (qb, kb):
                w3 = c.ws[:, base:base + 1024].rearrange("p (n f) -> p n f", n=8)
                ab = c.gnAb[:, :].rearrange("p (n o) -> p n o", o=1).to_broadcast([128, 8, 128])
                vop(lambda e, w3=w3, ab=ab: e.tensor_mul(out=w3, in0=w3, in1=ab),
                    r=WSK(base, base + 1024) + ["gnAb"], w=WSK(base, base + 1024))
            for dstbase, wsb, bci in ((R_QP, qb, p), (R_KP, kb, 8 + p)):
                for s4 in range(4):
                    bank = s4 % 2
                    for cc in range(NCH):
                        mm(pA[:, 512 * bank:512 * (bank + 1)],
                           c.ws[:, wsb + 128 * cc:wsb + 128 * (cc + 1)],
                           c.xnT[:, 2048 * cc + 512 * s4:2048 * cc + 512 * (s4 + 1)],
                           cc == 0, cc == 7,
                           r=WSK(wsb, wsb + 1024) + xkk(2048 * cc + 512 * s4, 2048 * cc + 512 * (s4 + 1)),
                           w=[f"pA:{bank}"])
                    sop(lambda e, dstbase=dstbase, s4=s4, bank=bank, bci=bci: e.activation(
                        out=R[:, dstbase + 512 * s4:dstbase + 512 * (s4 + 1)],
                        in_=pA[:, 512 * bank:512 * (bank + 1)], func=AF.Identity,
                        bias=c.bcall[:, bci:bci + 1], scale=1.0),
                        r=[f"pA:{bank}", "bcall"], w=ak(dstbase + 512 * s4, dstbase + 512 * (s4 + 1)))

        def emit_vdma(p):
            vb = 4096 + 1024 * (p % 2)
            gdma(lambda e, p=p, vb=vb: e.dma_start(
                out=c.ws[:, vb:vb + 1024].rearrange("p (n f) -> p n f", n=8),
                in_=wq3[:, :, 2048 + 128 * p:2048 + 128 * (p + 1)]),
                r=(), w=WSK(vb, vb + 1024), tag="wv")
            w3 = c.ws[:, vb:vb + 1024].rearrange("p (n f) -> p n f", n=8)
            ab = c.gnAb[:, :].rearrange("p (n o) -> p n o", o=1).to_broadcast([128, 8, 128])
            vop(lambda e, w3=w3, ab=ab: e.tensor_mul(out=w3, in0=w3, in1=ab),
                r=WSK(vb, vb + 1024) + ["gnAb"], w=WSK(vb, vb + 1024))
            gop(lambda e: e.memset(R[:, R_VP:R_VP + 2080], 1.0), r=(), w=ak(R_VP, R_VP + 2080))

        def emit_v_tile(p, tt):
            vb = 4096 + 1024 * (p % 2)
            for cc in range(NCH):
                mm(pA[:, 1536:1664],
                   c.xnT[:, 2048 * cc + 128 * tt:2048 * cc + 128 * (tt + 1)],
                   c.ws[:, vb + 128 * cc:vb + 128 * (cc + 1)],
                   cc == 0, cc == 7,
                   r=WSK(vb, vb + 1024) + xkk(2048 * cc + 128 * tt, 2048 * cc + 128 * (tt + 1)),
                   w=["pA:3"])
            dst = R[:, R_VP + 130 * tt:R_VP + 130 * tt + 130].rearrange(
                "p (a b) -> p a b", a=2)[:, :, 0:64]
            src = pA[:, 1536:1664].rearrange("p (a b) -> p a b", a=2)
            vbt3 = c.botile[:, 128 * p:128 * (p + 1)].rearrange("p (a b) -> p a b", a=2)
            vop(lambda e, dst=dst, src=src, vbt3=vbt3: e.tensor_add(out=dst, in0=src, in1=vbt3),
                r=["pA:3", "botile"], w=ak(R_VP + 130 * tt, R_VP + 130 * tt + 130))

        def emit_mhat(p):
            for src, col in ((R_QP, 0), (R_KP, 1)):
                for half in range(2):
                    sop(lambda e, src=src, half=half: e.activation(
                        out=c.ET[:, 1024 * half:1024 * (half + 1)],
                        in_=R[:, src + 1024 * half:src + 1024 * (half + 1)], func=AF.Square),
                        r=ak(src + 1024 * half, src + 1024 * (half + 1)), w=[f"ET:{half}"], tag="mhat_sq")
                for s4 in range(4):
                    mm(pB[0:2, 512 * s4:512 * s4 + 512], c.h2b[:, :],
                       c.ET[:, 512 * s4:512 * (s4 + 1)], True, True,
                       r=["h2_b", "ET:0", "ET:1"], w=[f"pB:{s4}"], tag="mhat_mm")
                vop(lambda e, col=col: e.reduce_max(out=c.tmpc[0:2, col:col + 1], in_=pB[0:2, 0:2048],
                                                    axis=AX.X), r=pbk(0, 2048), w=["tmpc"], tag="mhat_red")
            vop(lambda e: e.tensor_add(out=c.tmpc[0:2, 3:4], in0=c.tmpc[0:2, 0:1],
                                       in1=c.tmpc[0:2, 1:2]), r=["tmpc"], w=["tmpc"])
            tr(pB[0:1, 1536:1538], c.tmpc[0:2, 3:4], c.idf[0:2, 0:2], r=["tmpc", "idf_sb"], w=["pB:3"])
            vop(lambda e: e.tensor_copy(out=c.trow[0:1, 424:426], in_=pB[0:1, 1536:1538]),
                r=["pB:3"], w=["trow"])
            for j in range(2):
                mm(pA[:, 1664 + j:1665 + j], c.onesf[0:1, 0:128], c.trow[0:1, 424 + j:425 + j],
                   True, True, r=["onesf_sb", "trow"], w=["pA:3"])
            vop(lambda e: e.tensor_scalar(out=c.mb2[:, 0:2], in0=pA[:, 1664:1666],
                                          scalar1=-0.0625, scalar2=None, op0=ALU.mult),
                r=["pA:3"], w=["mb2"])

        def p2_okeys(qh):
            return pak(0, 1024) if qh == 0 else pak(1024, 2048)

        def p2_oview(qh):
            return pA[0:65, 0:1024] if qh == 0 else pA[0:65, 1024:2048]

        def emit_p2_st(p, j, qh, kt):
            r0 = 64 * j
            hb = kt % 2
            sv = pB[:, 1024 * hb:1024 * (hb + 1)]
            for s in range(2):
                mm(sv[:, 512 * s:512 * (s + 1)],
                   R[r0:r0 + 64, R_KP + 128 * kt:R_KP + 128 * (kt + 1)],
                   R[r0:r0 + 64, R_QP + 1024 * qh + 512 * s:R_QP + 1024 * qh + 512 * (s + 1)],
                   True, True,
                   r=ak(R_KP + 128 * kt, R_KP + 128 * (kt + 1)) + ak(R_QP + 1024 * qh + 512 * s, R_QP + 1024 * qh + 512 * (s + 1)),
                   w=pbk(1024 * hb + 512 * s, 1024 * hb + 512 * (s + 1)))
            sop(lambda e, sv=sv, hb=hb, j=j: e.activation(
                out=c.ET[:, 1024 * hb:1024 * (hb + 1)], in_=sv, func=AF.Exp,
                bias=c.mb2[:, j:j + 1], scale=0.125),
                r=pbk(1024 * hb, 1024 * (hb + 1)) + ["mb2"], w=[f"ET:{hb}"])

        def emit_p2_pv(p, j, qh, kt):
            hb = kt % 2
            ov = p2_oview(qh)
            for s in range(2):
                mm(ov[:, 512 * s:512 * (s + 1)],
                   R[:, R_VP + 130 * kt + 65 * j:R_VP + 130 * kt + 65 * (j + 1)],
                   c.ET[:, 1024 * hb + 512 * s:1024 * hb + 512 * (s + 1)],
                   kt == 0, kt == 15,
                   r=[f"ET:{hb}"] + ak(R_VP + 130 * kt, R_VP + 130 * (kt + 1)),
                   w=p2_okeys(qh))

        def emit_p2_norm(p, j, qh):
            okeys = p2_okeys(qh)
            ov = p2_oview(qh)
            vop(lambda e, ov=ov: e.tensor_copy(out=c.scr1[64:65, 0:1024], in_=ov[64:65, 0:1024]),
                r=okeys, w=[SCR1[0]])
            vop(lambda e: e.reciprocal(out=c.scr1[64:65, 0:1024], in_=c.scr1[64:65, 0:1024]),
                r=[SCR1[0]], w=[SCR1[0]])
            for s in range(2):
                mm(pB[0:64, 0:512], c.onesf[64:65, 0:64],
                   c.scr1[64:65, 512 * s:512 * (s + 1)], True, True,
                   r=["onesf_sb", SCR1[0]], w=["pB:0"])
                vop(lambda e, s=s: e.tensor_copy(out=c.scr1[0:64, 512 * s:512 * (s + 1)],
                                                 in_=pB[0:64, 0:512]),
                    r=["pB:0"], w=[SCR1[0]])
            dst0 = R_OT + 2048 * p + 1024 * qh
            if j == 0:
                vop(lambda e, ov=ov, dst0=dst0: e.tensor_mul(
                    out=R[0:64, dst0:dst0 + 1024], in0=ov[0:64, 0:1024],
                    in1=c.scr1[0:64, 0:1024]),
                    r=okeys + [SCR1[0]], w=ak(dst0, dst0 + 1024))
            else:
                vop(lambda e, ov=ov: e.tensor_mul(
                    out=c.otmp[:, :], in0=ov[0:64, 0:1024],
                    in1=c.scr1[0:64, 0:1024]),
                    r=okeys + [SCR1[0]], w=["otmp"])
                for s in range(2):
                    mm(pB[:, 0:512], c.s64b[:, :], c.otmp[:, 512 * s:512 * (s + 1)],
                       True, True, r=["s64_b", "otmp"], w=["pB:0"])
                    vop(lambda e, s=s, dst0=dst0: e.tensor_copy(
                        out=R[64:128, dst0 + 512 * s:dst0 + 512 * (s + 1)],
                        in_=pB[64:128, 0:512]),
                        r=["pB:0"], w=ak(dst0 + 512 * s, dst0 + 512 * (s + 1)))

        for p in range(int(os.environ.get("K_NPAIRS", "8"))):
            emit_qk(p)
            emit_vdma(p)
            emit_mhat(p)
            for u in range(16):
                emit_v_tile(p, u)
            for j in range(2):
                for qh in range(2):
                    for kt in range(NT):
                        emit_p2_st(p, j, qh, kt)
                        if kt > 0:
                            emit_p2_pv(p, j, qh, kt - 1)
                        if kt == 6 and c.att_fill:
                            c.att_fill.pop(0)()
                    emit_p2_pv(p, j, qh, NT - 1)
                    c.att_fill.append(lambda p=p, j=j, qh=qh: emit_p2_norm(p, j, qh))
            while c.att_fill:
                c.att_fill.pop(0)()

        while c.att_fill:
            c.att_fill.pop(0)()
        sdma(lambda e: e.dma_start(out=c.scr1[0:1, 0:1024], in_=dr["bo"][:, :]), r=(), w=[SCR1[0]])
        for s in range(2):
            mm(pB[:, 512 * s:512 * (s + 1)], c.onesf[0:1, 0:128],
               c.scr1[0:1, 512 * s:512 * (s + 1)], True, True,
               r=["onesf_sb", SCR1[0]], w=pbk(512 * s, 512 * (s + 1)))
        vop(lambda e: e.tensor_copy(out=c.botile[:, :], in_=pB[:, 0:1024]), r=pbk(0, 1024), w=["botile"])
        wo3 = dr["wol"].rearrange("p (n f) -> p n f", n=8)
        for os_ in range(2):
            gdma(lambda e, os_=os_: e.dma_start(
                out=c.ws[:, 0:4096].rearrange("p (n f) -> p n f", n=8),
                in_=wo3[:, :, 512 * os_:512 * (os_ + 1)]),
                r=(), w=WSK(0, 4096), tag="wo")
            for tt in range(NT):
                bank = tt % 4
                for pch in range(8):
                    mm(pA[:, 512 * bank:512 * (bank + 1)],
                       R[:, R_OT + 2048 * pch + 128 * tt:R_OT + 2048 * pch + 128 * (tt + 1)],
                       c.ws[:, 512 * pch:512 * pch + 512],
                       pch == 0, pch == 7,
                       r=ak(R_OT + 2048 * pch + 128 * tt, R_OT + 2048 * pch + 128 * (tt + 1)) + WSK(512 * pch, 512 * pch + 512),
                       w=[f"pA:{bank}"])
                xs = c.x[:, 1024 * tt + 512 * os_:1024 * tt + 512 * (os_ + 1)]
                vop(lambda e, xs=xs, bank=bank: e.tensor_add(out=xs, in0=xs,
                                                             in1=pA[:, 512 * bank:512 * (bank + 1)]),
                    r=[f"pA:{bank}", f"x:{tt}"], w=[f"x:{tt}"])
                vop(lambda e, xs=xs, os_=os_: e.tensor_add(out=xs, in0=xs,
                                                           in1=c.botile[:, 512 * os_:512 * (os_ + 1)]),
                    r=["botile", f"x:{tt}"], w=[f"x:{tt}"])

    def l0_ffn():
        gdma(lambda e: e.dma_start(out=R[0:16, 28672:30720], in_=dr["sel16"][:, :]),
             r=(), w=ak(28672, 30720), tag="sel16")
        for tt in range(NT):
            bank = tt % 2
            pv = pA[:, 512 * bank:512 * bank + 16]
            for cc in range(NCH):
                mm(pv, c.xnT[:, 2048 * cc + 128 * tt:2048 * cc + 128 * (tt + 1)],
                   c.rwts[:, 16 * cc:16 * (cc + 1)], cc == 0, cc == 7,
                   r=["rwt_b"] + xkk(2048 * cc + 128 * tt, 2048 * cc + 128 * (tt + 1)),
                   w=[f"pA:{bank}"])
            vop(lambda e, pv=pv: e.reduce_max(out=c.tmpc[:, 2:3], in_=pv, axis=AX.X),
                r=[f"pA:{bank}"], w=["tmpc"])
            vop(lambda e: e.tensor_scalar(out=c.tmpc[:, 2:3], in0=c.tmpc[:, 2:3], scalar1=-1.0,
                                          scalar2=None, op0=ALU.mult), r=["tmpc"], w=["tmpc"])
            sop(lambda e, pv=pv: e.activation(out=c.scr0[:, 0:16], in_=pv, func=AF.Exp,
                                              bias=c.tmpc[:, 2:3], scale=1.0),
                r=[f"pA:{bank}", "tmpc"], w=["scr0"])
            vop(lambda e: e.reduce_sum(out=c.tmpc[:, 3:4], in_=c.scr0[:, 0:16], axis=AX.X),
                r=["scr0"], w=["tmpc"])
            vop(lambda e: e.reciprocal(out=c.tmpc[:, 3:4], in_=c.tmpc[:, 3:4]), r=["tmpc"], w=["tmpc"])
            vop(lambda e: e.max(out=c.scr0[:, 16:24], in_=c.scr0[:, 0:16]), r=["scr0"], w=["scr0"])
            vop(lambda e: e.tensor_scalar(out=c.scr0[:, 24:40], in0=c.scr0[:, 0:16],
                                          scalar1=c.scr0[:, 19:20], scalar2=0.0,
                                          op0=ALU.subtract, op1=ALU.is_ge), r=["scr0"], w=["scr0"])
            vop(lambda e: e.scalar_tensor_tensor(out=c.scr0[:, 40:56], in0=c.scr0[:, 24:40],
                                                 scalar=c.tmpc[:, 3:4], in1=c.scr0[:, 0:16],
                                                 op0=ALU.mult, op1=ALU.mult),
                r=["scr0", "tmpc"], w=["scr0"])
            tr(pB[0:16, 1024 + 128 * (tt % 2):1024 + 128 * (tt % 2) + 128], c.scr0[:, 40:56], c.idf,
               r=["scr0", "idf_sb"], w=["pB:2"])
            vop(lambda e, tt=tt: e.tensor_copy(
                out=R[0:16, R_GATET + 128 * tt:R_GATET + 128 * (tt + 1)],
                in_=pB[0:16, 1024 + 128 * (tt % 2):1024 + 128 * (tt % 2) + 128]),
                r=["pB:2"], w=ak(R_GATET + 128 * tt, R_GATET + 128 * (tt + 1)))

        EXP = [("sut", "sdt", 0, False), ("sut", "sdt", 1, False)] + \
              [("rut", "rdt", r, True) for r in range(16)]
        for g in range(9):
            for gi in range(2):
                usrc, dsrc, idx, routed = EXP[2 * g + gi]
                ub = R_U0 if gi == 0 else R_U1
                db = R_D0 if gi == 0 else R_D1
                gdma(lambda e, usrc=usrc, idx=idx, ub=ub: e.dma_start(
                    out=R[:, ub:ub + 4096], in_=dr[usrc][idx]), r=(), w=ak(ub, ub + 4096), tag="U")
                gdma(lambda e, dsrc=dsrc, idx=idx, db=db: e.dma_start(
                    out=R[:, db:db + 4096], in_=dr[dsrc][idx]), r=(), w=ak(db, db + 4096), tag="D")
            for th in range(2):
                for gi in range(2):
                    usrc, dsrc, idx, routed = EXP[2 * g + gi]
                    ub = R_U0 if gi == 0 else R_U1
                    ab = R_A0 if gi == 0 else R_A1
                    for ec in range(4):
                        for s in range(2):
                            bank = (ec * 2 + s) % 2
                            for cc in range(NCH):
                                mm(pA[:, 512 * bank:512 * (bank + 1)],
                                   R[:, ub + 512 * cc + 128 * ec:ub + 512 * cc + 128 * (ec + 1)],
                                   c.xnT[:, 2048 * cc + 1024 * th + 512 * s:2048 * cc + 1024 * th + 512 * (s + 1)],
                                   cc == 0, cc == 7,
                                   r=ak(ub, ub + 4096) + xkk(2048 * cc + 1024 * th + 512 * s, 2048 * cc + 1024 * th + 512 * (s + 1)),
                                   w=[f"pA:{bank}"])
                            sop(lambda e, ab=ab, ec=ec, s=s, bank=bank: e.activation(
                                out=R[:, ab + 1024 * ec + 512 * s:ab + 1024 * ec + 512 * (s + 1)],
                                in_=pA[:, 512 * bank:512 * (bank + 1)], func=AF.Gelu),
                                r=[f"pA:{bank}"], w=ak(ab + 1024 * ec + 512 * s, ab + 1024 * ec + 512 * (s + 1)))
                    if routed:
                        for s in range(2):
                            mm(pB[:, 512 * s:512 * (s + 1)], R[0:16, 28672 + 128 * idx:28672 + 128 * (idx + 1)],
                               R[0:16, R_GATET + 1024 * th + 512 * s:R_GATET + 1024 * th + 512 * (s + 1)],
                               True, True,
                               r=ak(28672, 30720) + ak(R_GATET + 1024 * th + 512 * s, R_GATET + 1024 * th + 512 * (s + 1)),
                               w=[f"pB:{s}"])
                        for ec in range(4):
                            a3 = R[:, ab + 1024 * ec:ab + 1024 * (ec + 1)]
                            vop(lambda e, a3=a3: e.tensor_mul(out=a3, in0=a3, in1=pB[:, 0:1024]),
                                r=["pB:0", "pB:1"] + ak(ab + 1024 * ec, ab + 1024 * (ec + 1)),
                                w=ak(ab + 1024 * ec, ab + 1024 * (ec + 1)))
                for tt8 in range(8):
                    for os_ in range(2):
                        bank = 2 + (tt8 * 2 + os_) % 2
                        nmm = 0
                        for gi in range(2):
                            ab = R_A0 if gi == 0 else R_A1
                            db = R_D0 if gi == 0 else R_D1
                            for ec in range(4):
                                mm(pB[:, 512 * bank:512 * (bank + 1)],
                                   R[:, ab + 1024 * ec + 128 * tt8:ab + 1024 * ec + 128 * (tt8 + 1)],
                                   R[:, db + 1024 * ec + 512 * os_:db + 1024 * ec + 512 * (os_ + 1)],
                                   nmm == 0, nmm == 7,
                                   r=ak(ab + 1024 * ec + 128 * tt8, ab + 1024 * ec + 128 * (tt8 + 1)) + ak(db, db + 4096),
                                   w=[f"pB:{bank}"])
                                nmm += 1
                        tglob = th * 8 + tt8
                        xs = c.x[:, 1024 * tglob + 512 * os_:1024 * tglob + 512 * (os_ + 1)]
                        vop(lambda e, xs=xs, bank=bank: e.tensor_add(
                            out=xs, in0=xs, in1=pB[:, 512 * bank:512 * (bank + 1)]),
                            r=[f"pB:{bank}", f"x:{tglob}"], w=[f"x:{tglob}"])

    def l1_attn():
        def take_fill(n=1):
            for _ in range(n):
                if c.fill:
                    c.fill.pop(0)()
        sdma(lambda e: e.dma_start(out=c.scr1[0:1, 0:1024], in_=dr["bo1"][:, :]), r=(), w=[SCR1[0]])
        for s in range(2):
            mm(pB[:, 512 * s:512 * (s + 1)], c.onesf[0:1, 0:128],
               c.scr1[0:1, 512 * s:512 * (s + 1)], True, True,
               r=["onesf_sb", SCR1[0]], w=pbk(512 * s, 512 * (s + 1)))
        vop(lambda e: e.tensor_copy(out=c.botile[:, :], in_=pB[:, 0:1024]), r=pbk(0, 1024), w=["botile"])
        wv3 = dr["wv1"].rearrange("p (n f) -> p n f", n=8)
        for ch in range(NCH):
            wb = 4096 + 1024 * (ch % 2)
            gdma(lambda e, ch=ch, wb=wb: e.dma_start(
                out=c.ws[:, wb:wb + 1024].rearrange("p (n f) -> p n f", n=8),
                in_=wv3[:, :, 128 * ch:128 * (ch + 1)]), r=(), w=WSK(wb, wb + 1024), tag="wv1")
            for s4 in range(4):
                bank = s4 % 2
                for cc in range(NCH):
                    mm(pB[:, 512 * bank:512 * (bank + 1)],
                       c.ws[:, wb + 128 * cc:wb + 128 * (cc + 1)],
                       c.xnT[:, 2048 * cc + 512 * s4:2048 * cc + 512 * (s4 + 1)],
                       cc == 0, cc == 7,
                       r=WSK(wb, wb + 1024) + xkk(2048 * cc + 512 * s4, 2048 * cc + 512 * (s4 + 1)),
                       w=[f"pB:{bank}"])
                sop(lambda e, ch=ch, s4=s4, bank=bank: e.activation(
                    out=R[:, R_VBUF + 2048 * ch + 512 * s4:R_VBUF + 2048 * ch + 512 * (s4 + 1)],
                    in_=pB[:, 512 * bank:512 * (bank + 1)], func=AF.Identity,
                    bias=c.bv1s[:, ch:ch + 1], scale=1.0),
                    r=[f"pB:{bank}", "bv1_sb"],
                    w=ak(R_VBUF + 2048 * ch + 512 * s4, R_VBUF + 2048 * ch + 512 * (s4 + 1)))
                take_fill(1)
        kcv = c.scr1[:, :].bitcast(BF16)
        wq3l = dr["wq1"].rearrange("p (n f) -> p n f", n=8)
        wk3l = dr["wk1"].rearrange("p (n f) -> p n f", n=8)
        for ch in range(NCH):
            wbq = 1024 * (ch % 2)
            wbk = 2048 + 1024 * (ch % 2)
            gdma(lambda e, ch=ch, wbq=wbq: e.dma_start(
                out=c.ws[:, wbq:wbq + 1024].rearrange("p (n f) -> p n f", n=8),
                in_=wq3l[:, :, 128 * ch:128 * (ch + 1)]), r=(), w=WSK(wbq, wbq + 1024), tag="wq1")
            gdma(lambda e, ch=ch, wbk=wbk: e.dma_start(
                out=c.ws[:, wbk:wbk + 1024].rearrange("p (n f) -> p n f", n=8),
                in_=wk3l[:, :, 128 * ch:128 * (ch + 1)]), r=(), w=WSK(wbk, wbk + 1024), tag="wk1")
            for (wb, bsb, bkey, dst, dkey) in (
                    (wbq, c.bq1s, "bq1_sb", c.ET, ["ET:0", "ET:1"]),
                    (wbk, c.bk1s, "bk1_sb", kcv, SCR1)):
                for s4 in range(4):
                    bank = s4 % 2
                    for cc in range(NCH):
                        mm(pB[:, 512 * bank:512 * (bank + 1)],
                           c.ws[:, wb + 128 * cc:wb + 128 * (cc + 1)],
                           c.xnT[:, 2048 * cc + 512 * s4:2048 * cc + 512 * (s4 + 1)],
                           cc == 0, cc == 7,
                           r=WSK(wb, wb + 1024) + xkk(2048 * cc + 512 * s4, 2048 * cc + 512 * (s4 + 1)),
                           w=[f"pB:{bank}"])
                    sop(lambda e, ch=ch, bank=bank, s4=s4, bsb=bsb: e.activation(
                        out=c.scr0[:, 512 * (s4 % 2):512 * (s4 % 2) + 512],
                        in_=pB[:, 512 * bank:512 * (bank + 1)], func=AF.Square,
                        bias=bsb[:, ch:ch + 1], scale=1.0),
                        r=[f"pB:{bank}", bkey], w=["scr0"])
                    sop(lambda e, s4=s4, dst=dst: e.activation(
                        out=dst[:, 512 * s4:512 * (s4 + 1)],
                        in_=c.scr0[:, 512 * (s4 % 2):512 * (s4 % 2) + 512], func=AF.Exp, scale=-0.5),
                        r=["scr0"], w=dkey)
            vop(lambda e: e.tensor_mul(out=c.ET[:, :], in0=c.ET[:, :], in1=kcv[:, 0:2048]),
                r=["ET:0", "ET:1"] + SCR1, w=["ET:0", "ET:1"])
            for s4 in range(4):
                mm(pA[0:16, 512 * s4:512 * (s4 + 1)], c.eheadb[:, 16 * ch:16 * (ch + 1)],
                   c.ET[:, 512 * s4:512 * (s4 + 1)], ch == 0, ch == 7,
                   r=["ehead_b", "ET:0", "ET:1"], w=[f"pA:{s4}"])
        vop(lambda e: e.tensor_copy(out=c.ET[0:16, 0:2048], in_=pA[0:16, 0:2048]),
            r=pak(0, 2048), w=["ET:0", "ET:1"])
        for ch in range(NCH):
            for s4 in range(4):
                bank = s4 % 2
                mm(pB[:, 512 * bank:512 * (bank + 1)], c.ehead2b[:, 128 * ch:128 * (ch + 1)],
                   c.ET[0:16, 512 * s4:512 * (s4 + 1)], True, True,
                   r=["ehead2_b", "ET:0", "ET:1"], w=[f"pB:{bank}"])
                vv = R[:, R_VBUF + 2048 * ch + 512 * s4:R_VBUF + 2048 * ch + 512 * (s4 + 1)]
                vk = ak(R_VBUF + 2048 * ch + 512 * s4, R_VBUF + 2048 * ch + 512 * (s4 + 1))
                vop(lambda e, vv=vv, bank=bank: e.tensor_mul(
                    out=vv, in0=vv, in1=pB[:, 512 * bank:512 * (bank + 1)]),
                    r=[f"pB:{bank}"] + vk, w=vk)
                take_fill(1)
        wo3l = dr["wo1"].rearrange("p (n f) -> p n f", n=8)
        for os_ in range(2):
            gdma(lambda e, os_=os_: e.dma_start(
                out=c.ws[:, 0:4096].rearrange("p (n f) -> p n f", n=8),
                in_=wo3l[:, :, 512 * os_:512 * (os_ + 1)]), r=(), w=WSK(0, 4096), tag="wo1")
            for tt in range(NT):
                bank = tt % 4
                for ch in range(NCH):
                    mm(pA[:, 512 * bank:512 * (bank + 1)],
                       R[:, R_VBUF + 2048 * ch + 128 * tt:R_VBUF + 2048 * ch + 128 * (tt + 1)],
                       c.ws[:, 512 * ch:512 * ch + 512], ch == 0, ch == 7,
                       r=ak(R_VBUF + 2048 * ch + 128 * tt, R_VBUF + 2048 * ch + 128 * (tt + 1)) + WSK(512 * ch, 512 * ch + 512),
                       w=[f"pA:{bank}"])
                xs = c.x[:, 1024 * tt + 512 * os_:1024 * tt + 512 * (os_ + 1)]
                vop(lambda e, xs=xs, bank=bank: e.tensor_add(out=xs, in0=xs,
                                                             in1=pA[:, 512 * bank:512 * (bank + 1)]),
                    r=[f"pA:{bank}", f"x:{tt}"], w=[f"x:{tt}"])
                vop(lambda e, xs=xs, os_=os_: e.tensor_add(out=xs, in0=xs,
                                                           in1=c.botile[:, 512 * os_:512 * (os_ + 1)]),
                    r=["botile", f"x:{tt}"], w=[f"x:{tt}"])

    def l1_route_scores():
        for tt in range(NT):
            bank = tt % 2
            pv = pA[:, 512 * bank:512 * bank + 8]
            for cc in range(NCH):
                mm(pv, c.xnT[:, 2048 * cc + 128 * tt:2048 * cc + 128 * (tt + 1)],
                   c.gwls[:, 8 * cc:8 * (cc + 1)], cc == 0, cc == 7,
                   r=["gwl_b"] + xkk(2048 * cc + 128 * tt, 2048 * cc + 128 * (tt + 1)),
                   w=[f"pA:{bank}"])
            vop(lambda e, pv=pv: e.reduce_max(out=c.tmpc[:, 2:3], in_=pv, axis=AX.X),
                r=[f"pA:{bank}"], w=["tmpc"])
            vop(lambda e: e.tensor_scalar(out=c.tmpc[:, 2:3], in0=c.tmpc[:, 2:3], scalar1=-1.0,
                                          scalar2=None, op0=ALU.mult), r=["tmpc"], w=["tmpc"])
            sop(lambda e, pv=pv: e.activation(out=c.scr0[:, 0:8], in_=pv, func=AF.Exp,
                                              bias=c.tmpc[:, 2:3], scale=1.0),
                r=[f"pA:{bank}", "tmpc"], w=["scr0"])
            vop(lambda e: e.reduce_sum(out=c.tmpc[:, 3:4], in_=c.scr0[:, 0:8], axis=AX.X),
                r=["scr0"], w=["tmpc"])
            vop(lambda e: e.reciprocal(out=c.tmpc[:, 3:4], in_=c.tmpc[:, 3:4]), r=["tmpc"], w=["tmpc"])
            vop(lambda e, tt=tt: e.tensor_scalar(out=c.probs[:, 8 * tt:8 * (tt + 1)],
                                                 in0=c.scr0[:, 0:8], scalar1=c.tmpc[:, 3:4],
                                                 scalar2=None, op0=ALU.mult),
                r=["scr0", "tmpc"], w=["probs"])
        vop(lambda e: e.tensor_copy(out=c.probsb[:, :], in_=c.probs[:, :]), r=["probs"], w=["probsb"])

    def l1_route_fill_steps():
        steps = []

        def mask_from(colidx, final):
            tr(pB[0:1, 1536:1544], c.bis[:, colidx:colidx + 1], c.idf[0:8, 0:8], r=["bis", "idf_sb"], w=["pB:3"])
            vop(lambda e: e.tensor_copy(out=c.trow[0:1, 0:8], in_=pB[0:1, 1536:1544]),
                r=["pB:3"], w=["trow"])
            vop(lambda e: e.tensor_copy(
                out=c.trow[0:1, 128:256].rearrange("a (o f) -> a o f", o=16),
                in_=c.trow[0:1, 0:8].rearrange("a (o f) -> a o f", o=1).to_broadcast([1, 16, 8])),
                r=["trow"], w=["trow"])
            mm(pA[:, 1536:1664], c.onesf[0:1, 0:128], c.trow[0:1, 128:256], True, True,
               r=["onesf_sb", "trow"], w=["pA:3"])
            vop(lambda e: e.tensor_tensor(out=c.mask[:, :], in0=c.probs[:, :],
                                          in1=pA[:, 1536:1664], op=ALU.is_ge),
                r=["probs", "pA:3"], w=["mask"])
            if final:
                return
            mm(pB[:, 1664:1665], c.mask[:, :], c.onesb[:, 0:1], True, True,
               r=["mask", "onesf_b"], w=["pB:3"])
            vop(lambda e: e.tensor_copy(out=c.cnt1b[:, :], in_=pB[:, 1664:1665]),
                r=["pB:3"], w=["cnt1b"])
            mm(pA[0:8, 1664:1665], c.sel8b[:, :], c.cnt1b[:, :], True, True,
               r=["sel8_b", "cnt1b"], w=["pA:3"])
            vop(lambda e: e.tensor_scalar(out=c.bis[:, 3:4], in0=pA[0:8, 1664:1665],
                                          scalar1=511.5, scalar2=None, op0=ALU.is_ge),
                r=["pA:3"], w=["bis"])
            vop(lambda e: e.tensor_sub(out=c.bis[:, 4:5], in0=c.bis[:, 2:3], in1=c.bis[:, 0:1]),
                r=["bis"], w=["bis"])
            vop(lambda e: e.tensor_sub(out=c.bis[:, 5:6], in0=c.bis[:, 1:2], in1=c.bis[:, 2:3]),
                r=["bis"], w=["bis"])
            vop(lambda e: e.scalar_tensor_tensor(out=c.bis[:, 0:1], in0=c.bis[:, 4:5],
                                                 scalar=c.bis[:, 3:4], in1=c.bis[:, 0:1],
                                                 op0=ALU.mult, op1=ALU.add), r=["bis"], w=["bis"])
            vop(lambda e: e.scalar_tensor_tensor(out=c.bis[:, 1:2], in0=c.bis[:, 5:6],
                                                 scalar=c.bis[:, 3:4], in1=c.bis[:, 2:3],
                                                 op0=ALU.mult, op1=ALU.add), r=["bis"], w=["bis"])

        def init():
            vop(lambda e: e.memset(c.bis[:, 0:1], 0.0), r=(), w=["bis"])
            vop(lambda e: e.memset(c.bis[:, 1:2], 1.0), r=(), w=["bis"])
        steps.append(init)

        def one_iter():
            vop(lambda e: e.tensor_add(out=c.bis[:, 2:3], in0=c.bis[:, 0:1], in1=c.bis[:, 1:2]),
                r=["bis"], w=["bis"])
            vop(lambda e: e.tensor_scalar(out=c.bis[:, 2:3], in0=c.bis[:, 2:3], scalar1=0.5,
                                          scalar2=None, op0=ALU.mult), r=["bis"], w=["bis"])
            mask_from(2, False)
        for it in range(30):
            steps.append(one_iter)

        def final_mask():
            mask_from(0, True)
            vop(lambda e: e.tensor_copy(out=c.maskf[:, :], in_=c.mask[:, :]), r=["mask"], w=["maskf"])
        steps.append(final_mask)

        def cum_tt(tt):
            bank = tt % 2
            mm(pB[:, 512 * bank:512 * bank + 8], c.trilb[:, :], c.mask[:, 8 * tt:8 * (tt + 1)],
               True, True, r=["tril_b", "mask"], w=[f"pB:{bank}"])
            vop(lambda e, tt=tt, bank=bank: e.tensor_copy(
                out=c.cum[:, 8 * tt:8 * (tt + 1)], in_=pB[:, 512 * bank:512 * bank + 8]),
                r=[f"pB:{bank}"], w=["cum"])
            mm(pB[0:1, 1024 + 8 * tt:1024 + 8 * (tt + 1)], c.onesb[:, 0:1], c.mask[:, 8 * tt:8 * (tt + 1)],
               True, True, r=["onesf_b", "mask"], w=["pB:2"])
            vop(lambda e, tt=tt: e.tensor_copy(out=c.totrow[0:1, 8 * tt:8 * (tt + 1)],
                                               in_=pB[0:1, 1024 + 8 * tt:1024 + 8 * (tt + 1)]),
                r=["pB:2"], w=["totrow"])
        for tt in range(NT):
            steps.append(lambda tt=tt: cum_tt(tt))

        def offs():
            tr(pB[:, 1536:1537], c.totrow[0:1, 0:128], c.idf[0:1, 0:1], r=["totrow", "idf_sb"], w=["pB:3"])
            vop(lambda e: e.tensor_copy(out=c.totb[:, :], in_=pB[:, 1536:1537]), r=["pB:3"], w=["totb"])
            mm(pB[:, 1537:1538], c.trieb[:, :], c.totb[:, :], True, True,
               r=["trie_b", "totb"], w=["pB:3"])
            vop(lambda e: e.tensor_copy(out=c.tmpc[:, 1:2], in_=pB[:, 1537:1538]), r=["pB:3"], w=["tmpc"])
            tr(pB[0:1, 1544:1672], c.tmpc[:, 1:2], c.idf, r=["tmpc", "idf_sb"], w=["pB:3"])
            vop(lambda e: e.tensor_copy(out=c.trow[0:1, 256:384], in_=pB[0:1, 1544:1672]),
                r=["pB:3"], w=["trow"])
            mm(pA[:, 1536:1664], c.onesf[0:1, 0:128], c.trow[0:1, 256:384], True, True,
               r=["onesf_sb", "trow"], w=["pA:3"])
            vop(lambda e: e.tensor_copy(out=c.offsb[:, :], in_=pA[:, 1536:1664]), r=["pA:3"], w=["offsb"])
        steps.append(offs)
        return steps

    def l1_ffn():
        for st_ in c.fill:
            st_()
        c.fill = []
        for e_ in range(8):
            for tt in range(NT):
                vop(lambda e, e_=e_, tt=tt: e.tensor_scalar(
                    out=c.slots[:, tt:tt + 1], in0=c.cum[:, 8 * tt + e_:8 * tt + e_ + 1],
                    scalar1=c.offsb[:, 8 * tt + e_:8 * tt + e_ + 1], scalar2=32767.0,
                    op0=ALU.add, op1=ALU.add), r=["cum", "offsb"], w=["slots"])
                vop(lambda e, e_=e_, tt=tt: e.tensor_scalar(
                    out=c.slots[:, tt:tt + 1], in0=c.slots[:, tt:tt + 1],
                    scalar1=c.maskf[:, 8 * tt + e_:8 * tt + e_ + 1], scalar2=32768.0,
                    op0=ALU.mult, op1=ALU.subtract), r=["slots", "maskf"], w=["slots"])
            for tt in range(NT):
                vop(lambda e, tt=tt: e.tensor_scalar(
                    out=R[:, R_SEL + 512 * tt:R_SEL + 512 * (tt + 1)], in0=c.iota512[:, :],
                    scalar1=c.slots[:, tt:tt + 1], scalar2=0.0,
                    op0=ALU.subtract, op1=ALU.is_equal),
                    r=["iota512_sb", "slots"], w=ak(R_SEL + 512 * tt, R_SEL + 512 * (tt + 1)))
            for cc in range(NCH):
                bank = cc % 2
                for tt in range(NT):
                    mm(pA[:, 512 * bank:512 * (bank + 1)],
                       R[:, R_XNTOK + 1024 * tt + 128 * cc:R_XNTOK + 1024 * tt + 128 * (cc + 1)],
                       R[:, R_SEL + 512 * tt:R_SEL + 512 * (tt + 1)],
                       tt == 0, tt == 15,
                       r=ak(R_XNTOK + 1024 * tt + 128 * cc, R_XNTOK + 1024 * tt + 128 * (cc + 1)) + ak(R_SEL + 512 * tt, R_SEL + 512 * (tt + 1)),
                       w=[f"pA:{bank}"])
                vop(lambda e, cc=cc, bank=bank: e.tensor_copy(
                    out=R[:, R_XIN + 512 * cc:R_XIN + 512 * (cc + 1)],
                    in_=pA[:, 512 * bank:512 * (bank + 1)]),
                    r=[f"pA:{bank}"], w=ak(R_XIN + 512 * cc, R_XIN + 512 * (cc + 1)))
            for tt in range(NT):
                mm(pB[0:1, 1024:1536], c.probsb[:, 8 * tt + e_:8 * tt + e_ + 1],
                   R[:, R_SEL + 512 * tt:R_SEL + 512 * (tt + 1)], tt == 0, tt == 15,
                   r=["probsb"] + ak(R_SEL + 512 * tt, R_SEL + 512 * (tt + 1)), w=["pB:2"])
            vop(lambda e: e.tensor_copy(out=c.trow[0:1, 0:512], in_=pB[0:1, 1024:1536]),
                r=["pB:2"], w=["trow"])
            for sc in range(4):
                tr(pB[:, 1536 + sc:1537 + sc], c.trow[0:1, 128 * sc:128 * (sc + 1)], c.idf[0:1, 0:1],
                   r=["trow", "idf_sb"], w=["pB:3"])
            vop(lambda e: e.tensor_copy(out=c.pcols[:, 0:4], in_=pB[:, 1536:1540]),
                r=["pB:3"], w=["pcols_sb"])
            w13 = dr["w1l"][e_].rearrange("p (n f) -> p n f", n=8)
            for hg in range(16):
                wb = 2048 * (hg % 3)
                gdma(lambda e, hg=hg, wb=wb: e.dma_start(
                    out=c.ws[:, wb:wb + 2048].rearrange("p (n f) -> p n f", n=8),
                    in_=w13[:, :, 256 * hg:256 * (hg + 1)]), r=(), w=WSK(wb, wb + 2048), tag="w1")
                for hj in range(2):
                    hc = 2 * hg + hj
                    bank = hc % 2
                    for cc in range(NCH):
                        mm(pA[:, 512 * bank:512 * (bank + 1)],
                           c.ws[:, wb + 256 * cc + 128 * hj:wb + 256 * cc + 128 * (hj + 1)],
                           R[:, R_XIN + 512 * cc:R_XIN + 512 * (cc + 1)],
                           cc == 0, cc == 7,
                           r=WSK(wb, wb + 2048) + ak(R_XIN + 512 * cc, R_XIN + 512 * (cc + 1)),
                           w=[f"pA:{bank}"])
                    sop(lambda e, hc=hc, bank=bank: e.activation(
                        out=c.xnT[:, 512 * hc:512 * (hc + 1)],
                        in_=pA[:, 512 * bank:512 * (bank + 1)], func=AF.Gelu),
                        r=[f"pA:{bank}"], w=xkk(512 * hc, 512 * (hc + 1)))
            w23 = dr["w2l"][e_].rearrange("p (n f) -> p n f", n=32)
            for cs in range(2):
                for hg in range(8):
                    wb = 2048 * (hg % 3)
                    gdma(lambda e, hg=hg, cs=cs, wb=wb: e.dma_start(
                        out=c.ws[:, wb:wb + 2048].rearrange("p (n f) -> p n f", n=4),
                        in_=w23[:, 4 * hg:4 * (hg + 1), 512 * cs:512 * (cs + 1)]),
                        r=(), w=WSK(wb, wb + 2048), tag="w2")
                    for hj in range(4):
                        hc = 4 * hg + hj
                        for st in range(4):
                            mm(pB[:, 512 * st:512 * (st + 1)],
                               c.xnT[:, 512 * hc + 128 * st:512 * hc + 128 * (st + 1)],
                               c.ws[:, wb + 512 * hj:wb + 512 * (hj + 1)],
                               hc == 0, hc == 31,
                               r=WSK(wb, wb + 2048) + xkk(512 * hc + 128 * st, 512 * hc + 128 * (st + 1)),
                               w=[f"pB:{st}"])
                for st in range(4):
                    vop(lambda e, st=st, cs=cs: e.tensor_scalar(
                        out=R[:, R_Y + 1024 * st + 512 * cs:R_Y + 1024 * st + 512 * (cs + 1)],
                        in0=pB[:, 512 * st:512 * (st + 1)], scalar1=c.pcols[:, st:st + 1],
                        scalar2=None, op0=ALU.mult),
                        r=[f"pB:{st}", "pcols_sb"], w=ak(R_Y + 1024 * st + 512 * cs, R_Y + 1024 * st + 512 * (cs + 1)))
            for tt in range(NT):
                tr(pB[0:1, 1024 + 128 * (tt % 2):1024 + 128 * (tt % 2) + 128],
                   c.slots[:, tt:tt + 1], c.idf, r=["slots", "idf_sb"], w=["pB:2"])
                vop(lambda e, tt=tt: e.tensor_copy(
                    out=c.scr1[0:1, 128 * tt:128 * (tt + 1)],
                    in_=pB[0:1, 1024 + 128 * (tt % 2):1024 + 128 * (tt % 2) + 128]),
                    r=["pB:2"], w=[SCR1[tt // 8]])
            for s in range(4):
                mm(pA[:, 512 * s:512 * (s + 1)], c.onesf[0:1, 0:128],
                   c.scr1[0:1, 512 * s:512 * (s + 1)], True, True,
                   r=["onesf_sb"] + SCR1, w=[f"pA:{s}"])
            for s in range(4):
                vop(lambda e, s=s: e.tensor_copy(out=c.scr1[:, 512 * s:512 * (s + 1)],
                                                 in_=pA[:, 512 * s:512 * (s + 1)]),
                    r=[f"pA:{s}"], w=[SCR1[s // 2]])
            for sc in range(4):
                vop(lambda e, sc=sc: e.tensor_scalar(
                    out=R[:, R_SEL + 2048 * sc:R_SEL + 2048 * (sc + 1)], in0=c.scr1[:, :],
                    scalar1=c.iotacol[:, sc:sc + 1], scalar2=0.0,
                    op0=ALU.subtract, op1=ALU.is_equal),
                    r=SCR1 + ["iotacol_sb"], w=ak(R_SEL + 2048 * sc, R_SEL + 2048 * (sc + 1)))
            for tt in range(NT):
                for cs in range(2):
                    bank = (tt * 2 + cs) % 2
                    for sc in range(4):
                        mm(pA[:, 512 * bank:512 * (bank + 1)],
                           R[:, R_SEL + 2048 * sc + 128 * tt:R_SEL + 2048 * sc + 128 * (tt + 1)],
                           R[:, R_Y + 1024 * sc + 512 * cs:R_Y + 1024 * sc + 512 * (cs + 1)],
                           sc == 0, sc == 3,
                           r=ak(R_SEL + 2048 * sc + 128 * tt, R_SEL + 2048 * sc + 128 * (tt + 1)) + ak(R_Y + 1024 * sc + 512 * cs, R_Y + 1024 * sc + 512 * (cs + 1)),
                           w=[f"pA:{bank}"])
                    xs = c.x[:, 1024 * tt + 512 * cs:1024 * tt + 512 * (cs + 1)]
                    vop(lambda e, xs=xs, bank=bank: e.tensor_add(
                        out=xs, in0=xs, in1=pA[:, 512 * bank:512 * (bank + 1)]),
                        r=[f"pA:{bank}", f"x:{tt}"], w=[f"x:{tt}"])

    def delay_pad():
        for i in range(int(os.environ.get("K_DELAY", "0"))):
            vop(lambda e: e.tensor_copy(out=c.scr0[:, 0:512], in_=c.scr0[:, 512:1024]),
                r=["scr0"], w=["scr0"])

    def store():
        sdma(lambda e: e.dma_start(
            out=c.out.rearrange("(n p) f -> p n f", p=128),
            in_=c.x[:, :].rearrange("p (n f) -> p n f", n=NT)), r=XT(), w=(), tag="stx")

    stages = ["ln0", "xnt0", "gn", "attn0", "ffn0", "ln1", "xnt1", "attn1", "ffn1"]
    stop = len(stages) if upto == "full" else stages.index(upto) + 1
    si = 0
    ln(0); si += 1
    if si < stop:
        build_xnT(); si += 1
    if si < stop:
        gn(); si += 1
    if si < stop:
        l0_attn(); si += 1
    if os.environ.get("K_DELAYMID"):
        for i in range(int(os.environ["K_DELAYMID"])):
            vop(lambda e: e.tensor_copy(out=c.scr0[:, 0:512], in_=c.scr0[:, 512:1024]),
                r=["scr0"], w=["scr0"])
    if si < stop:
        l0_ffn(); si += 1
    if si < stop:
        ln(1); si += 1
    if si < stop:
        build_xnT(); si += 1
    if si < stop:
        l1_route_scores()
        c.fill = l1_route_fill_steps()
        l1_attn(); si += 1
    if si < stop:
        l1_ffn(); si += 1
    delay_pad()
    if "xdump" in taps:
        tap("xdump", c.x[:, :], XT(), (128, 16384), F32)
    if "xntok" in taps:
        tap("xntok", R[:, R_XNTOK:R_XNTOK + 16384], ak(R_XNTOK, R_XNTOK + 16384), (128, 16384), BF16)
    if "xnt" in taps:
        tap("xnt", c.xnT[:, :], xkk(0, 16384), (128, 16384), BF16)
    if "gnab" in taps:
        tap("gnab", c.gnA[:, :], ["gnA"], (128, 8), F32)
        tap("gnbb", c.gnB[:, :], ["gnB"], (128, 8), F32)
    if "dbg1" in taps:
        tap("dbg1", c.offsb[0:1, :], ["offsb"], (1, 128), F32)
        tap("dbg2", c.offsb[:, :], ["offsb"], (128, 128), F32)
        tap("bqkvd", c.bqkvc[:, :], ["bqkv_sb"], (128, 24), F32)
        tap("gnbbd", c.gnBb[:, :], ["gnBb"], (128, 8), BF16)
        tap("tmpcd", c.tmpc[:, :], ["tmpc"], (128, 4), F32)
        tap("etd", c.ET[:, :], ["ET:0", "ET:1"], (128, 2048), BF16)
    if "qkv" in taps:
        tap("qp", R[:, R_QP:R_QP + 2048], ak(R_QP, R_QP + 2048), (128, 2048), BF16)
        tap("kp", R[:, R_KP:R_KP + 2048], ak(R_KP, R_KP + 2048), (128, 2048), BF16)
        tap("vp", R[:, R_VP:R_VP + 2080], ak(R_VP, R_VP + 2080), (128, 2080), BF16)
        tap("mb2", c.mb2[:, :], ["mb2"], (128, 2), F32)
        tap("bcall", c.bcall[:, :], ["bcall"], (128, 24), F32)
    if "ot" in taps:
        tap("ot", R[:, R_OT:R_OT + 16384], ak(R_OT, R_OT + 16384), (128, 16384), BF16)
    if "gatet" in taps:
        tap("gatet", R[0:16, R_GATET:R_GATET + 2048], ak(R_GATET, R_GATET + 2048), (16, 2048), BF16)
    if "probs" in taps:
        tap("probs", c.probs[:, :], ["probs"], (128, 128), F32)
    if "mask" in taps:
        tap("mask", c.mask[:, :], ["mask"], (128, 128), BF16)
    if "cum" in taps:
        tap("cum", c.cum[:, :], ["cum"], (128, 128), F32)
        tap("offsb", c.offsb[:, :], ["offsb"], (128, 128), F32)
    if "coefb" in taps:
        tap("coefb", c.ET[0:16, :], ["ET:0", "ET:1"], (16, 2048), BF16)
    store()


_NC_CACHE = {}


def kernel(**inputs):
    host = host_prep(inputs)
    x_full = np.asarray(inputs["L0_attn_x"], dtype=np.float32)
    if "full" not in _NC_CACHE:
        _NC_CACHE["full"] = build_program()
    nc = _NC_CACHE["full"]
    in_maps = []
    for b in range(8):
        m = dict(host)
        m["x"] = np.ascontiguousarray(x_full[b])
        in_maps.append(m)
    res = run_bass_kernel_spmd(nc, in_maps, core_ids=list(range(8)))
    out = np.stack([np.asarray(res.results[i]["out"]) for i in range(8)])
    return out.astype(np.float32)
